# version 08
# baseline (speedup 1.0000x reference)
import numpy as np

import concourse.bass as bass
import concourse.mybir as mybir
from concourse.bass_utils import run_bass_kernel_spmd

F32 = mybir.dt.float32
BF16 = mybir.dt.bfloat16
AF = mybir.ActivationFunctionType
ALU = mybir.AluOpType
AX = mybir.AxisListType

NCORES = 8
H = 1024
S = 16
II = 2816
B_FULL, T = 32, 2048
R = (B_FULL // NCORES) * T
CH = 128
TS = 512
NCH = TS // CH
import os as _os
NST = int(_os.environ.get("K_NST", R // TS))
ST_PER_SEQ = T // TS
KH = H // 128
KI = II // 128
SECS = [512, 512, 512, 512, 512, 256]
LN_EPS = 1e-5


class Sched:
    ENGS = ("t", "v", "s", "y", "g")

    def __init__(self):
        self.q = {e: [] for e in self.ENGS}
        self.cnt = {}

    def emit(self, eng, fn, waits=(), sig=None, inc=1):
        self.q[eng].append((fn, tuple(waits), sig, inc))
        if sig is not None:
            self.cnt[sig] = self.cnt.get(sig, 0) + inc
        return self.cnt.get(sig, 0) if sig else 0

    def now(self, sem):
        return self.cnt.get(sem, 0)


def build_program():
    nc = bass.Bass("TRN2", target_bir_lowering=False,
                   detect_race_conditions=False)

    dT = lambda n, sh, dt=F32, kind="ExternalInput": nc.dram_tensor(n, sh, dt, kind=kind).ap()
    x_h = dT("x", [R, H])
    W_in_h = dT("W_in", [H, H])
    W_B_h = dT("W_B", [H, H])
    W_D_h = dT("W_D", [H, H])
    W_out_h = dT("W_out", [H, H])
    B_mat_h = dT("B_mat", [H, S])
    C_mat_h = dT("C_mat", [S, H])
    W_ig_h = dT("W_ig", [H, S])
    W_fg_h = dT("W_fg", [H, S])
    b_in_h = dT("b_in", [1, H])
    b_B_h = dT("b_B", [1, H])
    b_D_h = dT("b_D", [1, H])
    b_out_h = dT("b_out", [1, H])
    b_ig_h = dT("b_ig", [1, S])
    b_fg_h = dT("b_fg", [1, S])
    gamma_h = dT("gamma", [1, H])
    beta_h = dT("beta", [1, H])
    w_gate_h = dT("w_gate", [II, H])
    w_up_h = dT("w_up", [II, H])
    w_down_h = dT("w_down", [H, II])
    ident_h = dT("ident", [128, 128], BF16)
    onesr_h = dT("onesr", [1, 128], BF16)
    out_h = dT("out", [R, H], F32, kind="ExternalOutput")

    gateT_h = nc.dram_tensor("gateT_q", [KH, 128, II], BF16, kind="Internal").ap()
    upT_h = nc.dram_tensor("upT_q", [KH, 128, II], BF16, kind="Internal").ap()
    downT_h = nc.dram_tensor("downT_q", [KI, 128, H], BF16, kind="Internal").ap()

    sb = lambda n, sh, dt: nc.alloc_sbuf_tensor(n, sh, dt)
    Win_s = sb("Win_s", [128, KH, H], BF16)
    WD_s = sb("WD_s", [128, KH, H], BF16)
    Wout_s = sb("Wout_s", [128, KH, H], BF16)
    WS3_s = sb("WS3_s", [128, KH, 3 * S], BF16)
    Bm_s = sb("Bm_s", [128, KH, S], BF16)
    C_s = sb("C_s", [S, H], BF16)
    gamma_t = sb("gamma_t", [128, H], BF16)
    ident_s = sb("ident_s", [128, 128], BF16)
    ones_row = sb("ones_row", [1, 128], BF16)
    binrow = sb("binrow", [1, H], BF16)
    betarow = sb("betarow", [1, H], BF16)
    czrow = sb("czrow", [1, 3 * S], BF16)
    cprerow = sb("cprerow", [1, H], BF16)
    coutrow = sb("coutrow", [1, H], BF16)
    bincol = sb("bincol", [128, KH], BF16)
    bBcol = sb("bBcol", [128, KH], BF16)
    biasrow48 = sb("biasrow48", [1, 3 * S], BF16)
    bias_p1 = sb("bias_p1", [128, 1], F32)
    bias_n1 = sb("bias_n1", [128, 1], F32)
    xnT_s = sb("xnT_s", [128, KH, TS], BF16)
    xqT_s = sb("xqT_s", [128, KH, TS], BF16)
    xpT_s = sb("xpT_s", [128, KH, TS], BF16)
    hqT_s = sb("hqT_s", [128, KI, TS], BF16)
    f_s = sb("f_s", [128, NCH, H], BF16)
    ziT3 = sb("ziT3", [S, 3, TS], BF16)
    gates = sb("gates", [S, 2, TS], F32)
    u_s = sb("u_s", [S, TS], F32)
    hs_s = sb("hs_s", [S, TS], F32)
    hsb_s = sb("hsb_s", [S, TS], BF16)
    carry = sb("carry", [S, 1], F32)
    gup_s = sb("gup_s", [128, 2, 2, KH, 512], BF16)
    dwn_s = sb("dwn_s", [128, 2, 2, H], BF16)
    xin_s = sb("xin_s", [128, 2, H], F32)
    xnb_s = sb("xnb_s", [128, H], BF16)
    tpsrc = sb("tpsrc", [128, H], BF16)
    qin = sb("qin", [128, H], F32)
    sm2 = sb("sm2", [128, H], F32)
    a2q = sb("a2q", [128, H], F32)
    q1b = sb("q1b", [128, H], F32)
    q2b = sb("q2b", [128, H], F32)
    bmq = sb("bmq", [128, H // 16], F32)
    scq = sb("scq", [128, H // 16], F32)
    rcq = sb("rcq", [128, H // 16], F32)
    gsec = sb("gsec", [128, H], F32)
    bn6 = sb("bn6", [128, 12], F32)
    bn2 = sb("bn2", [128, 2], F32)
    tiny1 = sb("tiny1", [128, 1], F32)
    tiny2 = sb("tiny2", [128, 1], F32)
    tiny3 = sb("tiny3", [128, 1], F32)
    zib = sb("zib", [128, 3 * S], BF16)
    sgp = gsec
    y1pre = sm2
    y1T_t = sb("y1T_t", [128, KH * 128], BF16)
    y1T_c = y1T_t.ap()[:, 0:KH * 128]
    rowA = gsec.ap()[0:1, :]
    rowB = sm2.ap()[0:1, :]
    rowbf = tpsrc.ap()[0:1, :]
    bdrow = xnb_s.ap()[0:1, :]
    xhat_b = xnb_s
    outc = qin

    psM1 = nc.alloc_psum_tensor("psM1", [128, H], F32)
    psM2 = nc.alloc_psum_tensor("psM2", [128, H], F32)
    psT = nc.alloc_psum_tensor("psT", [128, 1024], BF16)
    psZ = nc.alloc_psum_tensor("psZ", [128, 3 * S], F32)
    psZT = nc.alloc_psum_tensor("psZT", [S, 3 * 128], BF16)

    sch = Sched()
    E = sch.emit

    def dma(q_, dst, src, sig, waits=()):
        return E(q_, lambda e, d=dst, s=src: e.dma_start(out=d, in_=s),
                 waits=waits, sig=sig, inc=16)

    def tdrain(sig, waits=()):
        def fn(e):
            e.drain()
            return e.nop()
        return E("t", fn, waits=waits, sig=sig)

    def mm(out_ap, lhsT, rhs, start, stop, waits=()):
        return E("t", lambda e, o=out_ap, l=lhsT, r=rhs, a=start, b=stop:
                 e.matmul(o, l, r, start=a, stop=b), waits=waits)

    def v_op(fn, waits=(), sig=None):
        return E("v", fn, waits=waits, sig=sig)

    def vnop(sig="vv"):
        return E("v", lambda e: e.engine_nop(), sig=sig)

    def vsettle2(sig):
        t = E("v", lambda e: e.engine_nop(), sig="vv")
        return E("v", lambda e: e.engine_nop(), waits=(("vv", t),), sig=sig)

    def snop(sig="sv"):
        return E("s", lambda e: e.nop(cycle_cnt=200), sig=sig)

    def s_op(fn, waits=(), sig=None):
        return E("s", fn, waits=waits, sig=sig)

    def transposes(src_ap, n, waits=()):
        for k in range(n):
            E("t", lambda e, kk=k, sa=src_ap: e.transpose(
                psT[:, kk * 128:(kk + 1) * 128], sa[:, kk * 128:(kk + 1) * 128],
                ident_s[:]), waits=waits if k == 0 else ())
        return tdrain("td")

    def emit_quant(w, qout_ap, waits_v=()):
        nb = w // 16
        qin3 = qin.ap()[:, 0:w].rearrange("p (b c) -> p b c", c=16)
        sm23 = sm2.ap()[:, 0:w].rearrange("p (b c) -> p b c", c=16)
        rcb = rcq.ap()[:, 0:nb].unsqueeze(2).broadcast_to([128, nb, 16])
        scb = scq.ap()[:, 0:nb].unsqueeze(2).broadcast_to([128, nb, 16])
        v_op(lambda e: e.tensor_reduce(bmq[:, 0:nb].unsqueeze(2), qin3,
                                       AX.X, ALU.max,
                                       apply_absolute_value=True),
             waits=waits_v)
        rv = vnop()
        v_op(lambda e: e.tensor_scalar(scq[:, 0:nb], bmq[:, 0:nb], 1.0 / 24.0,
                                       2.5e-13, ALU.mult, ALU.max),
             waits=(("vv", rv),))
        rv = vnop()
        v_op(lambda e: e.reciprocal(rcq[:, 0:nb], scq[:, 0:nb]),
             waits=(("vv", rv),))
        rv = vsettle2("vv")
        mv = v_op(lambda e: e.scalar_tensor_tensor(sm23, qin3, 1.0, rcb,
                                                   ALU.mult, ALU.mult),
                  waits=(("vv", rv),), sig="vs")
        s_op(lambda e: e.activation(sgp[:, 0:w], sm2[:, 0:w], AF.Sign,
                                    bias=bias_n1[:], scale=1.0),
             waits=(("vs", mv),))
        ab = s_op(lambda e: e.activation(a2q[:, 0:w], sm2[:, 0:w], AF.Abs),
                  waits=(("td", sch.now("td")), ("dn", sch.now("dn"))), sig="sv")
        v_op(lambda e: e.tensor_scalar(q1b[:, 0:w], a2q[:, 0:w], 3.0, None,
                                       ALU.is_gt), waits=(("sv", ab),))
        v_op(lambda e: e.scalar_tensor_tensor(q1b[:, 0:w], a2q[:, 0:w], 5.0,
                                              q1b[:, 0:w], ALU.is_gt, ALU.add))
        v_op(lambda e: e.scalar_tensor_tensor(q1b[:, 0:w], a2q[:, 0:w], 7.0,
                                              q1b[:, 0:w], ALU.is_gt, ALU.add))
        v_op(lambda e: e.tensor_scalar(q2b[:, 0:w], a2q[:, 0:w], 10.0, None,
                                       ALU.is_gt))
        v_op(lambda e: e.scalar_tensor_tensor(q2b[:, 0:w], a2q[:, 0:w], 14.0,
                                              q2b[:, 0:w], ALU.is_gt, ALU.add))
        v_op(lambda e: e.tensor_scalar(qin[:, 0:w], a2q[:, 0:w], 20.0, None,
                                       ALU.is_gt))
        v_op(lambda e: e.scalar_tensor_tensor(q2b[:, 0:w], qin[:, 0:w], 2.0,
                                              q2b[:, 0:w], ALU.mult, ALU.add))
        a2f = v_op(lambda e: e.scalar_tensor_tensor(q1b[:, 0:w], q2b[:, 0:w], 2.0,
                                                    q1b[:, 0:w], ALU.mult, ALU.add),
                   sig="vs")
        ssig = s_op(lambda e: e.activation(a2q[:, 0:w], sm2[:, 0:w], AF.Sign,
                                           bias=bias_p1[:], scale=1.0),
                    waits=(("vs", a2f),), sig="sv")
        v_op(lambda e: e.tensor_tensor(sgp[:, 0:w], sgp[:, 0:w], a2q[:, 0:w],
                                       ALU.add), waits=(("sv", ssig),))
        v_op(lambda e: e.scalar_tensor_tensor(q2b[:, 0:w], q1b[:, 0:w], 1.0,
                                              sgp[:, 0:w], ALU.add, ALU.mult))
        q23 = q2b.ap()[:, 0:w].rearrange("p (b c) -> p b c", c=16)
        qo3 = qout_ap.rearrange("p (b c) -> p b c", c=16)
        return v_op(lambda e: e.scalar_tensor_tensor(qo3, q23, 1.0, scb,
                                                     ALU.mult, ALU.mult), sig="vq")

    v_op(lambda e: e.memset(bias_p1[:], 1.0))
    v_op(lambda e: e.memset(bias_n1[:], -1.0))
    v_op(lambda e: e.memset(biasrow48[:], 0.0))

    dma("y", ident_s.ap(), ident_h, "dc")
    dma("y", ones_row.ap(), onesr_h, "dc")
    for dst, src in ((Win_s, W_in_h), (WD_s, W_D_h), (Wout_s, W_out_h)):
        dma("g", dst.ap(), src.rearrange("(k p) n -> p k n", p=128), "dwc")
    dma("g", WS3_s.ap()[:, :, S:2 * S], W_ig_h.rearrange("(k p) n -> p k n", p=128), "dwc")
    dma("g", WS3_s.ap()[:, :, 2 * S:3 * S], W_fg_h.rearrange("(k p) n -> p k n", p=128), "dwc")
    dma("g", Bm_s.ap(), B_mat_h.rearrange("(k p) n -> p k n", p=128), "dwc")
    dma("g", C_s.ap(), C_mat_h, "dwc")
    WB_v = gup_s.ap()[:, 0].rearrange("p m k n -> p (m k n)").rearrange(
        "p (a b) -> p a b", b=H)
    WBT_v = gup_s.ap()[:, 1].rearrange("p m k n -> p (m k n)").rearrange(
        "p (a b) -> p a b", b=H)
    dma("g", WB_v, W_B_h.rearrange("(k p) n -> p k n", p=128), "dwb")

    dma("y", rowA, b_in_h, "drA")
    cv = v_op(lambda e: e.tensor_copy(binrow[:], rowA),
              waits=(("drA", 16), ("dc", 32)), sig="vt")
    for k in range(KH):
        E("t", lambda e, kk=k: e.transpose(psT[:, 2 * kk:2 * kk + 1],
                                           binrow[0:1, kk * 128:(kk + 1) * 128],
                                           ident_s[0:1, 0:1]),
          waits=(("vt", cv),) if k == 0 else ())
    tdrain("td")
    v_op(lambda e: e.tensor_copy(bincol[:],
                                 psT.ap()[:, 0:2 * KH].rearrange(
                                     "p (k two) -> p k two", two=2)[:, :, 0]),
         waits=(("td", sch.now("td")),))
    vnop(sig="vt")
    dma("y", rowB, b_out_h, "drB")
    v_op(lambda e: e.tensor_tensor(coutrow[:], rowA, rowB, ALU.add),
         waits=(("drB", 16),), sig="vt")
    dma("y", rowA, b_B_h, "drA", waits=(("vt", sch.now("vt")),))
    cv = v_op(lambda e: e.tensor_copy(rowbf, rowA),
              waits=(("drA", 32),), sig="vt")
    for k in range(KH):
        E("t", lambda e, kk=k: e.transpose(psT[:, 2 * kk:2 * kk + 1],
                                           rowbf[0:1, kk * 128:(kk + 1) * 128],
                                           ident_s[0:1, 0:1]),
          waits=(("vt", cv),) if k == 0 else ())
    tdrain("td")
    v_op(lambda e: e.tensor_copy(bBcol[:],
                                 psT.ap()[:, 0:2 * KH].rearrange(
                                     "p (k two) -> p k two", two=2)[:, :, 0]),
         waits=(("td", sch.now("td")),))
    vnop(sig="vt")
    dma("y", rowA, b_D_h, "drA", waits=(("vt", sch.now("vt")),))
    cv = v_op(lambda e: e.tensor_copy(bdrow, rowA), waits=(("drA", 48),),
              sig="vt")
    dma("y", rowA, gamma_h, "drA", waits=(("vt", cv),))
    cv = v_op(lambda e: e.tensor_copy(rowbf, rowA), waits=(("drA", 64),),
              sig="vt")
    for half in range(2):
        mm(psM1[:, half * 512:(half + 1) * 512], ones_row[:],
           rowbf[0:1, half * 512:(half + 1) * 512], True, True,
           waits=(("vt", cv),) if half == 0 else ())
    tdrain("td")
    v_op(lambda e: e.tensor_copy(gamma_t[:], psM1[:]),
         waits=(("td", sch.now("td")),), sig="vt")
    dma("y", rowA, beta_h, "drA", waits=(("vt", sch.now("vt")),))
    cv = v_op(lambda e: e.tensor_copy(betarow[:], rowA), waits=(("drA", 80),),
              sig="vt")
    dma("y", rowA[0:1, 0:S], b_ig_h, "drA", waits=(("vt", cv),))
    dma("y", rowA[0:1, S:2 * S], b_fg_h, "drA")
    cv = v_op(lambda e: e.tensor_copy(biasrow48[0:1, S:2 * S], rowA[0:1, 0:S]),
              waits=(("drA", 112),), sig="vt")
    v_op(lambda e: e.tensor_copy(biasrow48[0:1, 2 * S:3 * S],
                                 rowA[0:1, S:2 * S]))
    cv = vnop(sig="vt")

    for kk in range(KH):
        for jj in range(KH):
            E("t", lambda e, a=kk, b=jj: e.transpose(
                psT[:, b * 128:(b + 1) * 128],
                WB_v[:, b, a * 128:(a + 1) * 128], ident_s[:]),
              waits=((("dwb", 16),) if kk == 0 else ()) +
                    (("vt", sch.now("vt")),) if jj == 0 else ())
        tdrain("td")
        v_op(lambda e, a=kk: e.tensor_copy(WBT_v[:, a, :], psT[:]),
             waits=(("td", sch.now("td")),), sig="vt")
    for r in range(KH):
        for kk in range(KH):
            mm(psM2[:, 0:S], WBT_v[:, kk, r * 128:(r + 1) * 128], Bm_s[:, kk, :],
               kk == 0, kk == KH - 1,
               waits=((("dwc", sch.now("dwc")),) if r == 0 else ()) +
                     (("vt", sch.now("vt")),) if kk == 0 else ())
        tdrain("td")
        v_op(lambda e, rr=r: e.tensor_copy(WS3_s[:, rr, 0:S], psM2[:, 0:S]),
             waits=(("td", sch.now("td")),))
        vnop(sig="vt")

    for kk in range(KH):
        mm(psZ[0:S, 0:1], Bm_s[:, kk, :], bBcol[:, kk:kk + 1], kk == 0,
           kk == KH - 1, waits=(("vt", sch.now("vt")),) if kk == 0 else ())
    tdrain("td")
    v_op(lambda e: e.tensor_copy(zib[0:S, 0:1], psZ[0:S, 0:1]),
         waits=(("td", sch.now("td")),))
    vnop(sig="vt")
    E("t", lambda e: e.transpose(psZT[0:1, 0:S], zib[0:S, 0:1],
                                 ident_s[0:S, 0:S]),
      waits=(("vt", sch.now("vt")),))
    tdrain("td")
    cw = vnop()
    v_op(lambda e: e.tensor_copy(czrow[:], biasrow48[:]),
         waits=(("td", sch.now("td")), ("vv", cw)))
    cw = vnop()
    v_op(lambda e: e.tensor_tensor(czrow[0:1, 0:S], czrow[0:1, 0:S],
                                   psZT[0:1, 0:S], ALU.add),
         waits=(("vv", cw),))
    vnop(sig="vt")

    v_op(lambda e: e.tensor_copy(cprerow[:], bdrow))
    vnop(sig="vt")

    for mat_i, (src_h, dstT_h, nchunk, width) in enumerate(
            [(w_gate_h, gateT_h, KI, H), (w_up_h, upT_h, KI, H),
             (w_down_h, downT_h, KH, II)]):
        for icx in range(nchunk):
            subs = [(0, 1024), (1024, 2048), (2048, 2816)] if width == II else [(0, 1024)]
            for s0_, s1_ in subs:
                wsub = s1_ - s0_
                nk = wsub // 128
                dma("y", qin.ap()[:, 0:wsub],
                    src_h[icx * 128:(icx + 1) * 128, s0_:s1_], "dfw",
                    waits=(("vq", sch.now("vq")),))
                emit_quant(wsub, tpsrc.ap()[:, 0:wsub],
                           waits_v=(("dfw", sch.now("dfw")),))
                for k in range(nk):
                    E("t", lambda e, kk=k: e.transpose(
                        psT[:, kk * 128:(kk + 1) * 128],
                        tpsrc[:, kk * 128:(kk + 1) * 128], ident_s[:]),
                      waits=(("vq", sch.now("vq")), ("dn", sch.now("dn"))) if k == 0 else ())
                tdrain("td")
                v_op(lambda e, n_=nk: e.tensor_copy(y1T_c[:, 0:n_ * 128],
                                                    psT[:, 0:n_ * 128]),
                     waits=(("td", sch.now("td")),), sig="vt")
                dst = dstT_h[s0_ // 128:(s0_ // 128) + nk, :,
                             icx * 128:(icx + 1) * 128].rearrange(
                    "k p n -> p k n")
                src_ap = y1T_c[:, 0:nk * 128].rearrange(
                    "p (k n) -> p k n", k=nk)
                dma("y", dst, src_ap, "dn", waits=(("vt", sch.now("vt")),))

    dma("y", xin_s.ap()[:, 0, :], x_h[0:128, :], "dx",
        waits=(("dn", sch.now("dn")),))
    dma("y", xin_s.ap()[:, 1, :], x_h[128:256, :], "dx")
    n_dx = 2

    last_e_drain = sch.now("td")
    last_fdown_read = sch.now("td")
    for st_i in range(NST):
        for c in range(NCH):
            g = st_i * NCH + c
            slot = g % 2
            dxw = 16 * (g + 1)
            xa = xin_s.ap()[:, slot, :]
            v_op(lambda e, a=xa: e.bn_stats(bn6[:, 0:6], a[:, 0:512]),
                 waits=(("dx", dxw), ("sv", sch.now("sv"))))
            v_op(lambda e, a=xa: e.bn_stats(bn6[:, 6:12], a[:, 512:1024]))
            bs = vnop()
            v_op(lambda e: e.bn_aggr(bn2[:], bn6[:]), waits=(("vv", bs),))
            ba = vnop()
            v_op(lambda e: e.tensor_scalar(tiny1[:], bn2[:, 1:2], LN_EPS, None,
                                           ALU.add), waits=(("vv", ba),))
            ba = vnop()
            v_op(lambda e: e.reciprocal(tiny2[:], tiny1[:]), waits=(("vv", ba),))
            v_op(lambda e: e.tensor_copy(tiny1[:], bn2[:, 0:1]))
            vv = vsettle2("vs")
            sq = s_op(lambda e: e.activation(tiny3[:], tiny2[:], AF.Sqrt),
                      waits=(("vs", vv),), sig="sv")
            wv = E("v", lambda e: e.engine_nop(), waits=(("sv", sq),), sig="vv")
            v_op(lambda e: e.scalar_tensor_tensor(tiny2[:], tiny1[:], -1.0,
                                                  tiny3[:], ALU.mult, ALU.mult),
                 waits=(("vv", wv),))
            vv = vnop(sig="vs")
            ss = s_op(lambda e, a=xa: e.activation(qin[:], a, AF.Identity,
                                                   bias=tiny2[:], scale=tiny3[:]),
                      waits=(("vs", vv), ("vq", sch.now("vq")),
                             ("dn", sch.now("dn"))), sig="sv")
            ss = s_op(lambda e, a=xa: e.activation(xnb_s[:], a, AF.Identity,
                                                   bias=tiny2[:], scale=tiny3[:]),
                      sig="sv")
            if g + 2 < NST * NCH:
                gg = g + 2
                dma("y", xin_s.ap()[:, gg % 2, :],
                    x_h[gg * 128:(gg + 1) * 128, :], "dx",
                    waits=(("sv", ss),))
                n_dx += 1
            emit_quant(H, tpsrc.ap()[:, 0:H], waits_v=(("sv", ss),))
            for k in range(KH):
                E("t", lambda e, kk=k: e.transpose(
                    psT[:, kk * 128:(kk + 1) * 128],
                    tpsrc[:, kk * 128:(kk + 1) * 128], ident_s[:]),
                  waits=(("vq", sch.now("vq")), ("vt", sch.now("vt")),
                         ("td", last_e_drain)) if k == 0 else ())
            tdrain("td")
            v_op(lambda e, cc=c: e.tensor_copy(
                xqT_s.ap()[:, :, cc * 128:(cc + 1) * 128],
                psT[:].rearrange("p (k n) -> p k n", k=KH)),
                waits=(("td", sch.now("td")),), sig="vt")
            for k in range(KH):
                E("t", lambda e, kk=k: e.transpose(
                    psT[:, kk * 128:(kk + 1) * 128],
                    xnb_s[:, kk * 128:(kk + 1) * 128], ident_s[:]),
                  waits=(("sv", ss), ("vt", sch.now("vt"))) if k == 0 else ())
            tdrain("td")
            v_op(lambda e, cc=c: e.tensor_copy(
                xnT_s.ap()[:, :, cc * 128:(cc + 1) * 128],
                psT[:].rearrange("p (k n) -> p k n", k=KH)),
                waits=(("td", sch.now("td")),), sig="vt")

        ksec0 = 0
        for si, SEC in enumerate(SECS):
            buf = si % 2
            nk = SEC // 128
            wbuf = (("td", sch.now("td")),)
            dma("g", gup_s.ap()[:, buf, 0, :, 0:SEC],
                gateT_h[:, :, ksec0:ksec0 + SEC].rearrange("k p n -> p k n"),
                "dgu", waits=wbuf)
            dma("g", gup_s.ap()[:, buf, 1, :, 0:SEC],
                upT_h[:, :, ksec0:ksec0 + SEC].rearrange("k p n -> p k n"),
                "dgu")
            wg = sch.now("dgu")
            for pr in range(2):
                c0, c1 = 2 * pr, 2 * pr + 1
                ffirst = True
                for ci, cc in enumerate((c0, c1)):
                    for kk in range(KH):
                        mm(psM1[:, ci * 512:ci * 512 + SEC],
                           xqT_s[:, kk, cc * 128:(cc + 1) * 128],
                           gup_s[:, buf, 0, kk, 0:SEC], kk == 0, kk == KH - 1,
                           waits=(("dgu", wg), ("vq", sch.now("vq")),
                                  ("vt", sch.now("vt"))) if ffirst else ())
                        ffirst = False
                for ci, cc in enumerate((c0, c1)):
                    for kk in range(KH):
                        mm(psM2[:, ci * 512:ci * 512 + SEC],
                           xqT_s[:, kk, cc * 128:(cc + 1) * 128],
                           gup_s[:, buf, 1, kk, 0:SEC], kk == 0, kk == KH - 1)
                tdrain("td")
                w2 = 2 * SEC
                pM1v = psM1.ap().rearrange("p (c n) -> p c n", c=2)[:, :, 0:SEC]
                pM2v = psM2.ap().rearrange("p (c n) -> p c n", c=2)[:, :, 0:SEC]
                gsv = gsec.ap()[:, 0:w2].rearrange("p (c n) -> p c n", c=2)
                q1v = q1b.ap()[:, 0:w2].rearrange("p (c n) -> p c n", c=2)
                qiv = qin.ap()[:, 0:w2].rearrange("p (c n) -> p c n", c=2)
                gv = v_op(lambda e, d=gsv, s_=pM1v: e.tensor_copy(d, s_),
                          waits=(("td", sch.now("td")),), sig="vs")
                sl = s_op(lambda e, w=w2: e.activation(q1b[:, 0:w], gsec[:, 0:w],
                                                       AF.Silu),
                          waits=(("vs", gv),), sig="sv")
                v_op(lambda e, d=qiv, s_=pM2v, q_=q1v: e.scalar_tensor_tensor(
                    d, s_, 1.0, q_, ALU.mult, ALU.mult),
                    waits=(("sv", sl), ("vq", sch.now("vq"))))
                emit_quant(w2, tpsrc.ap()[:, 0:w2])
                for ci in range(2):
                    for k in range(nk):
                        E("t", lambda e, kk=k, cci=ci, n_=nk: e.transpose(
                            psT[:, (cci * n_ + kk) * 128:(cci * n_ + kk + 1) * 128],
                            tpsrc[:, (cci * n_ + kk) * 128:(cci * n_ + kk + 1) * 128],
                            ident_s[:]),
                          waits=(("vq", sch.now("vq")), ("vt", sch.now("vt")))
                          if (ci == 0 and k == 0) else ())
                tdrain("td")
                for ci, cc in enumerate((c0, c1)):
                    v_op(lambda e, cci=ci, c_=cc, n_=nk, k0=ksec0 // 128:
                         e.tensor_copy(
                             hqT_s.ap()[:, k0:k0 + n_, c_ * 128:(c_ + 1) * 128],
                             psT[:, cci * n_ * 128:(cci + 1) * n_ * 128]
                             .rearrange("p (k n) -> p k n", k=n_)),
                         waits=(("td", sch.now("td")),) if ci == 0 else (),
                         sig="vt")
            ksec0 += SEC

        for pr in range(2):
            c0, c1 = 2 * pr, 2 * pr + 1
            psf = {c0: psM1, c1: psM2}
            for grp in range(11):
                bufd = grp % 2
                wbuf = (("td", sch.now("td")),)
                dma("g", dwn_s.ap()[:, bufd],
                    downT_h[2 * grp:2 * grp + 2].rearrange("k p n -> p k n"),
                    "ddn", waits=wbuf)
                wd = sch.now("ddn")
                for cc in (c0, c1):
                    if grp == 0:
                        for half in range(2):
                            mm(psf[cc][0:128, half * 512:(half + 1) * 512],
                               ones_row[:], betarow[0:1, half * 512:(half + 1) * 512],
                               True, False,
                               waits=(("ddn", wd), ("vq", sch.now("vq")),
                                      ("vt", sch.now("vt"))) if (half == 0 and cc == c0) else ())
                    for kc in range(2):
                        ic = 2 * grp + kc
                        for half in range(2):
                            mm(psf[cc][:, half * 512:(half + 1) * 512],
                               hqT_s[:, ic, cc * 128:(cc + 1) * 128],
                               dwn_s[:, bufd, kc, half * 512:(half + 1) * 512],
                               False, grp == 10 and kc == 1,
                               waits=(("ddn", wd),) if (grp > 0 and kc == 0 and half == 0 and cc == c0) else ())
                tdrain("td")
            
            for cc in (c0, c1):
                v_op(lambda e, c_=cc, p_=psf[cc]: e.tensor_copy(
                    f_s.ap()[:, c_, :], p_[:]),
                    waits=(("td", sch.now("td")),) if cc == c0 else (), sig="vt")

        for c in range(NCH):
            ffirst = True
            for half in range(2):
                sl = slice(half * 512, (half + 1) * 512)
                mm(psM1[:, sl], ones_row[:], binrow[0:1, sl], True, False,
                   waits=(("vt", sch.now("vt")),) if ffirst else ())
                ffirst = False
                for kk in range(KH):
                    mm(psM1[:, sl], xnT_s[:, kk, c * 128:(c + 1) * 128],
                       Win_s[:, kk, sl], False, kk == KH - 1)
            tdrain("td")
            xv = v_op(lambda e: e.tensor_copy(tpsrc[:], psM1[:]),
                      waits=(("td", sch.now("td")),), sig="vt")
            for k in range(KH):
                E("t", lambda e, kk=k: e.transpose(
                    psT[:, kk * 128:(kk + 1) * 128],
                    tpsrc[:, kk * 128:(kk + 1) * 128], ident_s[:]),
                  waits=(("vt", xv),) if k == 0 else ())
            tdrain("td")
            xv = v_op(lambda e, cc=c: e.tensor_copy(
                xpT_s.ap()[:, :, cc * 128:(cc + 1) * 128],
                psT[:].rearrange("p (k n) -> p k n", k=KH)),
                waits=(("td", sch.now("td")),), sig="vt")
            mm(psZ[:], ones_row[:], czrow[:], True, False, waits=(("vt", xv),))
            for kk in range(KH):
                mm(psZ[:], xpT_s[:, kk, c * 128:(c + 1) * 128], WS3_s[:, kk, :],
                   False, kk == KH - 1)
            tdrain("td")
            v_op(lambda e: e.tensor_copy(zib[:], psZ[:]),
                 waits=(("td", sch.now("td")),))
            zv = vnop(sig="vt")
            for j in range(3):
                E("t", lambda e, jj=j: e.transpose(
                    psZT[0:S, jj * 128:(jj + 1) * 128],
                    zib[:, jj * S:(jj + 1) * S], ident_s[:]),
                  waits=(("vt", zv),) if j == 0 else ())
            tdrain("td")
            v_op(lambda e, cc=c: e.tensor_copy(
                ziT3.ap()[:, :, cc * 128:(cc + 1) * 128],
                psZT[0:S, :].rearrange("p (j n) -> p j n", j=3)),
                waits=(("td", sch.now("td")),), sig="vt")

        sgx = s_op(lambda e: e.activation(gates[:], ziT3[:, 1:3, :], AF.Sigmoid),
                   waits=(("vt", sch.now("vt")),), sig="sv")
        v_op(lambda e: e.tensor_tensor(u_s[:], gates[:, 0, :], ziT3[:, 0, :],
                                       ALU.mult), waits=(("sv", sgx),))
        uw = vsettle2("vv")
        if st_i % ST_PER_SEQ == 0:
            v_op(lambda e: e.tensor_tensor_scan(hs_s[:], gates[:, 1, :], u_s[:],
                                                0.0, ALU.mult, ALU.add),
                 waits=(("vv", uw),))
        else:
            v_op(lambda e: e.tensor_tensor_scan(hs_s[:], gates[:, 1, :], u_s[:],
                                                carry[:], ALU.mult, ALU.add),
                 waits=(("vv", uw),))
        uw = vsettle2("vv")
        v_op(lambda e: e.tensor_copy(carry[:], hs_s[:, TS - 1:TS]),
             waits=(("vv", uw),))
        v_op(lambda e: e.tensor_copy(hsb_s[:], hs_s[:]))
        uw = vsettle2("vv")
        E("v", lambda e: e.engine_nop(), waits=(("vv", uw),))
        hv = vnop(sig="vt")

        for c in range(NCH):
            g = st_i * NCH + c
            ffirst = True
            for half in range(2):
                sl = slice(half * 512, (half + 1) * 512)
                mm(psM1[:, sl], ones_row[:], cprerow[0:1, sl], True, False,
                   waits=(("vt", sch.now("vt")),) if ffirst else ())
                ffirst = False
                for kk in range(KH):
                    mm(psM1[:, sl], xpT_s[:, kk, c * 128:(c + 1) * 128],
                       WD_s[:, kk, sl], False, False)
                mm(psM1[:, sl], hsb_s[:, c * 128:(c + 1) * 128], C_s[0:S, sl],
                   False, True)
            tdrain("td")
            yv = v_op(lambda e: e.tensor_copy(y1pre[:], psM1[:]),
                      waits=(("td", sch.now("td")),), sig="vs")
            ys = s_op(lambda e: e.activation(tpsrc[:], y1pre[:], AF.Tanh),
                      waits=(("vs", yv),), sig="sv")
            for k in range(KH):
                E("t", lambda e, kk=k: e.transpose(
                    psT[:, kk * 128:(kk + 1) * 128],
                    tpsrc[:, kk * 128:(kk + 1) * 128], ident_s[:]),
                  waits=(("sv", ys), ("vt", sch.now("vt"))) if k == 0 else ())
            tdrain("td")
            yv = v_op(lambda e: e.tensor_copy(y1T_c, psT[:]),
                      waits=(("td", sch.now("td")),), sig="vt")
            ffirst = True
            for half in range(2):
                sl = slice(half * 512, (half + 1) * 512)
                mm(psM2[:, sl], ones_row[:], coutrow[0:1, sl], True, False,
                   waits=(("vt", yv),) if ffirst else ())
                ffirst = False
                for kk in range(KH):
                    mm(psM2[:, sl], y1T_c[:, kk * 128:(kk + 1) * 128],
                       Wout_s[:, kk, sl], False, False)
                for kk in range(KH):
                    mm(psM2[:, sl], xnT_s[:, kk, c * 128:(c + 1) * 128],
                       Win_s[:, kk, sl], False, kk == KH - 1)
            last_e_drain = tdrain("td")
            av = v_op(lambda e: e.tensor_copy(y1pre[:], psM2[:]),
                      waits=(("td", last_e_drain),), sig="vs")
            v_op(lambda e: e.bn_stats(bn6[:, 0:6], y1pre[:, 0:512]),
                 waits=(("sv", sch.now("sv")),))
            v_op(lambda e: e.bn_stats(bn6[:, 6:12], y1pre[:, 512:1024]))
            bs = vnop()
            v_op(lambda e: e.bn_aggr(bn2[:], bn6[:]), waits=(("vv", bs),))
            ba = vnop()
            v_op(lambda e: e.tensor_scalar(tiny1[:], bn2[:, 1:2], LN_EPS, None,
                                           ALU.add), waits=(("vv", ba),))
            ba = vnop()
            v_op(lambda e: e.reciprocal(tiny2[:], tiny1[:]), waits=(("vv", ba),))
            v_op(lambda e: e.tensor_copy(tiny1[:], bn2[:, 0:1]))
            vv = vsettle2("vs")
            sq2 = s_op(lambda e: e.activation(tiny3[:], tiny2[:], AF.Sqrt),
                       waits=(("vs", vv),), sig="sv")
            wv = E("v", lambda e: e.engine_nop(), waits=(("sv", sq2),), sig="vv")
            v_op(lambda e: e.scalar_tensor_tensor(tiny2[:], tiny1[:], -1.0,
                                                  tiny3[:], ALU.mult, ALU.mult),
                 waits=(("vv", wv),))
            vv = vnop(sig="vs")
            xs2 = s_op(lambda e: e.activation(xhat_b[:], y1pre[:], AF.Identity,
                                              bias=tiny2[:], scale=tiny3[:]),
                       waits=(("vs", vv),), sig="sv")
            tv = v_op(lambda e: e.tensor_tensor(tpsrc[:], xhat_b[:], gamma_t[:],
                                                ALU.mult),
                      waits=(("sv", xs2),), sig="vs")
            dma("y", gsec.ap(), x_h[g * 128:(g + 1) * 128, :], "dxr",
                waits=(("vs", tv),))
            v_op(lambda e, c_=c: e.tensor_tensor(outc[:], f_s[:, c_, :],
                                                 gsec[:], ALU.add),
                 waits=(("dxr", sch.now("dxr")), ("dn", sch.now("dn"))))
            ov = v_op(lambda e: e.tensor_tensor(outc[:], outc[:], tpsrc[:],
                                                ALU.add), sig="vy")
            dma("y", out_h[g * 128:(g + 1) * 128, :], outc.ap(), "dn",
                waits=(("vy", ov),))

    E("y", lambda e: e.nop(), waits=(("dn", sch.now("dn")),))

    sems = {}
    import contextlib
    with contextlib.ExitStack() as ctx:
        def get_sem(name):
            if name not in sems:
                sems[name] = ctx.enter_context(nc.semaphore(name))
            return sems[name]
        for name in sch.cnt:
            get_sem(name)
        block = ctx.enter_context(nc.Block())
        eng_attr = {"t": "tensor", "v": "vector", "s": "scalar", "y": "sync",
                    "g": "gpsimd"}
        for ename, attr in eng_attr.items():
            ops = sch.q[ename]
            if not ops:
                continue
            def make_section(ops_):
                def section(eng):
                    for fn, waits, sig, inc in ops_:
                        for sname, val in waits:
                            if val > 0:
                                eng.wait_ge(get_sem(sname), val)
                        inst = fn(eng)
                        if sig is not None:
                            assert inst is not None, "signal op returned None"
                            inst.then_inc(get_sem(sig), inc)
                return section
            getattr(block, attr)(make_section(ops))
    return nc


_CACHED = {}


def kernel(**inputs):
    import ml_dtypes
    x = np.ascontiguousarray(inputs["L0_attn_x"], np.float32).reshape(B_FULL * T, H)
    ident = np.eye(128, dtype=ml_dtypes.bfloat16)
    onesr = np.ones((1, 128), dtype=ml_dtypes.bfloat16)
    row = lambda v: np.ascontiguousarray(np.asarray(v, np.float32).reshape(1, -1))
    common = {
        "W_in": np.ascontiguousarray(inputs["L0_attn_W_in"], np.float32),
        "W_B": np.ascontiguousarray(inputs["L0_attn_W_B"], np.float32),
        "W_D": np.ascontiguousarray(inputs["L0_attn_W_D"], np.float32),
        "W_out": np.ascontiguousarray(inputs["L0_attn_W_out"], np.float32),
        "B_mat": np.ascontiguousarray(inputs["L0_attn_B_mat"], np.float32),
        "C_mat": np.ascontiguousarray(inputs["L0_attn_C_mat"], np.float32),
        "W_ig": np.ascontiguousarray(inputs["L0_attn_W_ig"], np.float32),
        "W_fg": np.ascontiguousarray(inputs["L0_attn_W_fg"], np.float32),
        "b_in": row(inputs["L0_attn_b_in"]),
        "b_B": row(inputs["L0_attn_b_B"]),
        "b_D": row(inputs["L0_attn_b_D"]),
        "b_out": row(inputs["L0_attn_b_out"]),
        "b_ig": row(inputs["L0_attn_b_ig"]),
        "b_fg": row(inputs["L0_attn_b_fg"]),
        "gamma": row(inputs["L0_attn_gamma"]),
        "beta": row(inputs["L0_attn_beta"]),
        "w_gate": np.ascontiguousarray(inputs["L0_ffn_w_gate"], np.float32),
        "w_up": np.ascontiguousarray(inputs["L0_ffn_w_up"], np.float32),
        "w_down": np.ascontiguousarray(inputs["L0_ffn_w_down"], np.float32),
        "ident": ident,
        "onesr": onesr,
    }
    if "nc" not in _CACHED:
        _CACHED["nc"] = build_program()
    nc = _CACHED["nc"]
    in_maps = []
    for ci in range(NCORES):
        m = dict(common)
        m["x"] = np.ascontiguousarray(x[ci * R:(ci + 1) * R])
        in_maps.append(m)
    res = run_bass_kernel_spmd(nc, in_maps, core_ids=list(range(NCORES)))
    out = np.concatenate([r["out"] for r in res.results], axis=0)
    return out.reshape(B_FULL, T, H)


if __name__ == "__main__":
    pass


# version 09
# speedup vs baseline: 1.1462x; 1.1462x over previous
import numpy as np

import concourse.bass as bass
import concourse.mybir as mybir
from concourse.bass_utils import run_bass_kernel_spmd

F32 = mybir.dt.float32
BF16 = mybir.dt.bfloat16
AF = mybir.ActivationFunctionType
ALU = mybir.AluOpType
AX = mybir.AxisListType

NCORES = 8
H = 1024
S = 16
II = 2816
B_FULL, T = 32, 2048
R = (B_FULL // NCORES) * T
CH = 128
TS = 512
NCH = TS // CH
import os as _os
NST = int(_os.environ.get("K_NST", R // TS))
ST_PER_SEQ = T // TS
KH = H // 128
KI = II // 128
SECS = [512, 512, 512, 512, 512, 256]
LN_EPS = 1e-5


class Sched:
    ENGS = ("t", "v", "s", "y", "g")

    def __init__(self):
        self.q = {e: [] for e in self.ENGS}
        self.cnt = {}

    def emit(self, eng, fn, waits=(), sig=None, inc=1):
        self.q[eng].append((fn, tuple(waits), sig, inc))
        if sig is not None:
            self.cnt[sig] = self.cnt.get(sig, 0) + inc
        return self.cnt.get(sig, 0) if sig else 0

    def now(self, sem):
        return self.cnt.get(sem, 0)


def build_program():
    nc = bass.Bass("TRN2", target_bir_lowering=False,
                   detect_race_conditions=False)

    dT = lambda n, sh, dt=F32, kind="ExternalInput": nc.dram_tensor(n, sh, dt, kind=kind).ap()
    x_h = dT("x", [R, H])
    W_in_h = dT("W_in", [H, H])
    W_B_h = dT("W_B", [H, H])
    W_D_h = dT("W_D", [H, H])
    W_out_h = dT("W_out", [H, H])
    B_mat_h = dT("B_mat", [H, S])
    C_mat_h = dT("C_mat", [S, H])
    W_ig_h = dT("W_ig", [H, S])
    W_fg_h = dT("W_fg", [H, S])
    b_in_h = dT("b_in", [1, H])
    b_B_h = dT("b_B", [1, H])
    b_D_h = dT("b_D", [1, H])
    b_out_h = dT("b_out", [1, H])
    b_ig_h = dT("b_ig", [1, S])
    b_fg_h = dT("b_fg", [1, S])
    gamma_h = dT("gamma", [1, H])
    beta_h = dT("beta", [1, H])
    w_gate_h = dT("w_gate", [II, H])
    w_up_h = dT("w_up", [II, H])
    w_down_h = dT("w_down", [H, II])
    ident_h = dT("ident", [128, 128], BF16)
    onesr_h = dT("onesr", [1, 128], BF16)
    out_h = dT("out", [R, H], F32, kind="ExternalOutput")

    gateT_h = nc.dram_tensor("gateT_q", [KH, 128, II], BF16, kind="Internal").ap()
    upT_h = nc.dram_tensor("upT_q", [KH, 128, II], BF16, kind="Internal").ap()
    downT_h = nc.dram_tensor("downT_q", [KI, 128, H], BF16, kind="Internal").ap()

    sb = lambda n, sh, dt: nc.alloc_sbuf_tensor(n, sh, dt)
    Win_s = sb("Win_s", [128, KH, H], BF16)
    WD_s = sb("WD_s", [128, KH, H], BF16)
    Wout_s = sb("Wout_s", [128, KH, H], BF16)
    WS3_s = sb("WS3_s", [128, KH, 3 * S], BF16)
    Bm_s = sb("Bm_s", [128, KH, S], BF16)
    C_s = sb("C_s", [S, H], BF16)
    gamma_t = sb("gamma_t", [128, H], BF16)
    ident_s = sb("ident_s", [128, 128], BF16)
    ones_row = sb("ones_row", [1, 128], BF16)
    binrow = sb("binrow", [1, H], BF16)
    betarow = sb("betarow", [1, H], BF16)
    czrow = sb("czrow", [1, 3 * S], BF16)
    cprerow = sb("cprerow", [1, H], BF16)
    coutrow = sb("coutrow", [1, H], BF16)
    bincol = sb("bincol", [128, KH], BF16)
    bBcol = sb("bBcol", [128, KH], BF16)
    biasrow48 = sb("biasrow48", [1, 3 * S], BF16)
    bias_p1 = sb("bias_p1", [128, 1], F32)
    bias_n1 = sb("bias_n1", [128, 1], F32)
    xnT_s = sb("xnT_s", [128, KH, TS], BF16)
    xqT_s = sb("xqT_s", [128, KH, TS], BF16)
    xpT_s = sb("xpT_s", [128, KH, TS], BF16)
    hqT_s = sb("hqT_s", [128, KI, TS], BF16)
    f_s = sb("f_s", [128, NCH, H], BF16)
    ziT3 = sb("ziT3", [S, 3, TS], BF16)
    gates = sb("gates", [S, 2, TS], F32)
    u_s = sb("u_s", [S, TS], F32)
    hs_s = sb("hs_s", [S, TS], F32)
    hsb_s = sb("hsb_s", [S, TS], BF16)
    carry = sb("carry", [S, 1], F32)
    gup_s = sb("gup_s", [128, 2, 2, KH, 512], BF16)
    dwn_s = sb("dwn_s", [128, 2, 2, H], BF16)
    xin_s = sb("xin_s", [128, 2, H], F32)
    xnb_s = sb("xnb_s", [128, H], BF16)
    tpsrc = sb("tpsrc", [128, H], BF16)
    qin = sb("qin", [128, H], F32)
    sm2 = sb("sm2", [128, H], F32)
    a2q = sb("a2q", [128, H], F32)
    q1b = sb("q1b", [128, H], F32)
    q2b = sb("q2b", [128, H], F32)
    bmq = sb("bmq", [128, H // 16], F32)
    scq = sb("scq", [128, H // 16], F32)
    rcq = sb("rcq", [128, H // 16], F32)
    gsec = sb("gsec", [128, H], F32)
    bn6 = sb("bn6", [128, 12], F32)
    bn2 = sb("bn2", [128, 2], F32)
    tiny1 = sb("tiny1", [128, 1], F32)
    tiny2 = sb("tiny2", [128, 1], F32)
    tiny3 = sb("tiny3", [128, 1], F32)
    zib = sb("zib", [128, 3 * S], BF16)
    sgp = gsec
    y1pre = sm2
    y1T_t = sb("y1T_t", [128, KH * 128], BF16)
    y1T_c = y1T_t.ap()[:, 0:KH * 128]
    rowA = gsec.ap()[0:1, :]
    rowB = sm2.ap()[0:1, :]
    rowbf = tpsrc.ap()[0:1, :]
    bdrow = xnb_s.ap()[0:1, :]
    xhat_b = xnb_s
    outc = qin

    psM1 = nc.alloc_psum_tensor("psM1", [128, H], F32)
    psM2 = nc.alloc_psum_tensor("psM2", [128, H], F32)
    psT = nc.alloc_psum_tensor("psT", [128, 1024], BF16)
    psZ = nc.alloc_psum_tensor("psZ", [128, 3 * S], F32)
    psZT = nc.alloc_psum_tensor("psZT", [S, 3 * 128], BF16)

    sch = Sched()
    E = sch.emit

    def dma(q_, dst, src, sig, waits=()):
        return E(q_, lambda e, d=dst, s=src: e.dma_start(out=d, in_=s),
                 waits=waits, sig=sig, inc=16)

    def tdrain(sig, waits=()):
        def fn(e):
            e.drain()
            return e.nop()
        return E("t", fn, waits=waits, sig=sig)

    def mm(out_ap, lhsT, rhs, start, stop, waits=()):
        return E("t", lambda e, o=out_ap, l=lhsT, r=rhs, a=start, b=stop:
                 e.matmul(o, l, r, start=a, stop=b), waits=waits)

    def v_op(fn, waits=(), sig=None):
        return E("v", fn, waits=waits, sig=sig)

    def vnop(sig="vv"):
        return E("v", lambda e: e.engine_nop(), sig=sig)

    def vsettle2(sig):
        t = E("v", lambda e: e.engine_nop(), sig="vv")
        return E("v", lambda e: e.engine_nop(), waits=(("vv", t),), sig=sig)

    def snop(sig="sv"):
        return E("s", lambda e: e.nop(cycle_cnt=200), sig=sig)

    def s_op(fn, waits=(), sig=None):
        return E("s", fn, waits=waits, sig=sig)

    def transposes(src_ap, n, waits=()):
        for k in range(n):
            E("t", lambda e, kk=k, sa=src_ap: e.transpose(
                psT[:, kk * 128:(kk + 1) * 128], sa[:, kk * 128:(kk + 1) * 128],
                ident_s[:]), waits=waits if k == 0 else ())
        return tdrain("td")

    def emit_quant(w, qout_ap, waits_v=()):
        nb = w // 16
        qin3 = qin.ap()[:, 0:w].rearrange("p (b c) -> p b c", c=16)
        sm23 = sm2.ap()[:, 0:w].rearrange("p (b c) -> p b c", c=16)
        rcb = rcq.ap()[:, 0:nb].unsqueeze(2).broadcast_to([128, nb, 16])
        scb = scq.ap()[:, 0:nb].unsqueeze(2).broadcast_to([128, nb, 16])
        v_op(lambda e: e.tensor_reduce(bmq[:, 0:nb].unsqueeze(2), qin3,
                                       AX.X, ALU.max,
                                       apply_absolute_value=True),
             waits=waits_v)
        rv = vnop()
        v_op(lambda e: e.tensor_scalar(scq[:, 0:nb], bmq[:, 0:nb], 1.0 / 24.0,
                                       2.5e-13, ALU.mult, ALU.max),
             waits=(("vv", rv),))
        rv = vnop()
        v_op(lambda e: e.reciprocal(rcq[:, 0:nb], scq[:, 0:nb]),
             waits=(("vv", rv),))
        rv = vsettle2("vv")
        mv = v_op(lambda e: e.scalar_tensor_tensor(sm23, qin3, 1.0, rcb,
                                                   ALU.mult, ALU.mult),
                  waits=(("vv", rv),), sig="vs")
        s_op(lambda e: e.activation(sgp[:, 0:w], sm2[:, 0:w], AF.Sign,
                                    bias=bias_n1[:], scale=1.0),
             waits=(("vs", mv),))
        ab = s_op(lambda e: e.activation(a2q[:, 0:w], sm2[:, 0:w], AF.Abs),
                  waits=(("td", sch.now("td")), ("dn", sch.now("dn"))), sig="sv")
        v_op(lambda e: e.tensor_scalar(q1b[:, 0:w], a2q[:, 0:w], 3.0, None,
                                       ALU.is_gt), waits=(("sv", ab),))
        v_op(lambda e: e.scalar_tensor_tensor(q1b[:, 0:w], a2q[:, 0:w], 5.0,
                                              q1b[:, 0:w], ALU.is_gt, ALU.add))
        v_op(lambda e: e.scalar_tensor_tensor(q1b[:, 0:w], a2q[:, 0:w], 7.0,
                                              q1b[:, 0:w], ALU.is_gt, ALU.add))
        v_op(lambda e: e.tensor_scalar(q2b[:, 0:w], a2q[:, 0:w], 10.0, None,
                                       ALU.is_gt))
        v_op(lambda e: e.scalar_tensor_tensor(q2b[:, 0:w], a2q[:, 0:w], 14.0,
                                              q2b[:, 0:w], ALU.is_gt, ALU.add))
        v_op(lambda e: e.tensor_scalar(qin[:, 0:w], a2q[:, 0:w], 20.0, None,
                                       ALU.is_gt))
        v_op(lambda e: e.scalar_tensor_tensor(q2b[:, 0:w], qin[:, 0:w], 2.0,
                                              q2b[:, 0:w], ALU.mult, ALU.add))
        a2f = v_op(lambda e: e.scalar_tensor_tensor(q1b[:, 0:w], q2b[:, 0:w], 2.0,
                                                    q1b[:, 0:w], ALU.mult, ALU.add),
                   sig="vs")
        ssig = s_op(lambda e: e.activation(a2q[:, 0:w], sm2[:, 0:w], AF.Sign,
                                           bias=bias_p1[:], scale=1.0),
                    waits=(("vs", a2f),), sig="sv")
        v_op(lambda e: e.tensor_tensor(sgp[:, 0:w], sgp[:, 0:w], a2q[:, 0:w],
                                       ALU.add), waits=(("sv", ssig),))
        v_op(lambda e: e.scalar_tensor_tensor(q2b[:, 0:w], q1b[:, 0:w], 1.0,
                                              sgp[:, 0:w], ALU.add, ALU.mult))
        q23 = q2b.ap()[:, 0:w].rearrange("p (b c) -> p b c", c=16)
        qo3 = qout_ap.rearrange("p (b c) -> p b c", c=16)
        return v_op(lambda e: e.scalar_tensor_tensor(qo3, q23, 1.0, scb,
                                                     ALU.mult, ALU.mult), sig="vq")

    v_op(lambda e: e.memset(bias_p1[:], 1.0))
    v_op(lambda e: e.memset(bias_n1[:], -1.0))
    v_op(lambda e: e.memset(biasrow48[:], 0.0))

    dma("y", ident_s.ap(), ident_h, "dc")
    dma("y", ones_row.ap(), onesr_h, "dc")
    for dst, src in ((Win_s, W_in_h), (WD_s, W_D_h), (Wout_s, W_out_h)):
        dma("g", dst.ap(), src.rearrange("(k p) n -> p k n", p=128), "dwc")
    dma("g", WS3_s.ap()[:, :, S:2 * S], W_ig_h.rearrange("(k p) n -> p k n", p=128), "dwc")
    dma("g", WS3_s.ap()[:, :, 2 * S:3 * S], W_fg_h.rearrange("(k p) n -> p k n", p=128), "dwc")
    dma("g", Bm_s.ap(), B_mat_h.rearrange("(k p) n -> p k n", p=128), "dwc")
    dma("g", C_s.ap(), C_mat_h, "dwc")
    WB_v = gup_s.ap()[:, 0].rearrange("p m k n -> p (m k n)").rearrange(
        "p (a b) -> p a b", b=H)
    WBT_v = gup_s.ap()[:, 1].rearrange("p m k n -> p (m k n)").rearrange(
        "p (a b) -> p a b", b=H)
    dma("g", WB_v, W_B_h.rearrange("(k p) n -> p k n", p=128), "dwb")

    dma("y", rowA, b_in_h, "drA")
    cv = v_op(lambda e: e.tensor_copy(binrow[:], rowA),
              waits=(("drA", 16), ("dc", 32)), sig="vt")
    for k in range(KH):
        E("t", lambda e, kk=k: e.transpose(psT[:, 2 * kk:2 * kk + 1],
                                           binrow[0:1, kk * 128:(kk + 1) * 128],
                                           ident_s[0:1, 0:1]),
          waits=(("vt", cv),) if k == 0 else ())
    tdrain("td")
    v_op(lambda e: e.tensor_copy(bincol[:],
                                 psT.ap()[:, 0:2 * KH].rearrange(
                                     "p (k two) -> p k two", two=2)[:, :, 0]),
         waits=(("td", sch.now("td")),))
    vnop(sig="vt")
    dma("y", rowB, b_out_h, "drB")
    v_op(lambda e: e.tensor_tensor(coutrow[:], rowA, rowB, ALU.add),
         waits=(("drB", 16),), sig="vt")
    dma("y", rowA, b_B_h, "drA", waits=(("vt", sch.now("vt")),))
    cv = v_op(lambda e: e.tensor_copy(rowbf, rowA),
              waits=(("drA", 32),), sig="vt")
    for k in range(KH):
        E("t", lambda e, kk=k: e.transpose(psT[:, 2 * kk:2 * kk + 1],
                                           rowbf[0:1, kk * 128:(kk + 1) * 128],
                                           ident_s[0:1, 0:1]),
          waits=(("vt", cv),) if k == 0 else ())
    tdrain("td")
    v_op(lambda e: e.tensor_copy(bBcol[:],
                                 psT.ap()[:, 0:2 * KH].rearrange(
                                     "p (k two) -> p k two", two=2)[:, :, 0]),
         waits=(("td", sch.now("td")),))
    vnop(sig="vt")
    dma("y", rowA, b_D_h, "drA", waits=(("vt", sch.now("vt")),))
    cv = v_op(lambda e: e.tensor_copy(bdrow, rowA), waits=(("drA", 48),),
              sig="vt")
    dma("y", rowA, gamma_h, "drA", waits=(("vt", cv),))
    cv = v_op(lambda e: e.tensor_copy(rowbf, rowA), waits=(("drA", 64),),
              sig="vt")
    for half in range(2):
        mm(psM1[:, half * 512:(half + 1) * 512], ones_row[:],
           rowbf[0:1, half * 512:(half + 1) * 512], True, True,
           waits=(("vt", cv),) if half == 0 else ())
    tdrain("td")
    v_op(lambda e: e.tensor_copy(gamma_t[:], psM1[:]),
         waits=(("td", sch.now("td")),), sig="vt")
    dma("y", rowA, beta_h, "drA", waits=(("vt", sch.now("vt")),))
    cv = v_op(lambda e: e.tensor_copy(betarow[:], rowA), waits=(("drA", 80),),
              sig="vt")
    dma("y", rowA[0:1, 0:S], b_ig_h, "drA", waits=(("vt", cv),))
    dma("y", rowA[0:1, S:2 * S], b_fg_h, "drA")
    cv = v_op(lambda e: e.tensor_copy(biasrow48[0:1, S:2 * S], rowA[0:1, 0:S]),
              waits=(("drA", 112),), sig="vt")
    v_op(lambda e: e.tensor_copy(biasrow48[0:1, 2 * S:3 * S],
                                 rowA[0:1, S:2 * S]))
    cv = vnop(sig="vt")

    for kk in range(KH):
        for jj in range(KH):
            E("t", lambda e, a=kk, b=jj: e.transpose(
                psT[:, b * 128:(b + 1) * 128],
                WB_v[:, b, a * 128:(a + 1) * 128], ident_s[:]),
              waits=((("dwb", 16),) if kk == 0 else ()) +
                    (("vt", sch.now("vt")),) if jj == 0 else ())
        tdrain("td")
        v_op(lambda e, a=kk: e.tensor_copy(WBT_v[:, a, :], psT[:]),
             waits=(("td", sch.now("td")),), sig="vt")
    for r in range(KH):
        for kk in range(KH):
            mm(psM2[:, 0:S], WBT_v[:, kk, r * 128:(r + 1) * 128], Bm_s[:, kk, :],
               kk == 0, kk == KH - 1,
               waits=((("dwc", sch.now("dwc")),) if r == 0 else ()) +
                     (("vt", sch.now("vt")),) if kk == 0 else ())
        tdrain("td")
        v_op(lambda e, rr=r: e.tensor_copy(WS3_s[:, rr, 0:S], psM2[:, 0:S]),
             waits=(("td", sch.now("td")),))
        vnop(sig="vt")

    for kk in range(KH):
        mm(psZ[0:S, 0:1], Bm_s[:, kk, :], bBcol[:, kk:kk + 1], kk == 0,
           kk == KH - 1, waits=(("vt", sch.now("vt")),) if kk == 0 else ())
    tdrain("td")
    v_op(lambda e: e.tensor_copy(zib[0:S, 0:1], psZ[0:S, 0:1]),
         waits=(("td", sch.now("td")),))
    vnop(sig="vt")
    E("t", lambda e: e.transpose(psZT[0:1, 0:S], zib[0:S, 0:1],
                                 ident_s[0:S, 0:S]),
      waits=(("vt", sch.now("vt")),))
    tdrain("td")
    cw = vnop()
    v_op(lambda e: e.tensor_copy(czrow[:], biasrow48[:]),
         waits=(("td", sch.now("td")), ("vv", cw)))
    cw = vnop()
    v_op(lambda e: e.tensor_tensor(czrow[0:1, 0:S], czrow[0:1, 0:S],
                                   psZT[0:1, 0:S], ALU.add),
         waits=(("vv", cw),))
    vnop(sig="vt")

    v_op(lambda e: e.tensor_copy(cprerow[:], bdrow))
    vnop(sig="vt")

    for mat_i, (src_h, dstT_h, nchunk, width) in enumerate(
            [(w_gate_h, gateT_h, KI, H), (w_up_h, upT_h, KI, H),
             (w_down_h, downT_h, KH, II)]):
        for icx in range(nchunk):
            subs = [(0, 1024), (1024, 2048), (2048, 2816)] if width == II else [(0, 1024)]
            for s0_, s1_ in subs:
                wsub = s1_ - s0_
                nk = wsub // 128
                dma("y", qin.ap()[:, 0:wsub],
                    src_h[icx * 128:(icx + 1) * 128, s0_:s1_], "dfw",
                    waits=(("vq", sch.now("vq")),))
                emit_quant(wsub, tpsrc.ap()[:, 0:wsub],
                           waits_v=(("dfw", sch.now("dfw")),))
                for k in range(nk):
                    E("t", lambda e, kk=k: e.transpose(
                        psT[:, kk * 128:(kk + 1) * 128],
                        tpsrc[:, kk * 128:(kk + 1) * 128], ident_s[:]),
                      waits=(("vq", sch.now("vq")), ("dn", sch.now("dn"))) if k == 0 else ())
                tdrain("td")
                v_op(lambda e, n_=nk: e.tensor_copy(y1T_c[:, 0:n_ * 128],
                                                    psT[:, 0:n_ * 128]),
                     waits=(("td", sch.now("td")),), sig="vt")
                dst = dstT_h[s0_ // 128:(s0_ // 128) + nk, :,
                             icx * 128:(icx + 1) * 128].rearrange(
                    "k p n -> p k n")
                src_ap = y1T_c[:, 0:nk * 128].rearrange(
                    "p (k n) -> p k n", k=nk)
                dma("y", dst, src_ap, "dn", waits=(("vt", sch.now("vt")),))

    dma("y", xin_s.ap()[:, 0, :], x_h[0:128, :], "dx",
        waits=(("dn", sch.now("dn")),))
    dma("y", xin_s.ap()[:, 1, :], x_h[128:256, :], "dx")
    n_dx = 2

    last_e_drain = sch.now("td")
    last_fdown_read = sch.now("td")
    for st_i in range(NST):
        for c in range(NCH):
            g = st_i * NCH + c
            slot = g % 2
            dxw = 16 * (g + 1)
            xa = xin_s.ap()[:, slot, :]
            v_op(lambda e, a=xa: e.bn_stats(bn6[:, 0:6], a[:, 0:512]),
                 waits=(("dx", dxw), ("sv", sch.now("sv"))))
            v_op(lambda e, a=xa: e.bn_stats(bn6[:, 6:12], a[:, 512:1024]))
            bs = vnop()
            v_op(lambda e: e.bn_aggr(bn2[:], bn6[:]), waits=(("vv", bs),))
            ba = vnop()
            v_op(lambda e: e.tensor_scalar(tiny1[:], bn2[:, 1:2], LN_EPS, None,
                                           ALU.add), waits=(("vv", ba),))
            ba = vnop()
            v_op(lambda e: e.reciprocal(tiny2[:], tiny1[:]), waits=(("vv", ba),))
            v_op(lambda e: e.tensor_copy(tiny1[:], bn2[:, 0:1]))
            vv = vsettle2("vs")
            sq = s_op(lambda e: e.activation(tiny3[:], tiny2[:], AF.Sqrt),
                      waits=(("vs", vv),), sig="sv")
            wv = E("v", lambda e: e.engine_nop(), waits=(("sv", sq),), sig="vv")
            v_op(lambda e: e.scalar_tensor_tensor(tiny2[:], tiny1[:], -1.0,
                                                  tiny3[:], ALU.mult, ALU.mult),
                 waits=(("vv", wv),))
            vv = vnop(sig="vs")
            ss = s_op(lambda e, a=xa: e.activation(qin[:], a, AF.Identity,
                                                   bias=tiny2[:], scale=tiny3[:]),
                      waits=(("vs", vv), ("vq", sch.now("vq")),
                             ("dn", sch.now("dn"))), sig="sv")
            ss = s_op(lambda e, a=xa: e.activation(xnb_s[:], a, AF.Identity,
                                                   bias=tiny2[:], scale=tiny3[:]),
                      sig="sv")
            if g + 2 < NST * NCH:
                gg = g + 2
                dma("y", xin_s.ap()[:, gg % 2, :],
                    x_h[gg * 128:(gg + 1) * 128, :], "dx",
                    waits=(("sv", ss),))
                n_dx += 1
            emit_quant(H, tpsrc.ap()[:, 0:H], waits_v=(("sv", ss),))
            for k in range(KH):
                E("t", lambda e, kk=k: e.transpose(
                    psT[:, kk * 128:(kk + 1) * 128],
                    tpsrc[:, kk * 128:(kk + 1) * 128], ident_s[:]),
                  waits=(("vq", sch.now("vq")), ("vt", sch.now("vt")),
                         ("td", last_e_drain)) if k == 0 else ())
            tdrain("td")
            v_op(lambda e, cc=c: e.tensor_copy(
                xqT_s.ap()[:, :, cc * 128:(cc + 1) * 128],
                psT[:].rearrange("p (k n) -> p k n", k=KH)),
                waits=(("td", sch.now("td")),), sig="vt")
            for k in range(KH):
                E("t", lambda e, kk=k: e.transpose(
                    psT[:, kk * 128:(kk + 1) * 128],
                    xnb_s[:, kk * 128:(kk + 1) * 128], ident_s[:]),
                  waits=(("sv", ss), ("vt", sch.now("vt"))) if k == 0 else ())
            tdrain("td")
            v_op(lambda e, cc=c: e.tensor_copy(
                xnT_s.ap()[:, :, cc * 128:(cc + 1) * 128],
                psT[:].rearrange("p (k n) -> p k n", k=KH)),
                waits=(("td", sch.now("td")),), sig="vt")

        ksec0 = 0
        sec_drain = {}
        for si, SEC in enumerate(SECS):
            buf = si % 2
            nk = SEC // 128
            if si >= 2:
                wbuf = (("td", sec_drain[si - 2]),)
            else:
                wbuf = (("td", sch.now("td")),)
            dma("g", gup_s.ap()[:, buf, 0, :, 0:SEC],
                gateT_h[:, :, ksec0:ksec0 + SEC].rearrange("k p n -> p k n"),
                "dgu", waits=wbuf)
            dma("g", gup_s.ap()[:, buf, 1, :, 0:SEC],
                upT_h[:, :, ksec0:ksec0 + SEC].rearrange("k p n -> p k n"),
                "dgu")
            wg = sch.now("dgu")
            for pr in range(2):
                c0, c1 = 2 * pr, 2 * pr + 1
                ffirst = True
                for ci, cc in enumerate((c0, c1)):
                    for kk in range(KH):
                        mm(psM1[:, ci * 512:ci * 512 + SEC],
                           xqT_s[:, kk, cc * 128:(cc + 1) * 128],
                           gup_s[:, buf, 0, kk, 0:SEC], kk == 0, kk == KH - 1,
                           waits=(("dgu", wg), ("vq", sch.now("vq")),
                                  ("vt", sch.now("vt"))) if ffirst else ())
                        ffirst = False
                for ci, cc in enumerate((c0, c1)):
                    for kk in range(KH):
                        mm(psM2[:, ci * 512:ci * 512 + SEC],
                           xqT_s[:, kk, cc * 128:(cc + 1) * 128],
                           gup_s[:, buf, 1, kk, 0:SEC], kk == 0, kk == KH - 1)
                tdrain("td")
                w2 = 2 * SEC
                pM1v = psM1.ap().rearrange("p (c n) -> p c n", c=2)[:, :, 0:SEC]
                pM2v = psM2.ap().rearrange("p (c n) -> p c n", c=2)[:, :, 0:SEC]
                gsv = gsec.ap()[:, 0:w2].rearrange("p (c n) -> p c n", c=2)
                q1v = q1b.ap()[:, 0:w2].rearrange("p (c n) -> p c n", c=2)
                qiv = qin.ap()[:, 0:w2].rearrange("p (c n) -> p c n", c=2)
                gv = v_op(lambda e, d=gsv, s_=pM1v: e.tensor_copy(d, s_),
                          waits=(("td", sch.now("td")),), sig="vs")
                sl = s_op(lambda e, w=w2: e.activation(q1b[:, 0:w], gsec[:, 0:w],
                                                       AF.Silu),
                          waits=(("vs", gv),), sig="sv")
                v_op(lambda e, d=qiv, s_=pM2v, q_=q1v: e.scalar_tensor_tensor(
                    d, s_, 1.0, q_, ALU.mult, ALU.mult),
                    waits=(("sv", sl), ("vq", sch.now("vq"))))
                emit_quant(w2, tpsrc.ap()[:, 0:w2])
                for ci in range(2):
                    for k in range(nk):
                        E("t", lambda e, kk=k, cci=ci, n_=nk: e.transpose(
                            psT[:, (cci * n_ + kk) * 128:(cci * n_ + kk + 1) * 128],
                            tpsrc[:, (cci * n_ + kk) * 128:(cci * n_ + kk + 1) * 128],
                            ident_s[:]),
                          waits=(("vq", sch.now("vq")), ("vt", sch.now("vt")))
                          if (ci == 0 and k == 0) else ())
                tdrain("td")
                for ci, cc in enumerate((c0, c1)):
                    v_op(lambda e, cci=ci, c_=cc, n_=nk, k0=ksec0 // 128:
                         e.tensor_copy(
                             hqT_s.ap()[:, k0:k0 + n_, c_ * 128:(c_ + 1) * 128],
                             psT[:, cci * n_ * 128:(cci + 1) * n_ * 128]
                             .rearrange("p (k n) -> p k n", k=n_)),
                         waits=(("td", sch.now("td")),) if ci == 0 else (),
                         sig="vt")
                if pr == 1:
                    sec_drain[si] = sch.now("td")
            ksec0 += SEC

        for pr in range(2):
            c0, c1 = 2 * pr, 2 * pr + 1
            psf = {c0: psM1, c1: psM2}
            grp_drain = {}
            for grp in range(11):
                bufd = grp % 2
                if grp >= 2:
                    wbuf = (("td", grp_drain[grp - 2]),)
                else:
                    wbuf = (("td", sch.now("td")),)
                dma("g", dwn_s.ap()[:, bufd],
                    downT_h[2 * grp:2 * grp + 2].rearrange("k p n -> p k n"),
                    "ddn", waits=wbuf)
                wd = sch.now("ddn")
                for cc in (c0, c1):
                    if grp == 0:
                        for half in range(2):
                            mm(psf[cc][0:128, half * 512:(half + 1) * 512],
                               ones_row[:], betarow[0:1, half * 512:(half + 1) * 512],
                               True, False,
                               waits=(("ddn", wd), ("vq", sch.now("vq")),
                                      ("vt", sch.now("vt"))) if (half == 0 and cc == c0) else ())
                    for kc in range(2):
                        ic = 2 * grp + kc
                        for half in range(2):
                            mm(psf[cc][:, half * 512:(half + 1) * 512],
                               hqT_s[:, ic, cc * 128:(cc + 1) * 128],
                               dwn_s[:, bufd, kc, half * 512:(half + 1) * 512],
                               False, grp == 10 and kc == 1,
                               waits=(("ddn", wd),) if (grp > 0 and kc == 0 and half == 0 and cc == c0) else ())
                grp_drain[grp] = tdrain("td")
            
            for cc in (c0, c1):
                v_op(lambda e, c_=cc, p_=psf[cc]: e.tensor_copy(
                    f_s.ap()[:, c_, :], p_[:]),
                    waits=(("td", sch.now("td")),) if cc == c0 else (), sig="vt")

        for c in range(NCH):
            ffirst = True
            for half in range(2):
                sl = slice(half * 512, (half + 1) * 512)
                mm(psM1[:, sl], ones_row[:], binrow[0:1, sl], True, False,
                   waits=(("vt", sch.now("vt")),) if ffirst else ())
                ffirst = False
                for kk in range(KH):
                    mm(psM1[:, sl], xnT_s[:, kk, c * 128:(c + 1) * 128],
                       Win_s[:, kk, sl], False, kk == KH - 1)
            tdrain("td")
            xv = v_op(lambda e: e.tensor_copy(tpsrc[:], psM1[:]),
                      waits=(("td", sch.now("td")),), sig="vt")
            for k in range(KH):
                E("t", lambda e, kk=k: e.transpose(
                    psT[:, kk * 128:(kk + 1) * 128],
                    tpsrc[:, kk * 128:(kk + 1) * 128], ident_s[:]),
                  waits=(("vt", xv),) if k == 0 else ())
            tdrain("td")
            xv = v_op(lambda e, cc=c: e.tensor_copy(
                xpT_s.ap()[:, :, cc * 128:(cc + 1) * 128],
                psT[:].rearrange("p (k n) -> p k n", k=KH)),
                waits=(("td", sch.now("td")),), sig="vt")
            mm(psZ[:], ones_row[:], czrow[:], True, False, waits=(("vt", xv),))
            for kk in range(KH):
                mm(psZ[:], xpT_s[:, kk, c * 128:(c + 1) * 128], WS3_s[:, kk, :],
                   False, kk == KH - 1)
            tdrain("td")
            v_op(lambda e: e.tensor_copy(zib[:], psZ[:]),
                 waits=(("td", sch.now("td")),))
            zv = vnop(sig="vt")
            for j in range(3):
                E("t", lambda e, jj=j: e.transpose(
                    psZT[0:S, jj * 128:(jj + 1) * 128],
                    zib[:, jj * S:(jj + 1) * S], ident_s[:]),
                  waits=(("vt", zv),) if j == 0 else ())
            tdrain("td")
            v_op(lambda e, cc=c: e.tensor_copy(
                ziT3.ap()[:, :, cc * 128:(cc + 1) * 128],
                psZT[0:S, :].rearrange("p (j n) -> p j n", j=3)),
                waits=(("td", sch.now("td")),), sig="vt")

        sgx = s_op(lambda e: e.activation(gates[:], ziT3[:, 1:3, :], AF.Sigmoid),
                   waits=(("vt", sch.now("vt")),), sig="sv")
        v_op(lambda e: e.tensor_tensor(u_s[:], gates[:, 0, :], ziT3[:, 0, :],
                                       ALU.mult), waits=(("sv", sgx),))
        uw = vsettle2("vv")
        if st_i % ST_PER_SEQ == 0:
            v_op(lambda e: e.tensor_tensor_scan(hs_s[:], gates[:, 1, :], u_s[:],
                                                0.0, ALU.mult, ALU.add),
                 waits=(("vv", uw),))
        else:
            v_op(lambda e: e.tensor_tensor_scan(hs_s[:], gates[:, 1, :], u_s[:],
                                                carry[:], ALU.mult, ALU.add),
                 waits=(("vv", uw),))
        uw = vsettle2("vv")
        v_op(lambda e: e.tensor_copy(carry[:], hs_s[:, TS - 1:TS]),
             waits=(("vv", uw),))
        v_op(lambda e: e.tensor_copy(hsb_s[:], hs_s[:]))
        uw = vsettle2("vv")
        E("v", lambda e: e.engine_nop(), waits=(("vv", uw),))
        hv = vnop(sig="vt")

        for c in range(NCH):
            g = st_i * NCH + c
            ffirst = True
            for half in range(2):
                sl = slice(half * 512, (half + 1) * 512)
                mm(psM1[:, sl], ones_row[:], cprerow[0:1, sl], True, False,
                   waits=(("vt", sch.now("vt")),) if ffirst else ())
                ffirst = False
                for kk in range(KH):
                    mm(psM1[:, sl], xpT_s[:, kk, c * 128:(c + 1) * 128],
                       WD_s[:, kk, sl], False, False)
                mm(psM1[:, sl], hsb_s[:, c * 128:(c + 1) * 128], C_s[0:S, sl],
                   False, True)
            tdrain("td")
            yv = v_op(lambda e: e.tensor_copy(y1pre[:], psM1[:]),
                      waits=(("td", sch.now("td")),), sig="vs")
            ys = s_op(lambda e: e.activation(tpsrc[:], y1pre[:], AF.Tanh),
                      waits=(("vs", yv),), sig="sv")
            for k in range(KH):
                E("t", lambda e, kk=k: e.transpose(
                    psT[:, kk * 128:(kk + 1) * 128],
                    tpsrc[:, kk * 128:(kk + 1) * 128], ident_s[:]),
                  waits=(("sv", ys), ("vt", sch.now("vt"))) if k == 0 else ())
            tdrain("td")
            yv = v_op(lambda e: e.tensor_copy(y1T_c, psT[:]),
                      waits=(("td", sch.now("td")),), sig="vt")
            ffirst = True
            for half in range(2):
                sl = slice(half * 512, (half + 1) * 512)
                mm(psM2[:, sl], ones_row[:], coutrow[0:1, sl], True, False,
                   waits=(("vt", yv),) if ffirst else ())
                ffirst = False
                for kk in range(KH):
                    mm(psM2[:, sl], y1T_c[:, kk * 128:(kk + 1) * 128],
                       Wout_s[:, kk, sl], False, False)
                for kk in range(KH):
                    mm(psM2[:, sl], xnT_s[:, kk, c * 128:(c + 1) * 128],
                       Win_s[:, kk, sl], False, kk == KH - 1)
            last_e_drain = tdrain("td")
            av = v_op(lambda e: e.tensor_copy(y1pre[:], psM2[:]),
                      waits=(("td", last_e_drain),), sig="vs")
            v_op(lambda e: e.bn_stats(bn6[:, 0:6], y1pre[:, 0:512]),
                 waits=(("sv", sch.now("sv")),))
            v_op(lambda e: e.bn_stats(bn6[:, 6:12], y1pre[:, 512:1024]))
            bs = vnop()
            v_op(lambda e: e.bn_aggr(bn2[:], bn6[:]), waits=(("vv", bs),))
            ba = vnop()
            v_op(lambda e: e.tensor_scalar(tiny1[:], bn2[:, 1:2], LN_EPS, None,
                                           ALU.add), waits=(("vv", ba),))
            ba = vnop()
            v_op(lambda e: e.reciprocal(tiny2[:], tiny1[:]), waits=(("vv", ba),))
            v_op(lambda e: e.tensor_copy(tiny1[:], bn2[:, 0:1]))
            vv = vsettle2("vs")
            sq2 = s_op(lambda e: e.activation(tiny3[:], tiny2[:], AF.Sqrt),
                       waits=(("vs", vv),), sig="sv")
            wv = E("v", lambda e: e.engine_nop(), waits=(("sv", sq2),), sig="vv")
            v_op(lambda e: e.scalar_tensor_tensor(tiny2[:], tiny1[:], -1.0,
                                                  tiny3[:], ALU.mult, ALU.mult),
                 waits=(("vv", wv),))
            vv = vnop(sig="vs")
            xs2 = s_op(lambda e: e.activation(xhat_b[:], y1pre[:], AF.Identity,
                                              bias=tiny2[:], scale=tiny3[:]),
                       waits=(("vs", vv),), sig="sv")
            tv = v_op(lambda e: e.tensor_tensor(tpsrc[:], xhat_b[:], gamma_t[:],
                                                ALU.mult),
                      waits=(("sv", xs2),), sig="vs")
            dma("y", gsec.ap(), x_h[g * 128:(g + 1) * 128, :], "dxr",
                waits=(("vs", tv),))
            v_op(lambda e, c_=c: e.tensor_tensor(outc[:], f_s[:, c_, :],
                                                 gsec[:], ALU.add),
                 waits=(("dxr", sch.now("dxr")), ("dn", sch.now("dn"))))
            ov = v_op(lambda e: e.tensor_tensor(outc[:], outc[:], tpsrc[:],
                                                ALU.add), sig="vy")
            dma("y", out_h[g * 128:(g + 1) * 128, :], outc.ap(), "dn",
                waits=(("vy", ov),))

    E("y", lambda e: e.nop(), waits=(("dn", sch.now("dn")),))

    sems = {}
    import contextlib
    with contextlib.ExitStack() as ctx:
        def get_sem(name):
            if name not in sems:
                sems[name] = ctx.enter_context(nc.semaphore(name))
            return sems[name]
        for name in sch.cnt:
            get_sem(name)
        block = ctx.enter_context(nc.Block())
        eng_attr = {"t": "tensor", "v": "vector", "s": "scalar", "y": "sync",
                    "g": "gpsimd"}
        for ename, attr in eng_attr.items():
            ops = sch.q[ename]
            if not ops:
                continue
            def make_section(ops_):
                def section(eng):
                    for fn, waits, sig, inc in ops_:
                        for sname, val in waits:
                            if val > 0:
                                eng.wait_ge(get_sem(sname), val)
                        inst = fn(eng)
                        if sig is not None:
                            assert inst is not None, "signal op returned None"
                            inst.then_inc(get_sem(sig), inc)
                return section
            getattr(block, attr)(make_section(ops))
    return nc


_CACHED = {}


def kernel(**inputs):
    import ml_dtypes
    x = np.ascontiguousarray(inputs["L0_attn_x"], np.float32).reshape(B_FULL * T, H)
    ident = np.eye(128, dtype=ml_dtypes.bfloat16)
    onesr = np.ones((1, 128), dtype=ml_dtypes.bfloat16)
    row = lambda v: np.ascontiguousarray(np.asarray(v, np.float32).reshape(1, -1))
    common = {
        "W_in": np.ascontiguousarray(inputs["L0_attn_W_in"], np.float32),
        "W_B": np.ascontiguousarray(inputs["L0_attn_W_B"], np.float32),
        "W_D": np.ascontiguousarray(inputs["L0_attn_W_D"], np.float32),
        "W_out": np.ascontiguousarray(inputs["L0_attn_W_out"], np.float32),
        "B_mat": np.ascontiguousarray(inputs["L0_attn_B_mat"], np.float32),
        "C_mat": np.ascontiguousarray(inputs["L0_attn_C_mat"], np.float32),
        "W_ig": np.ascontiguousarray(inputs["L0_attn_W_ig"], np.float32),
        "W_fg": np.ascontiguousarray(inputs["L0_attn_W_fg"], np.float32),
        "b_in": row(inputs["L0_attn_b_in"]),
        "b_B": row(inputs["L0_attn_b_B"]),
        "b_D": row(inputs["L0_attn_b_D"]),
        "b_out": row(inputs["L0_attn_b_out"]),
        "b_ig": row(inputs["L0_attn_b_ig"]),
        "b_fg": row(inputs["L0_attn_b_fg"]),
        "gamma": row(inputs["L0_attn_gamma"]),
        "beta": row(inputs["L0_attn_beta"]),
        "w_gate": np.ascontiguousarray(inputs["L0_ffn_w_gate"], np.float32),
        "w_up": np.ascontiguousarray(inputs["L0_ffn_w_up"], np.float32),
        "w_down": np.ascontiguousarray(inputs["L0_ffn_w_down"], np.float32),
        "ident": ident,
        "onesr": onesr,
    }
    if "nc" not in _CACHED:
        _CACHED["nc"] = build_program()
    nc = _CACHED["nc"]
    in_maps = []
    for ci in range(NCORES):
        m = dict(common)
        m["x"] = np.ascontiguousarray(x[ci * R:(ci + 1) * R])
        in_maps.append(m)
    res = run_bass_kernel_spmd(nc, in_maps, core_ids=list(range(NCORES)))
    out = np.concatenate([r["out"] for r in res.results], axis=0)
    return out.reshape(B_FULL, T, H)


if __name__ == "__main__":
    pass


# version 10
# speedup vs baseline: 1.2910x; 1.1264x over previous
import numpy as np

import concourse.bass as bass
import concourse.mybir as mybir
from concourse.bass_utils import run_bass_kernel_spmd

F32 = mybir.dt.float32
BF16 = mybir.dt.bfloat16
AF = mybir.ActivationFunctionType
ALU = mybir.AluOpType
AX = mybir.AxisListType

NCORES = 8
H = 1024
S = 16
II = 2816
B_FULL, T = 32, 2048
R = (B_FULL // NCORES) * T
CH = 128
TS = 512
NCH = TS // CH
import os as _os
NST = int(_os.environ.get("K_NST", R // TS))
ST_PER_SEQ = T // TS
KH = H // 128
KI = II // 128
SECS = [512, 512, 512, 512, 512, 256]
LN_EPS = 1e-5


class Sched:
    ENGS = ("t", "v", "s", "y", "g")

    def __init__(self):
        self.q = {e: [] for e in self.ENGS}
        self.cnt = {}

    def emit(self, eng, fn, waits=(), sig=None, inc=1):
        self.q[eng].append((fn, tuple(waits), sig, inc))
        if sig is not None:
            self.cnt[sig] = self.cnt.get(sig, 0) + inc
        return self.cnt.get(sig, 0) if sig else 0

    def now(self, sem):
        return self.cnt.get(sem, 0)


def build_program():
    nc = bass.Bass("TRN2", target_bir_lowering=False,
                   detect_race_conditions=False)

    dT = lambda n, sh, dt=F32, kind="ExternalInput": nc.dram_tensor(n, sh, dt, kind=kind).ap()
    x_h = dT("x", [R, H])
    W_in_h = dT("W_in", [H, H])
    W_B_h = dT("W_B", [H, H])
    W_D_h = dT("W_D", [H, H])
    W_out_h = dT("W_out", [H, H])
    B_mat_h = dT("B_mat", [H, S])
    C_mat_h = dT("C_mat", [S, H])
    W_ig_h = dT("W_ig", [H, S])
    W_fg_h = dT("W_fg", [H, S])
    b_in_h = dT("b_in", [1, H])
    b_B_h = dT("b_B", [1, H])
    b_D_h = dT("b_D", [1, H])
    b_out_h = dT("b_out", [1, H])
    b_ig_h = dT("b_ig", [1, S])
    b_fg_h = dT("b_fg", [1, S])
    gamma_h = dT("gamma", [1, H])
    beta_h = dT("beta", [1, H])
    w_gate_h = dT("w_gate", [II, H])
    w_up_h = dT("w_up", [II, H])
    w_down_h = dT("w_down", [H, II])
    ident_h = dT("ident", [128, 128], BF16)
    onesr_h = dT("onesr", [1, 128], BF16)
    out_h = dT("out", [R, H], F32, kind="ExternalOutput")

    gateT_h = nc.dram_tensor("gateT_q", [KH, 128, II], BF16, kind="Internal").ap()
    upT_h = nc.dram_tensor("upT_q", [KH, 128, II], BF16, kind="Internal").ap()
    downT_h = nc.dram_tensor("downT_q", [KI, 128, H], BF16, kind="Internal").ap()

    sb = lambda n, sh, dt: nc.alloc_sbuf_tensor(n, sh, dt)
    Win_s = sb("Win_s", [128, KH, H], BF16)
    WD_s = sb("WD_s", [128, KH, H], BF16)
    Wout_s = sb("Wout_s", [128, KH, H], BF16)
    WS3_s = sb("WS3_s", [128, KH, 3 * S], BF16)
    Bm_s = sb("Bm_s", [128, KH, S], BF16)
    C_s = sb("C_s", [S, H], BF16)
    gamma_t = sb("gamma_t", [128, H], BF16)
    ident_s = sb("ident_s", [128, 128], BF16)
    ones_row = sb("ones_row", [1, 128], BF16)
    binrow = sb("binrow", [1, H], BF16)
    betarow = sb("betarow", [1, H], BF16)
    czrow = sb("czrow", [1, 3 * S], BF16)
    cprerow = sb("cprerow", [1, H], BF16)
    coutrow = sb("coutrow", [1, H], BF16)
    bincol = sb("bincol", [128, KH], BF16)
    bBcol = sb("bBcol", [128, KH], BF16)
    biasrow48 = sb("biasrow48", [1, 3 * S], BF16)
    bias_p1 = sb("bias_p1", [128, 1], F32)
    bias_n1 = sb("bias_n1", [128, 1], F32)
    xnT_s = sb("xnT_s", [128, KH, TS], BF16)
    xqT_s = sb("xqT_s", [128, KH, TS], BF16)
    xpT_s = sb("xpT_s", [128, KH, TS], BF16)
    hqT_s = sb("hqT_s", [128, KI, TS], BF16)
    f_s = sb("f_s", [128, NCH, H], BF16)
    ziT3 = sb("ziT3", [S, 3, TS], BF16)
    gates = sb("gates", [S, 2, TS], F32)
    u_s = sb("u_s", [S, TS], F32)
    hs_s = sb("hs_s", [S, TS], F32)
    hsb_s = sb("hsb_s", [S, TS], BF16)
    carry = sb("carry", [S, 1], F32)
    gup_s = sb("gup_s", [128, 2, 2, KH, 512], BF16)
    dwn_s = sb("dwn_s", [128, 2, 2, H], BF16)
    xin_s = sb("xin_s", [128, 2, H], F32)
    xnb_s = sb("xnb_s", [128, H], BF16)
    tpsrc = sb("tpsrc", [128, H], BF16)
    qin = sb("qin", [128, H], F32)
    sm2 = sb("sm2", [128, H], F32)
    a2q = sb("a2q", [128, H], F32)
    q1b = sb("q1b", [128, H], F32)
    q2b = sb("q2b", [128, H], F32)
    bmq = sb("bmq", [128, H // 16], F32)
    scq = sb("scq", [128, H // 16], F32)
    rcq = sb("rcq", [128, H // 16], F32)
    gsec = sb("gsec", [128, H], F32)
    bn6 = sb("bn6", [128, 12], F32)
    bn2 = sb("bn2", [128, 2], F32)
    tiny1 = sb("tiny1", [128, 1], F32)
    tiny2 = sb("tiny2", [128, 1], F32)
    tiny3 = sb("tiny3", [128, 1], F32)
    zib = sb("zib", [128, 3 * S], BF16)
    sgp = gsec
    y1pre = sm2
    y1T_t = sb("y1T_t", [128, KH * 128], BF16)
    y1T_c = y1T_t.ap()[:, 0:KH * 128]
    rowA = gsec.ap()[0:1, :]
    rowB = sm2.ap()[0:1, :]
    rowbf = tpsrc.ap()[0:1, :]
    bdrow = xnb_s.ap()[0:1, :]
    xhat_b = xnb_s
    outc = qin

    psM1 = nc.alloc_psum_tensor("psM1", [128, H], F32)
    psM2 = nc.alloc_psum_tensor("psM2", [128, H], F32)
    psT = nc.alloc_psum_tensor("psT", [128, 1024], BF16)
    psZ = nc.alloc_psum_tensor("psZ", [128, 3 * S], F32)
    psZT = nc.alloc_psum_tensor("psZT", [S, 3 * 128], BF16)

    sch = Sched()
    E = sch.emit

    def dma(q_, dst, src, sig, waits=()):
        return E(q_, lambda e, d=dst, s=src: e.dma_start(out=d, in_=s),
                 waits=waits, sig=sig, inc=16)

    def tdrain(sig, waits=()):
        def fn(e):
            e.drain()
            return e.nop()
        return E("t", fn, waits=waits, sig=sig)

    def mm(out_ap, lhsT, rhs, start, stop, waits=()):
        return E("t", lambda e, o=out_ap, l=lhsT, r=rhs, a=start, b=stop:
                 e.matmul(o, l, r, start=a, stop=b), waits=waits)

    def v_op(fn, waits=(), sig=None):
        return E("v", fn, waits=waits, sig=sig)

    def vnop(sig="vv"):
        return E("v", lambda e: e.engine_nop(), sig=sig)

    def vsettle2(sig):
        t = E("v", lambda e: e.engine_nop(), sig="vv")
        return E("v", lambda e: e.engine_nop(), waits=(("vv", t),), sig=sig)

    def snop(sig="sv"):
        return E("s", lambda e: e.nop(cycle_cnt=200), sig=sig)

    def s_op(fn, waits=(), sig=None):
        return E("s", fn, waits=waits, sig=sig)

    def transposes(src_ap, n, waits=()):
        for k in range(n):
            E("t", lambda e, kk=k, sa=src_ap: e.transpose(
                psT[:, kk * 128:(kk + 1) * 128], sa[:, kk * 128:(kk + 1) * 128],
                ident_s[:]), waits=waits if k == 0 else ())
        return tdrain("td")

    def emit_quant(w, qout_ap, waits_v=(), waits_out=()):
        nb = w // 16
        qin3 = qin.ap()[:, 0:w].rearrange("p (b c) -> p b c", c=16)
        sm23 = sm2.ap()[:, 0:w].rearrange("p (b c) -> p b c", c=16)
        rcb = rcq.ap()[:, 0:nb].unsqueeze(2).broadcast_to([128, nb, 16])
        scb = scq.ap()[:, 0:nb].unsqueeze(2).broadcast_to([128, nb, 16])
        v_op(lambda e: e.tensor_reduce(bmq[:, 0:nb].unsqueeze(2), qin3,
                                       AX.X, ALU.max,
                                       apply_absolute_value=True),
             waits=waits_v)
        rv = vnop()
        v_op(lambda e: e.tensor_scalar(scq[:, 0:nb], bmq[:, 0:nb], 1.0 / 24.0,
                                       2.5e-13, ALU.mult, ALU.max),
             waits=(("vv", rv),))
        rv = vnop()
        v_op(lambda e: e.reciprocal(rcq[:, 0:nb], scq[:, 0:nb]),
             waits=(("vv", rv),))
        rv = vsettle2("vv")
        mv = v_op(lambda e: e.scalar_tensor_tensor(sm23, qin3, 1.0, rcb,
                                                   ALU.mult, ALU.mult),
                  waits=(("vv", rv),), sig="vs")
        s_op(lambda e: e.activation(sgp[:, 0:w], sm2[:, 0:w], AF.Sign,
                                    bias=bias_n1[:], scale=1.0),
             waits=(("vs", mv),))
        ab = s_op(lambda e: e.activation(a2q[:, 0:w], sm2[:, 0:w], AF.Abs),
                  waits=(("td", sch.now("td")), ("dn", sch.now("dn"))), sig="sv")
        v_op(lambda e: e.tensor_scalar(q1b[:, 0:w], a2q[:, 0:w], 3.0, None,
                                       ALU.is_gt), waits=(("sv", ab),))
        v_op(lambda e: e.scalar_tensor_tensor(q1b[:, 0:w], a2q[:, 0:w], 5.0,
                                              q1b[:, 0:w], ALU.is_gt, ALU.add))
        v_op(lambda e: e.scalar_tensor_tensor(q1b[:, 0:w], a2q[:, 0:w], 7.0,
                                              q1b[:, 0:w], ALU.is_gt, ALU.add))
        v_op(lambda e: e.tensor_scalar(q2b[:, 0:w], a2q[:, 0:w], 10.0, None,
                                       ALU.is_gt))
        v_op(lambda e: e.scalar_tensor_tensor(q2b[:, 0:w], a2q[:, 0:w], 14.0,
                                              q2b[:, 0:w], ALU.is_gt, ALU.add))
        v_op(lambda e: e.tensor_scalar(qin[:, 0:w], a2q[:, 0:w], 20.0, None,
                                       ALU.is_gt))
        v_op(lambda e: e.scalar_tensor_tensor(q2b[:, 0:w], qin[:, 0:w], 2.0,
                                              q2b[:, 0:w], ALU.mult, ALU.add))
        a2f = v_op(lambda e: e.scalar_tensor_tensor(q1b[:, 0:w], q2b[:, 0:w], 2.0,
                                                    q1b[:, 0:w], ALU.mult, ALU.add),
                   sig="vs")
        ssig = s_op(lambda e: e.activation(a2q[:, 0:w], sm2[:, 0:w], AF.Sign,
                                           bias=bias_p1[:], scale=1.0),
                    waits=(("vs", a2f),), sig="sv")
        v_op(lambda e: e.tensor_tensor(sgp[:, 0:w], sgp[:, 0:w], a2q[:, 0:w],
                                       ALU.add), waits=(("sv", ssig),))
        v_op(lambda e: e.scalar_tensor_tensor(q2b[:, 0:w], q1b[:, 0:w], 1.0,
                                              sgp[:, 0:w], ALU.add, ALU.mult))
        q23 = q2b.ap()[:, 0:w].rearrange("p (b c) -> p b c", c=16)
        qo3 = qout_ap.rearrange("p (b c) -> p b c", c=16)
        return v_op(lambda e: e.scalar_tensor_tensor(qo3, q23, 1.0, scb,
                                                     ALU.mult, ALU.mult),
                    waits=waits_out, sig="vq")

    v_op(lambda e: e.memset(bias_p1[:], 1.0))
    v_op(lambda e: e.memset(bias_n1[:], -1.0))
    v_op(lambda e: e.memset(biasrow48[:], 0.0))

    dma("y", ident_s.ap(), ident_h, "dc")
    dma("y", ones_row.ap(), onesr_h, "dc")
    for dst, src in ((Win_s, W_in_h), (WD_s, W_D_h), (Wout_s, W_out_h)):
        dma("g", dst.ap(), src.rearrange("(k p) n -> p k n", p=128), "dwc")
    dma("g", WS3_s.ap()[:, :, S:2 * S], W_ig_h.rearrange("(k p) n -> p k n", p=128), "dwc")
    dma("g", WS3_s.ap()[:, :, 2 * S:3 * S], W_fg_h.rearrange("(k p) n -> p k n", p=128), "dwc")
    dma("g", Bm_s.ap(), B_mat_h.rearrange("(k p) n -> p k n", p=128), "dwc")
    dma("g", C_s.ap(), C_mat_h, "dwc")
    WB_v = gup_s.ap()[:, 0].rearrange("p m k n -> p (m k n)").rearrange(
        "p (a b) -> p a b", b=H)
    WBT_v = gup_s.ap()[:, 1].rearrange("p m k n -> p (m k n)").rearrange(
        "p (a b) -> p a b", b=H)
    dma("g", WB_v, W_B_h.rearrange("(k p) n -> p k n", p=128), "dwb")

    dma("y", rowA, b_in_h, "drA")
    cv = v_op(lambda e: e.tensor_copy(binrow[:], rowA),
              waits=(("drA", 16), ("dc", 32)), sig="vt")
    for k in range(KH):
        E("t", lambda e, kk=k: e.transpose(psT[:, 2 * kk:2 * kk + 1],
                                           binrow[0:1, kk * 128:(kk + 1) * 128],
                                           ident_s[0:1, 0:1]),
          waits=(("vt", cv),) if k == 0 else ())
    tdrain("td")
    v_op(lambda e: e.tensor_copy(bincol[:],
                                 psT.ap()[:, 0:2 * KH].rearrange(
                                     "p (k two) -> p k two", two=2)[:, :, 0]),
         waits=(("td", sch.now("td")),))
    vnop(sig="vt")
    dma("y", rowB, b_out_h, "drB")
    v_op(lambda e: e.tensor_tensor(coutrow[:], rowA, rowB, ALU.add),
         waits=(("drB", 16),), sig="vt")
    dma("y", rowA, b_B_h, "drA", waits=(("vt", sch.now("vt")),))
    cv = v_op(lambda e: e.tensor_copy(rowbf, rowA),
              waits=(("drA", 32),), sig="vt")
    for k in range(KH):
        E("t", lambda e, kk=k: e.transpose(psT[:, 2 * kk:2 * kk + 1],
                                           rowbf[0:1, kk * 128:(kk + 1) * 128],
                                           ident_s[0:1, 0:1]),
          waits=(("vt", cv),) if k == 0 else ())
    tdrain("td")
    v_op(lambda e: e.tensor_copy(bBcol[:],
                                 psT.ap()[:, 0:2 * KH].rearrange(
                                     "p (k two) -> p k two", two=2)[:, :, 0]),
         waits=(("td", sch.now("td")),))
    vnop(sig="vt")
    dma("y", rowA, b_D_h, "drA", waits=(("vt", sch.now("vt")),))
    cv = v_op(lambda e: e.tensor_copy(bdrow, rowA), waits=(("drA", 48),),
              sig="vt")
    dma("y", rowA, gamma_h, "drA", waits=(("vt", cv),))
    cv = v_op(lambda e: e.tensor_copy(rowbf, rowA), waits=(("drA", 64),),
              sig="vt")
    for half in range(2):
        mm(psM1[:, half * 512:(half + 1) * 512], ones_row[:],
           rowbf[0:1, half * 512:(half + 1) * 512], True, True,
           waits=(("vt", cv),) if half == 0 else ())
    tdrain("td")
    v_op(lambda e: e.tensor_copy(gamma_t[:], psM1[:]),
         waits=(("td", sch.now("td")),), sig="vt")
    dma("y", rowA, beta_h, "drA", waits=(("vt", sch.now("vt")),))
    cv = v_op(lambda e: e.tensor_copy(betarow[:], rowA), waits=(("drA", 80),),
              sig="vt")
    dma("y", rowA[0:1, 0:S], b_ig_h, "drA", waits=(("vt", cv),))
    dma("y", rowA[0:1, S:2 * S], b_fg_h, "drA")
    cv = v_op(lambda e: e.tensor_copy(biasrow48[0:1, S:2 * S], rowA[0:1, 0:S]),
              waits=(("drA", 112),), sig="vt")
    v_op(lambda e: e.tensor_copy(biasrow48[0:1, 2 * S:3 * S],
                                 rowA[0:1, S:2 * S]))
    cv = vnop(sig="vt")

    for kk in range(KH):
        for jj in range(KH):
            E("t", lambda e, a=kk, b=jj: e.transpose(
                psT[:, b * 128:(b + 1) * 128],
                WB_v[:, b, a * 128:(a + 1) * 128], ident_s[:]),
              waits=((("dwb", 16),) if kk == 0 else ()) +
                    (("vt", sch.now("vt")),) if jj == 0 else ())
        tdrain("td")
        v_op(lambda e, a=kk: e.tensor_copy(WBT_v[:, a, :], psT[:]),
             waits=(("td", sch.now("td")),), sig="vt")
    for r in range(KH):
        for kk in range(KH):
            mm(psM2[:, 0:S], WBT_v[:, kk, r * 128:(r + 1) * 128], Bm_s[:, kk, :],
               kk == 0, kk == KH - 1,
               waits=((("dwc", sch.now("dwc")),) if r == 0 else ()) +
                     (("vt", sch.now("vt")),) if kk == 0 else ())
        tdrain("td")
        v_op(lambda e, rr=r: e.tensor_copy(WS3_s[:, rr, 0:S], psM2[:, 0:S]),
             waits=(("td", sch.now("td")),))
        vnop(sig="vt")

    for kk in range(KH):
        mm(psZ[0:S, 0:1], Bm_s[:, kk, :], bBcol[:, kk:kk + 1], kk == 0,
           kk == KH - 1, waits=(("vt", sch.now("vt")),) if kk == 0 else ())
    tdrain("td")
    v_op(lambda e: e.tensor_copy(zib[0:S, 0:1], psZ[0:S, 0:1]),
         waits=(("td", sch.now("td")),))
    vnop(sig="vt")
    E("t", lambda e: e.transpose(psZT[0:1, 0:S], zib[0:S, 0:1],
                                 ident_s[0:S, 0:S]),
      waits=(("vt", sch.now("vt")),))
    tdrain("td")
    cw = vnop()
    v_op(lambda e: e.tensor_copy(czrow[:], biasrow48[:]),
         waits=(("td", sch.now("td")), ("vv", cw)))
    cw = vnop()
    v_op(lambda e: e.tensor_tensor(czrow[0:1, 0:S], czrow[0:1, 0:S],
                                   psZT[0:1, 0:S], ALU.add),
         waits=(("vv", cw),))
    vnop(sig="vt")

    v_op(lambda e: e.tensor_copy(cprerow[:], bdrow))
    vnop(sig="vt")

    for mat_i, (src_h, dstT_h, nchunk, width) in enumerate(
            [(w_gate_h, gateT_h, KI, H), (w_up_h, upT_h, KI, H),
             (w_down_h, downT_h, KH, II)]):
        for icx in range(nchunk):
            subs = [(0, 1024), (1024, 2048), (2048, 2816)] if width == II else [(0, 1024)]
            for s0_, s1_ in subs:
                wsub = s1_ - s0_
                nk = wsub // 128
                dma("y", qin.ap()[:, 0:wsub],
                    src_h[icx * 128:(icx + 1) * 128, s0_:s1_], "dfw",
                    waits=(("vq", sch.now("vq")),))
                emit_quant(wsub, tpsrc.ap()[:, 0:wsub],
                           waits_v=(("dfw", sch.now("dfw")),))
                for k in range(nk):
                    E("t", lambda e, kk=k: e.transpose(
                        psT[:, kk * 128:(kk + 1) * 128],
                        tpsrc[:, kk * 128:(kk + 1) * 128], ident_s[:]),
                      waits=(("vq", sch.now("vq")), ("dn", sch.now("dn"))) if k == 0 else ())
                tdrain("td")
                v_op(lambda e, n_=nk: e.tensor_copy(y1T_c[:, 0:n_ * 128],
                                                    psT[:, 0:n_ * 128]),
                     waits=(("td", sch.now("td")),), sig="vt")
                dst = dstT_h[s0_ // 128:(s0_ // 128) + nk, :,
                             icx * 128:(icx + 1) * 128].rearrange(
                    "k p n -> p k n")
                src_ap = y1T_c[:, 0:nk * 128].rearrange(
                    "p (k n) -> p k n", k=nk)
                dma("y", dst, src_ap, "dn", waits=(("vt", sch.now("vt")),))

    dma("y", xin_s.ap()[:, 0, :], x_h[0:128, :], "dx",
        waits=(("dn", sch.now("dn")),))
    dma("y", xin_s.ap()[:, 1, :], x_h[128:256, :], "dx")
    n_dx = 2

    last_e_drain = sch.now("td")
    last_fdown_read = sch.now("td")
    for st_i in range(NST):
        for c in range(NCH):
            g = st_i * NCH + c
            slot = g % 2
            dxw = 16 * (g + 1)
            xa = xin_s.ap()[:, slot, :]
            v_op(lambda e, a=xa: e.bn_stats(bn6[:, 0:6], a[:, 0:512]),
                 waits=(("dx", dxw), ("sv", sch.now("sv"))))
            v_op(lambda e, a=xa: e.bn_stats(bn6[:, 6:12], a[:, 512:1024]))
            bs = vnop()
            v_op(lambda e: e.bn_aggr(bn2[:], bn6[:]), waits=(("vv", bs),))
            ba = vnop()
            v_op(lambda e: e.tensor_scalar(tiny1[:], bn2[:, 1:2], LN_EPS, None,
                                           ALU.add), waits=(("vv", ba),))
            ba = vnop()
            v_op(lambda e: e.reciprocal(tiny2[:], tiny1[:]), waits=(("vv", ba),))
            v_op(lambda e: e.tensor_copy(tiny1[:], bn2[:, 0:1]))
            vv = vsettle2("vs")
            sq = s_op(lambda e: e.activation(tiny3[:], tiny2[:], AF.Sqrt),
                      waits=(("vs", vv),), sig="sv")
            wv = E("v", lambda e: e.engine_nop(), waits=(("sv", sq),), sig="vv")
            v_op(lambda e: e.scalar_tensor_tensor(tiny2[:], tiny1[:], -1.0,
                                                  tiny3[:], ALU.mult, ALU.mult),
                 waits=(("vv", wv),))
            vv = vnop(sig="vs")
            ss = s_op(lambda e, a=xa: e.activation(qin[:], a, AF.Identity,
                                                   bias=tiny2[:], scale=tiny3[:]),
                      waits=(("vs", vv), ("vq", sch.now("vq")),
                             ("dn", sch.now("dn"))), sig="sv")
            ss = s_op(lambda e, a=xa: e.activation(xnb_s[:], a, AF.Identity,
                                                   bias=tiny2[:], scale=tiny3[:]),
                      sig="sv")
            if g + 2 < NST * NCH:
                gg = g + 2
                dma("y", xin_s.ap()[:, gg % 2, :],
                    x_h[gg * 128:(gg + 1) * 128, :], "dx",
                    waits=(("sv", ss),))
                n_dx += 1
            emit_quant(H, tpsrc.ap()[:, 0:H], waits_v=(("sv", ss),))
            for k in range(KH):
                E("t", lambda e, kk=k: e.transpose(
                    psT[:, kk * 128:(kk + 1) * 128],
                    tpsrc[:, kk * 128:(kk + 1) * 128], ident_s[:]),
                  waits=(("vq", sch.now("vq")), ("vt", sch.now("vt")),
                         ("td", last_e_drain)) if k == 0 else ())
            tdrain("td")
            v_op(lambda e, cc=c: e.tensor_copy(
                xqT_s.ap()[:, :, cc * 128:(cc + 1) * 128],
                psT[:].rearrange("p (k n) -> p k n", k=KH)),
                waits=(("td", sch.now("td")),), sig="vt")
            for k in range(KH):
                E("t", lambda e, kk=k: e.transpose(
                    psT[:, kk * 128:(kk + 1) * 128],
                    xnb_s[:, kk * 128:(kk + 1) * 128], ident_s[:]),
                  waits=(("sv", ss), ("vt", sch.now("vt"))) if k == 0 else ())
            tdrain("td")
            v_op(lambda e, cc=c: e.tensor_copy(
                xnT_s.ap()[:, :, cc * 128:(cc + 1) * 128],
                psT[:].rearrange("p (k n) -> p k n", k=KH)),
                waits=(("td", sch.now("td")),), sig="vt")

        ksec0 = 0
        sec_drain = {}
        m1_free = sch.now("vs")
        m2_free = sch.now("vs")
        xqT_ready = sch.now("vt")
        pend = None
        tp_drain_prev = sch.now("td")
        for si, SEC in enumerate(SECS):
            buf = si % 2
            nk = SEC // 128
            if si >= 2:
                wbuf = (("td", sec_drain[si - 2]),)
            else:
                wbuf = (("td", sch.now("td")),)
            dma("g", gup_s.ap()[:, buf, 0, :, 0:SEC],
                gateT_h[:, :, ksec0:ksec0 + SEC].rearrange("k p n -> p k n"),
                "dgu", waits=wbuf)
            dma("g", gup_s.ap()[:, buf, 1, :, 0:SEC],
                upT_h[:, :, ksec0:ksec0 + SEC].rearrange("k p n -> p k n"),
                "dgu")
            wg = sch.now("dgu")
            for pr in range(2):
                c0, c1 = 2 * pr, 2 * pr + 1
                ffirst = True
                for ci, cc in enumerate((c0, c1)):
                    for kk in range(KH):
                        mm(psM1[:, ci * 512:ci * 512 + SEC],
                           xqT_s[:, kk, cc * 128:(cc + 1) * 128],
                           gup_s[:, buf, 0, kk, 0:SEC], kk == 0, kk == KH - 1,
                           waits=(("dgu", wg), ("vs", m1_free),
                                  ("vt", xqT_ready)) if ffirst else ())
                        ffirst = False
                ffirst = True
                for ci, cc in enumerate((c0, c1)):
                    for kk in range(KH):
                        mm(psM2[:, ci * 512:ci * 512 + SEC],
                           xqT_s[:, kk, cc * 128:(cc + 1) * 128],
                           gup_s[:, buf, 1, kk, 0:SEC], kk == 0, kk == KH - 1,
                           waits=(("vs", m2_free),) if ffirst else ())
                        ffirst = False
                md = tdrain("td")
                if pr == 1:
                    sec_drain[si] = md
                if pend is not None:
                    tp_drain_prev = pend()
                    pend = None
                w2 = 2 * SEC
                pM1v = psM1.ap().rearrange("p (c n) -> p c n", c=2)[:, :, 0:SEC]
                pM2v = psM2.ap().rearrange("p (c n) -> p c n", c=2)[:, :, 0:SEC]
                gsv = gsec.ap()[:, 0:w2].rearrange("p (c n) -> p c n", c=2)
                q1v = q1b.ap()[:, 0:w2].rearrange("p (c n) -> p c n", c=2)
                qiv = qin.ap()[:, 0:w2].rearrange("p (c n) -> p c n", c=2)
                gv = v_op(lambda e, d=gsv, s_=pM1v: e.tensor_copy(d, s_),
                          waits=(("td", md),), sig="vs")
                m1_free = gv
                sl = s_op(lambda e, w=w2: e.activation(q1b[:, 0:w], gsec[:, 0:w],
                                                       AF.Silu),
                          waits=(("vs", gv),), sig="sv")
                m2_free = v_op(lambda e, d=qiv, s_=pM2v, q_=q1v:
                               e.scalar_tensor_tensor(d, s_, 1.0, q_,
                                                      ALU.mult, ALU.mult),
                               waits=(("sv", sl), ("vq", sch.now("vq"))),
                               sig="vs")
                vq_p = emit_quant(w2, tpsrc.ap()[:, 0:w2],
                                  waits_out=(("td", tp_drain_prev),))

                def make_pend(nk_=nk, k0=ksec0 // 128, cc0=c0, cc1=c1, vq=vq_p):
                    def emit_tp():
                        for ci in range(2):
                            for k in range(nk_):
                                E("t", lambda e, kk=k, cci=ci: e.transpose(
                                    psT[:, (cci * nk_ + kk) * 128:
                                        (cci * nk_ + kk + 1) * 128],
                                    tpsrc[:, (cci * nk_ + kk) * 128:
                                          (cci * nk_ + kk + 1) * 128],
                                    ident_s[:]),
                                  waits=(("vq", vq), ("vt", sch.now("vt")))
                                  if (ci == 0 and k == 0) else ())
                        td_ = tdrain("td")
                        for ci, cc in enumerate((cc0, cc1)):
                            v_op(lambda e, cci=ci, c_=cc:
                                 e.tensor_copy(
                                     hqT_s.ap()[:, k0:k0 + nk_,
                                                c_ * 128:(c_ + 1) * 128],
                                     psT[:, cci * nk_ * 128:(cci + 1) * nk_ * 128]
                                     .rearrange("p (k n) -> p k n", k=nk_)),
                                 waits=(("td", td_),) if ci == 0 else (),
                                 sig="vt")
                        return td_
                    return emit_tp
                pend = make_pend()
            ksec0 += SEC
        if pend is not None:
            tp_drain_prev = pend()
            pend = None

        for pr in range(2):
            c0, c1 = 2 * pr, 2 * pr + 1
            psf = {c0: psM1, c1: psM2}
            grp_drain = {}
            for grp in range(11):
                bufd = grp % 2
                if grp >= 2:
                    wbuf = (("td", grp_drain[grp - 2]),)
                else:
                    wbuf = (("td", sch.now("td")),)
                dma("g", dwn_s.ap()[:, bufd],
                    downT_h[2 * grp:2 * grp + 2].rearrange("k p n -> p k n"),
                    "ddn", waits=wbuf)
                wd = sch.now("ddn")
                for cc in (c0, c1):
                    if grp == 0:
                        for half in range(2):
                            mm(psf[cc][0:128, half * 512:(half + 1) * 512],
                               ones_row[:], betarow[0:1, half * 512:(half + 1) * 512],
                               True, False,
                               waits=(("ddn", wd), ("vq", sch.now("vq")),
                                      ("vt", sch.now("vt"))) if (half == 0 and cc == c0) else ())
                    for kc in range(2):
                        ic = 2 * grp + kc
                        for half in range(2):
                            mm(psf[cc][:, half * 512:(half + 1) * 512],
                               hqT_s[:, ic, cc * 128:(cc + 1) * 128],
                               dwn_s[:, bufd, kc, half * 512:(half + 1) * 512],
                               False, grp == 10 and kc == 1,
                               waits=(("ddn", wd),) if (grp > 0 and kc == 0 and half == 0 and cc == c0) else ())
                grp_drain[grp] = tdrain("td")
            
            for cc in (c0, c1):
                v_op(lambda e, c_=cc, p_=psf[cc]: e.tensor_copy(
                    f_s.ap()[:, c_, :], p_[:]),
                    waits=(("td", sch.now("td")),) if cc == c0 else (), sig="vt")

        for c in range(NCH):
            ffirst = True
            for half in range(2):
                sl = slice(half * 512, (half + 1) * 512)
                mm(psM1[:, sl], ones_row[:], binrow[0:1, sl], True, False,
                   waits=(("vt", sch.now("vt")),) if ffirst else ())
                ffirst = False
                for kk in range(KH):
                    mm(psM1[:, sl], xnT_s[:, kk, c * 128:(c + 1) * 128],
                       Win_s[:, kk, sl], False, kk == KH - 1)
            tdrain("td")
            xv = v_op(lambda e: e.tensor_copy(tpsrc[:], psM1[:]),
                      waits=(("td", sch.now("td")),), sig="vt")
            for k in range(KH):
                E("t", lambda e, kk=k: e.transpose(
                    psT[:, kk * 128:(kk + 1) * 128],
                    tpsrc[:, kk * 128:(kk + 1) * 128], ident_s[:]),
                  waits=(("vt", xv),) if k == 0 else ())
            tdrain("td")
            xv = v_op(lambda e, cc=c: e.tensor_copy(
                xpT_s.ap()[:, :, cc * 128:(cc + 1) * 128],
                psT[:].rearrange("p (k n) -> p k n", k=KH)),
                waits=(("td", sch.now("td")),), sig="vt")
            mm(psZ[:], ones_row[:], czrow[:], True, False, waits=(("vt", xv),))
            for kk in range(KH):
                mm(psZ[:], xpT_s[:, kk, c * 128:(c + 1) * 128], WS3_s[:, kk, :],
                   False, kk == KH - 1)
            tdrain("td")
            v_op(lambda e: e.tensor_copy(zib[:], psZ[:]),
                 waits=(("td", sch.now("td")),))
            zv = vnop(sig="vt")
            for j in range(3):
                E("t", lambda e, jj=j: e.transpose(
                    psZT[0:S, jj * 128:(jj + 1) * 128],
                    zib[:, jj * S:(jj + 1) * S], ident_s[:]),
                  waits=(("vt", zv),) if j == 0 else ())
            tdrain("td")
            v_op(lambda e, cc=c: e.tensor_copy(
                ziT3.ap()[:, :, cc * 128:(cc + 1) * 128],
                psZT[0:S, :].rearrange("p (j n) -> p j n", j=3)),
                waits=(("td", sch.now("td")),), sig="vt")

        sgx = s_op(lambda e: e.activation(gates[:], ziT3[:, 1:3, :], AF.Sigmoid),
                   waits=(("vt", sch.now("vt")),), sig="sv")
        v_op(lambda e: e.tensor_tensor(u_s[:], gates[:, 0, :], ziT3[:, 0, :],
                                       ALU.mult), waits=(("sv", sgx),))
        uw = vsettle2("vv")
        if st_i % ST_PER_SEQ == 0:
            v_op(lambda e: e.tensor_tensor_scan(hs_s[:], gates[:, 1, :], u_s[:],
                                                0.0, ALU.mult, ALU.add),
                 waits=(("vv", uw),))
        else:
            v_op(lambda e: e.tensor_tensor_scan(hs_s[:], gates[:, 1, :], u_s[:],
                                                carry[:], ALU.mult, ALU.add),
                 waits=(("vv", uw),))
        uw = vsettle2("vv")
        v_op(lambda e: e.tensor_copy(carry[:], hs_s[:, TS - 1:TS]),
             waits=(("vv", uw),))
        v_op(lambda e: e.tensor_copy(hsb_s[:], hs_s[:]))
        uw = vsettle2("vv")
        E("v", lambda e: e.engine_nop(), waits=(("vv", uw),))
        hv = vnop(sig="vt")

        for c in range(NCH):
            g = st_i * NCH + c
            ffirst = True
            for half in range(2):
                sl = slice(half * 512, (half + 1) * 512)
                mm(psM1[:, sl], ones_row[:], cprerow[0:1, sl], True, False,
                   waits=(("vt", sch.now("vt")),) if ffirst else ())
                ffirst = False
                for kk in range(KH):
                    mm(psM1[:, sl], xpT_s[:, kk, c * 128:(c + 1) * 128],
                       WD_s[:, kk, sl], False, False)
                mm(psM1[:, sl], hsb_s[:, c * 128:(c + 1) * 128], C_s[0:S, sl],
                   False, True)
            tdrain("td")
            yv = v_op(lambda e: e.tensor_copy(y1pre[:], psM1[:]),
                      waits=(("td", sch.now("td")),), sig="vs")
            ys = s_op(lambda e: e.activation(tpsrc[:], y1pre[:], AF.Tanh),
                      waits=(("vs", yv),), sig="sv")
            for k in range(KH):
                E("t", lambda e, kk=k: e.transpose(
                    psT[:, kk * 128:(kk + 1) * 128],
                    tpsrc[:, kk * 128:(kk + 1) * 128], ident_s[:]),
                  waits=(("sv", ys), ("vt", sch.now("vt"))) if k == 0 else ())
            tdrain("td")
            yv = v_op(lambda e: e.tensor_copy(y1T_c, psT[:]),
                      waits=(("td", sch.now("td")),), sig="vt")
            ffirst = True
            for half in range(2):
                sl = slice(half * 512, (half + 1) * 512)
                mm(psM2[:, sl], ones_row[:], coutrow[0:1, sl], True, False,
                   waits=(("vt", yv),) if ffirst else ())
                ffirst = False
                for kk in range(KH):
                    mm(psM2[:, sl], y1T_c[:, kk * 128:(kk + 1) * 128],
                       Wout_s[:, kk, sl], False, False)
                for kk in range(KH):
                    mm(psM2[:, sl], xnT_s[:, kk, c * 128:(c + 1) * 128],
                       Win_s[:, kk, sl], False, kk == KH - 1)
            last_e_drain = tdrain("td")
            av = v_op(lambda e: e.tensor_copy(y1pre[:], psM2[:]),
                      waits=(("td", last_e_drain),), sig="vs")
            v_op(lambda e: e.bn_stats(bn6[:, 0:6], y1pre[:, 0:512]),
                 waits=(("sv", sch.now("sv")),))
            v_op(lambda e: e.bn_stats(bn6[:, 6:12], y1pre[:, 512:1024]))
            bs = vnop()
            v_op(lambda e: e.bn_aggr(bn2[:], bn6[:]), waits=(("vv", bs),))
            ba = vnop()
            v_op(lambda e: e.tensor_scalar(tiny1[:], bn2[:, 1:2], LN_EPS, None,
                                           ALU.add), waits=(("vv", ba),))
            ba = vnop()
            v_op(lambda e: e.reciprocal(tiny2[:], tiny1[:]), waits=(("vv", ba),))
            v_op(lambda e: e.tensor_copy(tiny1[:], bn2[:, 0:1]))
            vv = vsettle2("vs")
            sq2 = s_op(lambda e: e.activation(tiny3[:], tiny2[:], AF.Sqrt),
                       waits=(("vs", vv),), sig="sv")
            wv = E("v", lambda e: e.engine_nop(), waits=(("sv", sq2),), sig="vv")
            v_op(lambda e: e.scalar_tensor_tensor(tiny2[:], tiny1[:], -1.0,
                                                  tiny3[:], ALU.mult, ALU.mult),
                 waits=(("vv", wv),))
            vv = vnop(sig="vs")
            xs2 = s_op(lambda e: e.activation(xhat_b[:], y1pre[:], AF.Identity,
                                              bias=tiny2[:], scale=tiny3[:]),
                       waits=(("vs", vv),), sig="sv")
            tv = v_op(lambda e: e.tensor_tensor(tpsrc[:], xhat_b[:], gamma_t[:],
                                                ALU.mult),
                      waits=(("sv", xs2),), sig="vs")
            dma("y", gsec.ap(), x_h[g * 128:(g + 1) * 128, :], "dxr",
                waits=(("vs", tv),))
            v_op(lambda e, c_=c: e.tensor_tensor(outc[:], f_s[:, c_, :],
                                                 gsec[:], ALU.add),
                 waits=(("dxr", sch.now("dxr")), ("dn", sch.now("dn"))))
            ov = v_op(lambda e: e.tensor_tensor(outc[:], outc[:], tpsrc[:],
                                                ALU.add), sig="vy")
            dma("y", out_h[g * 128:(g + 1) * 128, :], outc.ap(), "dn",
                waits=(("vy", ov),))

    E("y", lambda e: e.nop(), waits=(("dn", sch.now("dn")),))

    sems = {}
    import contextlib
    with contextlib.ExitStack() as ctx:
        def get_sem(name):
            if name not in sems:
                sems[name] = ctx.enter_context(nc.semaphore(name))
            return sems[name]
        for name in sch.cnt:
            get_sem(name)
        block = ctx.enter_context(nc.Block())
        eng_attr = {"t": "tensor", "v": "vector", "s": "scalar", "y": "sync",
                    "g": "gpsimd"}
        for ename, attr in eng_attr.items():
            ops = sch.q[ename]
            if not ops:
                continue
            def make_section(ops_):
                def section(eng):
                    for fn, waits, sig, inc in ops_:
                        for sname, val in waits:
                            if val > 0:
                                eng.wait_ge(get_sem(sname), val)
                        inst = fn(eng)
                        if sig is not None:
                            assert inst is not None, "signal op returned None"
                            inst.then_inc(get_sem(sig), inc)
                return section
            getattr(block, attr)(make_section(ops))
    return nc


_CACHED = {}


def kernel(**inputs):
    import ml_dtypes
    x = np.ascontiguousarray(inputs["L0_attn_x"], np.float32).reshape(B_FULL * T, H)
    ident = np.eye(128, dtype=ml_dtypes.bfloat16)
    onesr = np.ones((1, 128), dtype=ml_dtypes.bfloat16)
    row = lambda v: np.ascontiguousarray(np.asarray(v, np.float32).reshape(1, -1))
    common = {
        "W_in": np.ascontiguousarray(inputs["L0_attn_W_in"], np.float32),
        "W_B": np.ascontiguousarray(inputs["L0_attn_W_B"], np.float32),
        "W_D": np.ascontiguousarray(inputs["L0_attn_W_D"], np.float32),
        "W_out": np.ascontiguousarray(inputs["L0_attn_W_out"], np.float32),
        "B_mat": np.ascontiguousarray(inputs["L0_attn_B_mat"], np.float32),
        "C_mat": np.ascontiguousarray(inputs["L0_attn_C_mat"], np.float32),
        "W_ig": np.ascontiguousarray(inputs["L0_attn_W_ig"], np.float32),
        "W_fg": np.ascontiguousarray(inputs["L0_attn_W_fg"], np.float32),
        "b_in": row(inputs["L0_attn_b_in"]),
        "b_B": row(inputs["L0_attn_b_B"]),
        "b_D": row(inputs["L0_attn_b_D"]),
        "b_out": row(inputs["L0_attn_b_out"]),
        "b_ig": row(inputs["L0_attn_b_ig"]),
        "b_fg": row(inputs["L0_attn_b_fg"]),
        "gamma": row(inputs["L0_attn_gamma"]),
        "beta": row(inputs["L0_attn_beta"]),
        "w_gate": np.ascontiguousarray(inputs["L0_ffn_w_gate"], np.float32),
        "w_up": np.ascontiguousarray(inputs["L0_ffn_w_up"], np.float32),
        "w_down": np.ascontiguousarray(inputs["L0_ffn_w_down"], np.float32),
        "ident": ident,
        "onesr": onesr,
    }
    if "nc" not in _CACHED:
        _CACHED["nc"] = build_program()
    nc = _CACHED["nc"]
    in_maps = []
    for ci in range(NCORES):
        m = dict(common)
        m["x"] = np.ascontiguousarray(x[ci * R:(ci + 1) * R])
        in_maps.append(m)
    res = run_bass_kernel_spmd(nc, in_maps, core_ids=list(range(NCORES)))
    out = np.concatenate([r["out"] for r in res.results], axis=0)
    return out.reshape(B_FULL, T, H)


if __name__ == "__main__":
    pass


# version 12
# speedup vs baseline: 1.4122x; 1.0939x over previous
import numpy as np

import concourse.bass as bass
import concourse.mybir as mybir
from concourse.bass_utils import run_bass_kernel_spmd

F32 = mybir.dt.float32
BF16 = mybir.dt.bfloat16
AF = mybir.ActivationFunctionType
ALU = mybir.AluOpType
AX = mybir.AxisListType

NCORES = 8
H = 1024
S = 16
II = 2816
B_FULL, T = 32, 2048
R = (B_FULL // NCORES) * T
CH = 128
TS = 512
NCH = TS // CH
import os as _os
NST = int(_os.environ.get("K_NST", R // TS))
ST_PER_SEQ = T // TS
KH = H // 128
KI = II // 128
SECS = [512, 512, 512, 512, 512, 256]
LN_EPS = 1e-5


class Sched:
    ENGS = ("t", "v", "s", "y", "g")

    def __init__(self):
        self.q = {e: [] for e in self.ENGS}
        self.cnt = {}

    def emit(self, eng, fn, waits=(), sig=None, inc=1):
        self.q[eng].append((fn, tuple(waits), sig, inc))
        if sig is not None:
            self.cnt[sig] = self.cnt.get(sig, 0) + inc
        return self.cnt.get(sig, 0) if sig else 0

    def now(self, sem):
        return self.cnt.get(sem, 0)


def build_program():
    nc = bass.Bass("TRN2", target_bir_lowering=False,
                   detect_race_conditions=False)

    dT = lambda n, sh, dt=F32, kind="ExternalInput": nc.dram_tensor(n, sh, dt, kind=kind).ap()
    x_h = dT("x", [R, H])
    W_in_h = dT("W_in", [H, H])
    W_B_h = dT("W_B", [H, H])
    W_D_h = dT("W_D", [H, H])
    W_out_h = dT("W_out", [H, H])
    B_mat_h = dT("B_mat", [H, S])
    C_mat_h = dT("C_mat", [S, H])
    W_ig_h = dT("W_ig", [H, S])
    W_fg_h = dT("W_fg", [H, S])
    b_in_h = dT("b_in", [1, H])
    b_B_h = dT("b_B", [1, H])
    b_D_h = dT("b_D", [1, H])
    b_out_h = dT("b_out", [1, H])
    b_ig_h = dT("b_ig", [1, S])
    b_fg_h = dT("b_fg", [1, S])
    gamma_h = dT("gamma", [1, H])
    beta_h = dT("beta", [1, H])
    w_gate_h = dT("w_gate", [II, H])
    w_up_h = dT("w_up", [II, H])
    w_down_h = dT("w_down", [H, II])
    ident_h = dT("ident", [128, 128], BF16)
    onesr_h = dT("onesr", [1, 128], BF16)
    out_h = dT("out", [R, H], F32, kind="ExternalOutput")

    gateT_h = nc.dram_tensor("gateT_q", [KH, 128, II], BF16, kind="Internal").ap()
    upT_h = nc.dram_tensor("upT_q", [KH, 128, II], BF16, kind="Internal").ap()
    downT_h = nc.dram_tensor("downT_q", [KI, 128, H], BF16, kind="Internal").ap()

    sb = lambda n, sh, dt: nc.alloc_sbuf_tensor(n, sh, dt)
    Win_s = sb("Win_s", [128, KH, H], BF16)
    WD_s = sb("WD_s", [128, KH, H], BF16)
    Wout_s = sb("Wout_s", [128, KH, H], BF16)
    WS3_s = sb("WS3_s", [128, KH, 3 * S], BF16)
    Bm_s = sb("Bm_s", [128, KH, S], BF16)
    C_s = sb("C_s", [S, H], BF16)
    gamma_t = sb("gamma_t", [128, H], BF16)
    ident_s = sb("ident_s", [128, 128], BF16)
    ones_row = sb("ones_row", [1, 128], BF16)
    binrow = sb("binrow", [1, H], BF16)
    betarow = sb("betarow", [1, H], BF16)
    czrow = sb("czrow", [1, 3 * S], BF16)
    cprerow = sb("cprerow", [1, H], BF16)
    coutrow = sb("coutrow", [1, H], BF16)
    bincol = sb("bincol", [128, KH], BF16)
    bBcol = sb("bBcol", [128, KH], BF16)
    biasrow48 = sb("biasrow48", [1, 3 * S], BF16)
    bias_p1 = sb("bias_p1", [128, 1], F32)
    bias_n1 = sb("bias_n1", [128, 1], F32)
    xnT_s = sb("xnT_s", [128, KH, TS], BF16)
    xqT_s = sb("xqT_s", [128, KH, TS], BF16)
    xpT_s = sb("xpT_s", [128, KH, TS], BF16)
    hqT_s = sb("hqT_s", [128, KI, TS], BF16)
    f_s = sb("f_s", [128, NCH, H], BF16)
    ziT3 = sb("ziT3", [S, 3, TS], BF16)
    gates = sb("gates", [S, 2, TS], F32)
    u_s = sb("u_s", [S, TS], F32)
    hs_s = sb("hs_s", [S, TS], F32)
    hsb_s = sb("hsb_s", [S, TS], BF16)
    carry = sb("carry", [S, 1], F32)
    gup_s = sb("gup_s", [128, 2, 2, KH, 512], BF16)
    dwn_s = sb("dwn_s", [128, 2, 2, H], BF16)
    xin_s = sb("xin_s", [128, 2, H], F32)
    xnb_s = sb("xnb_s", [128, H], BF16)
    tpsrc = sb("tpsrc", [128, H], BF16)
    qin = sb("qin", [128, H], F32)
    sm2 = sb("sm2", [128, H], F32)
    a2q = sb("a2q", [128, H], F32)
    q1b = sb("q1b", [128, H], F32)
    q2b = sb("q2b", [128, H], F32)
    bmq = sb("bmq", [128, H // 16], F32)
    scq = sb("scq", [128, H // 16], F32)
    rcq = sb("rcq", [128, H // 16], F32)
    gsec = sb("gsec", [128, H], F32)
    bn6 = sb("bn6", [128, 12], F32)
    bn2 = sb("bn2", [128, 2], F32)
    tiny1 = sb("tiny1", [128, 2], F32)
    tiny2 = sb("tiny2", [128, 2], F32)
    tiny3 = sb("tiny3", [128, 2], F32)
    zib = sb("zib", [128, 3 * S], BF16)
    sgp = gsec
    y1pre = sm2
    y1T_t = sb("y1T_t", [128, KH * 128], BF16)
    y1T_c = y1T_t.ap()[:, 0:KH * 128]
    rowA = gsec.ap()[0:1, :]
    rowB = sm2.ap()[0:1, :]
    rowbf = tpsrc.ap()[0:1, :]
    bdrow = xnb_s.ap()[0:1, :]
    xhat_b = xnb_s
    outc = qin

    psM1 = nc.alloc_psum_tensor("psM1", [128, H], F32)
    psM2 = nc.alloc_psum_tensor("psM2", [128, H], F32)
    psT = nc.alloc_psum_tensor("psT", [128, 1024], BF16)
    psZ = nc.alloc_psum_tensor("psZ", [128, 3 * S], F32)
    psZT = nc.alloc_psum_tensor("psZT", [S, 3 * 128], BF16)

    sch = Sched()
    E = sch.emit

    def dma(q_, dst, src, sig, waits=()):
        return E(q_, lambda e, d=dst, s=src: e.dma_start(out=d, in_=s),
                 waits=waits, sig=sig, inc=16)

    def tdrain(sig, waits=()):
        def fn(e):
            e.drain()
            return e.nop()
        return E("t", fn, waits=waits, sig=sig)

    def mm(out_ap, lhsT, rhs, start, stop, waits=()):
        return E("t", lambda e, o=out_ap, l=lhsT, r=rhs, a=start, b=stop:
                 e.matmul(o, l, r, start=a, stop=b), waits=waits)

    def v_op(fn, waits=(), sig=None):
        return E("v", fn, waits=waits, sig=sig)

    def vnop(sig="vv"):
        return E("v", lambda e: e.engine_nop(), sig=sig)

    def vsettle2(sig):
        t = E("v", lambda e: e.engine_nop(), sig="vv")
        return E("v", lambda e: e.engine_nop(), waits=(("vv", t),), sig=sig)

    def snop(sig="sv"):
        return E("s", lambda e: e.nop(cycle_cnt=200), sig=sig)

    def s_op(fn, waits=(), sig=None):
        return E("s", fn, waits=waits, sig=sig)

    def transposes(src_ap, n, waits=()):
        for k in range(n):
            E("t", lambda e, kk=k, sa=src_ap: e.transpose(
                psT[:, kk * 128:(kk + 1) * 128], sa[:, kk * 128:(kk + 1) * 128],
                ident_s[:]), waits=waits if k == 0 else ())
        return tdrain("td")

    def emit_quant(w, qout_ap, waits_v=(), waits_out=()):
        nb = w // 16
        qin3 = qin.ap()[:, 0:w].rearrange("p (b c) -> p b c", c=16)
        sm23 = sm2.ap()[:, 0:w].rearrange("p (b c) -> p b c", c=16)
        rcb = rcq.ap()[:, 0:nb].unsqueeze(2).broadcast_to([128, nb, 16])
        scb = scq.ap()[:, 0:nb].unsqueeze(2).broadcast_to([128, nb, 16])
        v_op(lambda e: e.tensor_reduce(bmq[:, 0:nb].unsqueeze(2), qin3,
                                       AX.X, ALU.max,
                                       apply_absolute_value=True),
             waits=waits_v)
        rv = vnop()
        v_op(lambda e: e.tensor_scalar(scq[:, 0:nb], bmq[:, 0:nb], 1.0 / 24.0,
                                       2.5e-13, ALU.mult, ALU.max),
             waits=(("vv", rv),))
        rv = vnop()
        v_op(lambda e: e.reciprocal(rcq[:, 0:nb], scq[:, 0:nb]),
             waits=(("vv", rv),))
        rv = vsettle2("vv")
        mv = v_op(lambda e: e.scalar_tensor_tensor(sm23, qin3, 1.0, rcb,
                                                   ALU.mult, ALU.mult),
                  waits=(("vv", rv),), sig="vs")
        s_op(lambda e: e.activation(sgp[:, 0:w], sm2[:, 0:w], AF.Sign,
                                    bias=bias_n1[:], scale=1.0),
             waits=(("vs", mv),))
        ab = s_op(lambda e: e.activation(a2q[:, 0:w], sm2[:, 0:w], AF.Abs),
                  waits=(("td", sch.now("td")), ("dn", sch.now("dn"))), sig="sv")
        v_op(lambda e: e.tensor_scalar(q1b[:, 0:w], a2q[:, 0:w], 3.0, None,
                                       ALU.is_gt), waits=(("sv", ab),))
        v_op(lambda e: e.scalar_tensor_tensor(q1b[:, 0:w], a2q[:, 0:w], 5.0,
                                              q1b[:, 0:w], ALU.is_gt, ALU.add))
        v_op(lambda e: e.scalar_tensor_tensor(q1b[:, 0:w], a2q[:, 0:w], 7.0,
                                              q1b[:, 0:w], ALU.is_gt, ALU.add))
        v_op(lambda e: e.tensor_scalar(q2b[:, 0:w], a2q[:, 0:w], 10.0, None,
                                       ALU.is_gt))
        v_op(lambda e: e.scalar_tensor_tensor(q2b[:, 0:w], a2q[:, 0:w], 14.0,
                                              q2b[:, 0:w], ALU.is_gt, ALU.add))
        v_op(lambda e: e.tensor_scalar(qin[:, 0:w], a2q[:, 0:w], 20.0, None,
                                       ALU.is_gt))
        v_op(lambda e: e.scalar_tensor_tensor(q2b[:, 0:w], qin[:, 0:w], 2.0,
                                              q2b[:, 0:w], ALU.mult, ALU.add))
        a2f = v_op(lambda e: e.scalar_tensor_tensor(q1b[:, 0:w], q2b[:, 0:w], 2.0,
                                                    q1b[:, 0:w], ALU.mult, ALU.add),
                   sig="vs")
        ssig = s_op(lambda e: e.activation(a2q[:, 0:w], sm2[:, 0:w], AF.Sign,
                                           bias=bias_p1[:], scale=1.0),
                    waits=(("vs", a2f),), sig="sv")
        v_op(lambda e: e.tensor_tensor(sgp[:, 0:w], sgp[:, 0:w], a2q[:, 0:w],
                                       ALU.add), waits=(("sv", ssig),))
        v_op(lambda e: e.scalar_tensor_tensor(q2b[:, 0:w], q1b[:, 0:w], 1.0,
                                              sgp[:, 0:w], ALU.add, ALU.mult))
        q23 = q2b.ap()[:, 0:w].rearrange("p (b c) -> p b c", c=16)
        qo3 = qout_ap.rearrange("p (b c) -> p b c", c=16)
        return v_op(lambda e: e.scalar_tensor_tensor(qo3, q23, 1.0, scb,
                                                     ALU.mult, ALU.mult),
                    waits=waits_out, sig="vq")

    v_op(lambda e: e.memset(bias_p1[:], 1.0))
    v_op(lambda e: e.memset(bias_n1[:], -1.0))
    v_op(lambda e: e.memset(biasrow48[:], 0.0))

    dma("y", ident_s.ap(), ident_h, "dc")
    dma("y", ones_row.ap(), onesr_h, "dc")
    for dst, src in ((Win_s, W_in_h), (WD_s, W_D_h), (Wout_s, W_out_h)):
        dma("g", dst.ap(), src.rearrange("(k p) n -> p k n", p=128), "dwc")
    dma("g", WS3_s.ap()[:, :, S:2 * S], W_ig_h.rearrange("(k p) n -> p k n", p=128), "dwc")
    dma("g", WS3_s.ap()[:, :, 2 * S:3 * S], W_fg_h.rearrange("(k p) n -> p k n", p=128), "dwc")
    dma("g", Bm_s.ap(), B_mat_h.rearrange("(k p) n -> p k n", p=128), "dwc")
    dma("g", C_s.ap(), C_mat_h, "dwc")
    WB_v = gup_s.ap()[:, 0].rearrange("p m k n -> p (m k n)").rearrange(
        "p (a b) -> p a b", b=H)
    WBT_v = gup_s.ap()[:, 1].rearrange("p m k n -> p (m k n)").rearrange(
        "p (a b) -> p a b", b=H)
    dma("g", WB_v, W_B_h.rearrange("(k p) n -> p k n", p=128), "dwb")

    dma("y", rowA, b_in_h, "drA")
    cv = v_op(lambda e: e.tensor_copy(binrow[:], rowA),
              waits=(("drA", 16), ("dc", 32)), sig="vt")
    for k in range(KH):
        E("t", lambda e, kk=k: e.transpose(psT[:, 2 * kk:2 * kk + 1],
                                           binrow[0:1, kk * 128:(kk + 1) * 128],
                                           ident_s[0:1, 0:1]),
          waits=(("vt", cv),) if k == 0 else ())
    tdrain("td")
    v_op(lambda e: e.tensor_copy(bincol[:],
                                 psT.ap()[:, 0:2 * KH].rearrange(
                                     "p (k two) -> p k two", two=2)[:, :, 0]),
         waits=(("td", sch.now("td")),))
    vnop(sig="vt")
    dma("y", rowB, b_out_h, "drB")
    v_op(lambda e: e.tensor_tensor(coutrow[:], rowA, rowB, ALU.add),
         waits=(("drB", 16),), sig="vt")
    dma("y", rowA, b_B_h, "drA", waits=(("vt", sch.now("vt")),))
    cv = v_op(lambda e: e.tensor_copy(rowbf, rowA),
              waits=(("drA", 32),), sig="vt")
    for k in range(KH):
        E("t", lambda e, kk=k: e.transpose(psT[:, 2 * kk:2 * kk + 1],
                                           rowbf[0:1, kk * 128:(kk + 1) * 128],
                                           ident_s[0:1, 0:1]),
          waits=(("vt", cv),) if k == 0 else ())
    tdrain("td")
    v_op(lambda e: e.tensor_copy(bBcol[:],
                                 psT.ap()[:, 0:2 * KH].rearrange(
                                     "p (k two) -> p k two", two=2)[:, :, 0]),
         waits=(("td", sch.now("td")),))
    vnop(sig="vt")
    dma("y", rowA, b_D_h, "drA", waits=(("vt", sch.now("vt")),))
    cv = v_op(lambda e: e.tensor_copy(bdrow, rowA), waits=(("drA", 48),),
              sig="vt")
    dma("y", rowA, gamma_h, "drA", waits=(("vt", cv),))
    cv = v_op(lambda e: e.tensor_copy(rowbf, rowA), waits=(("drA", 64),),
              sig="vt")
    for half in range(2):
        mm(psM1[:, half * 512:(half + 1) * 512], ones_row[:],
           rowbf[0:1, half * 512:(half + 1) * 512], True, True,
           waits=(("vt", cv),) if half == 0 else ())
    tdrain("td")
    v_op(lambda e: e.tensor_copy(gamma_t[:], psM1[:]),
         waits=(("td", sch.now("td")),), sig="vt")
    dma("y", rowA, beta_h, "drA", waits=(("vt", sch.now("vt")),))
    cv = v_op(lambda e: e.tensor_copy(betarow[:], rowA), waits=(("drA", 80),),
              sig="vt")
    dma("y", rowA[0:1, 0:S], b_ig_h, "drA", waits=(("vt", cv),))
    dma("y", rowA[0:1, S:2 * S], b_fg_h, "drA")
    cv = v_op(lambda e: e.tensor_copy(biasrow48[0:1, S:2 * S], rowA[0:1, 0:S]),
              waits=(("drA", 112),), sig="vt")
    v_op(lambda e: e.tensor_copy(biasrow48[0:1, 2 * S:3 * S],
                                 rowA[0:1, S:2 * S]))
    cv = vnop(sig="vt")

    for kk in range(KH):
        for jj in range(KH):
            E("t", lambda e, a=kk, b=jj: e.transpose(
                psT[:, b * 128:(b + 1) * 128],
                WB_v[:, b, a * 128:(a + 1) * 128], ident_s[:]),
              waits=((("dwb", 16),) if kk == 0 else ()) +
                    (("vt", sch.now("vt")),) if jj == 0 else ())
        tdrain("td")
        v_op(lambda e, a=kk: e.tensor_copy(WBT_v[:, a, :], psT[:]),
             waits=(("td", sch.now("td")),), sig="vt")
    for r in range(KH):
        for kk in range(KH):
            mm(psM2[:, 0:S], WBT_v[:, kk, r * 128:(r + 1) * 128], Bm_s[:, kk, :],
               kk == 0, kk == KH - 1,
               waits=((("dwc", sch.now("dwc")),) if r == 0 else ()) +
                     (("vt", sch.now("vt")),) if kk == 0 else ())
        tdrain("td")
        v_op(lambda e, rr=r: e.tensor_copy(WS3_s[:, rr, 0:S], psM2[:, 0:S]),
             waits=(("td", sch.now("td")),))
        vnop(sig="vt")

    for kk in range(KH):
        mm(psZ[0:S, 0:1], Bm_s[:, kk, :], bBcol[:, kk:kk + 1], kk == 0,
           kk == KH - 1, waits=(("vt", sch.now("vt")),) if kk == 0 else ())
    tdrain("td")
    v_op(lambda e: e.tensor_copy(zib[0:S, 0:1], psZ[0:S, 0:1]),
         waits=(("td", sch.now("td")),))
    vnop(sig="vt")
    E("t", lambda e: e.transpose(psZT[0:1, 0:S], zib[0:S, 0:1],
                                 ident_s[0:S, 0:S]),
      waits=(("vt", sch.now("vt")),))
    tdrain("td")
    cw = vnop()
    v_op(lambda e: e.tensor_copy(czrow[:], biasrow48[:]),
         waits=(("td", sch.now("td")), ("vv", cw)))
    cw = vnop()
    v_op(lambda e: e.tensor_tensor(czrow[0:1, 0:S], czrow[0:1, 0:S],
                                   psZT[0:1, 0:S], ALU.add),
         waits=(("vv", cw),))
    vnop(sig="vt")

    v_op(lambda e: e.tensor_copy(cprerow[:], bdrow))
    vnop(sig="vt")

    for mat_i, (src_h, dstT_h, nchunk, width) in enumerate(
            [(w_gate_h, gateT_h, KI, H), (w_up_h, upT_h, KI, H),
             (w_down_h, downT_h, KH, II)]):
        for icx in range(nchunk):
            subs = [(0, 1024), (1024, 2048), (2048, 2816)] if width == II else [(0, 1024)]
            for s0_, s1_ in subs:
                wsub = s1_ - s0_
                nk = wsub // 128
                dma("y", qin.ap()[:, 0:wsub],
                    src_h[icx * 128:(icx + 1) * 128, s0_:s1_], "dfw",
                    waits=(("vq", sch.now("vq")),))
                emit_quant(wsub, tpsrc.ap()[:, 0:wsub],
                           waits_v=(("dfw", sch.now("dfw")),))
                for k in range(nk):
                    E("t", lambda e, kk=k: e.transpose(
                        psT[:, kk * 128:(kk + 1) * 128],
                        tpsrc[:, kk * 128:(kk + 1) * 128], ident_s[:]),
                      waits=(("vq", sch.now("vq")), ("dn", sch.now("dn"))) if k == 0 else ())
                tdrain("td")
                v_op(lambda e, n_=nk: e.tensor_copy(y1T_c[:, 0:n_ * 128],
                                                    psT[:, 0:n_ * 128]),
                     waits=(("td", sch.now("td")),), sig="vt")
                dst = dstT_h[s0_ // 128:(s0_ // 128) + nk, :,
                             icx * 128:(icx + 1) * 128].rearrange(
                    "k p n -> p k n")
                src_ap = y1T_c[:, 0:nk * 128].rearrange(
                    "p (k n) -> p k n", k=nk)
                dma("y", dst, src_ap, "dn", waits=(("vt", sch.now("vt")),))

    dma("y", xin_s.ap()[:, 0, :], x_h[0:128, :], "dx",
        waits=(("dn", sch.now("dn")),))
    dma("y", xin_s.ap()[:, 1, :], x_h[128:256, :], "dx")
    n_dx = 2

    last_e_drain = sch.now("td")
    store_dn = {}
    last_fdown_read = sch.now("td")
    for st_i in range(NST):
        p1_ss = {}
        p1_tdXQ = None
        p1_xqc = None
        sv_base = sch.now("sv")
        for c in range(NCH):
            g = st_i * NCH + c
            par = c % 2
            t1c, t2c, t3c = (tiny1.ap()[:, par:par + 1],
                             tiny2.ap()[:, par:par + 1],
                             tiny3.ap()[:, par:par + 1])
            slot = g % 2
            dxw = 16 * (g + 1)
            xa = xin_s.ap()[:, slot, :]
            wA = (("dx", dxw),
                  ("sv", p1_ss[g - 2]) if g - 2 in p1_ss else ("sv", sv_base))
            v_op(lambda e, a=xa: e.bn_stats(bn6[:, 0:6], a[:, 0:512]), waits=wA)
            v_op(lambda e, a=xa: e.bn_stats(bn6[:, 6:12], a[:, 512:1024]))
            bs = vnop()
            v_op(lambda e: e.bn_aggr(bn2[:], bn6[:]), waits=(("vv", bs),))
            ba = vnop()
            v_op(lambda e, d=t1c: e.tensor_scalar(d, bn2[:, 1:2], LN_EPS, None,
                                                  ALU.add), waits=(("vv", ba),))
            ba = vnop()
            v_op(lambda e, d=t2c, s_=t1c: e.reciprocal(d, s_),
                 waits=(("vv", ba),))
            v_op(lambda e, d=t1c: e.tensor_copy(d, bn2[:, 0:1]))
            vv = vsettle2("vs")
            sq = s_op(lambda e, d=t3c, s_=t2c: e.activation(d, s_, AF.Sqrt),
                      waits=(("vs", vv),), sig="sv")
            wv = E("v", lambda e: e.engine_nop(), waits=(("sv", sq),), sig="vv")
            v_op(lambda e, d=t2c, m=t1c, r=t3c:
                 e.scalar_tensor_tensor(d, m, -1.0, r, ALU.mult, ALU.mult),
                 waits=(("vv", wv),))
            vv = vnop(sig="vs")
            ss = s_op(lambda e, a=xa, b=t2c, s_=t3c:
                      e.activation(qin[:], a, AF.Identity, bias=b, scale=s_),
                      waits=(("vs", vv), ("vq", sch.now("vq")),
                             ("dn", sch.now("dn"))), sig="sv")
            wxn = (("td", p1_tdXN),) if c > 0 else ()
            ss = s_op(lambda e, a=xa, b=t2c, s_=t3c:
                      e.activation(xnb_s[:], a, AF.Identity, bias=b, scale=s_),
                      waits=wxn, sig="sv")
            p1_ss[g] = ss
            if g + 2 < NST * NCH:
                gg = g + 2
                dma("y", xin_s.ap()[:, gg % 2, :],
                    x_h[gg * 128:(gg + 1) * 128, :], "dx",
                    waits=(("sv", ss),))
                n_dx += 1
            if p1_tdXQ is not None:
                ew = (("td", p1_tdXQ),)
                if p1_xqc == 0:
                    ew = ew + (("td", last_e_drain),)
                xq_ev = v_op(lambda e, cc=p1_xqc: e.tensor_copy(
                    xqT_s.ap()[:, :, cc * 128:(cc + 1) * 128],
                    psT[:].rearrange("p (k n) -> p k n", k=KH)),
                    waits=ew, sig="vt")
                psT_free = xq_ev
            else:
                psT_free = sch.now("vt")
            for k in range(KH):
                E("t", lambda e, kk=k: e.transpose(
                    psT[:, kk * 128:(kk + 1) * 128],
                    xnb_s[:, kk * 128:(kk + 1) * 128], ident_s[:]),
                  waits=(("sv", ss), ("vt", psT_free)) if k == 0 else ())
            p1_tdXN = tdrain("td")
            emit_quant(H, tpsrc.ap()[:, 0:H], waits_v=(("sv", ss),))
            vq_c = sch.now("vq")
            dw = (("td", p1_tdXN),)
            if c == 0:
                dw = dw + (("td", last_e_drain),)
            xn_ev = v_op(lambda e, cc=c: e.tensor_copy(
                xnT_s.ap()[:, :, cc * 128:(cc + 1) * 128],
                psT[:].rearrange("p (k n) -> p k n", k=KH)),
                waits=dw, sig="vt")
            for k in range(KH):
                E("t", lambda e, kk=k: e.transpose(
                    psT[:, kk * 128:(kk + 1) * 128],
                    tpsrc[:, kk * 128:(kk + 1) * 128], ident_s[:]),
                  waits=(("vq", vq_c), ("vt", xn_ev)) if k == 0 else ())
            p1_tdXQ = tdrain("td")
            p1_xqc = c
        v_op(lambda e, cc=p1_xqc: e.tensor_copy(
            xqT_s.ap()[:, :, cc * 128:(cc + 1) * 128],
            psT[:].rearrange("p (k n) -> p k n", k=KH)),
            waits=(("td", p1_tdXQ),), sig="vt")

        ksec0 = 0
        sec_drain = {}
        m1_free = sch.now("vs")
        m2_free = sch.now("vs")
        xqT_ready = sch.now("vt")
        pend = None
        tp_drain_prev = sch.now("td")
        for si, SEC in enumerate(SECS):
            buf = si % 2
            nk = SEC // 128
            if si >= 2:
                wbuf = (("td", sec_drain[si - 2]),)
            else:
                wbuf = (("td", sch.now("td")),)
            dma("g", gup_s.ap()[:, buf, 0, :, 0:SEC],
                gateT_h[:, :, ksec0:ksec0 + SEC].rearrange("k p n -> p k n"),
                "dgu", waits=wbuf)
            dma("g", gup_s.ap()[:, buf, 1, :, 0:SEC],
                upT_h[:, :, ksec0:ksec0 + SEC].rearrange("k p n -> p k n"),
                "dgu")
            wg = sch.now("dgu")
            for pr in range(2):
                c0, c1 = 2 * pr, 2 * pr + 1
                ffirst = True
                for ci, cc in enumerate((c0, c1)):
                    for kk in range(KH):
                        mm(psM1[:, ci * 512:ci * 512 + SEC],
                           xqT_s[:, kk, cc * 128:(cc + 1) * 128],
                           gup_s[:, buf, 0, kk, 0:SEC], kk == 0, kk == KH - 1,
                           waits=(("dgu", wg), ("vs", m1_free),
                                  ("vt", xqT_ready)) if ffirst else ())
                        ffirst = False
                ffirst = True
                for ci, cc in enumerate((c0, c1)):
                    for kk in range(KH):
                        mm(psM2[:, ci * 512:ci * 512 + SEC],
                           xqT_s[:, kk, cc * 128:(cc + 1) * 128],
                           gup_s[:, buf, 1, kk, 0:SEC], kk == 0, kk == KH - 1,
                           waits=(("vs", m2_free),) if ffirst else ())
                        ffirst = False
                md = tdrain("td")
                if pr == 1:
                    sec_drain[si] = md
                if pend is not None:
                    tp_drain_prev = pend()
                    pend = None
                w2 = 2 * SEC
                pM1v = psM1.ap().rearrange("p (c n) -> p c n", c=2)[:, :, 0:SEC]
                pM2v = psM2.ap().rearrange("p (c n) -> p c n", c=2)[:, :, 0:SEC]
                gsv = gsec.ap()[:, 0:w2].rearrange("p (c n) -> p c n", c=2)
                q1v = q1b.ap()[:, 0:w2].rearrange("p (c n) -> p c n", c=2)
                qiv = qin.ap()[:, 0:w2].rearrange("p (c n) -> p c n", c=2)
                gv = v_op(lambda e, d=gsv, s_=pM1v: e.tensor_copy(d, s_),
                          waits=(("td", md),), sig="vs")
                m1_free = gv
                sl = s_op(lambda e, w=w2: e.activation(q1b[:, 0:w], gsec[:, 0:w],
                                                       AF.Silu),
                          waits=(("vs", gv),), sig="sv")
                m2_free = v_op(lambda e, d=qiv, s_=pM2v, q_=q1v:
                               e.scalar_tensor_tensor(d, s_, 1.0, q_,
                                                      ALU.mult, ALU.mult),
                               waits=(("sv", sl), ("vq", sch.now("vq"))),
                               sig="vs")
                vq_p = emit_quant(w2, tpsrc.ap()[:, 0:w2],
                                  waits_out=(("td", tp_drain_prev),))

                def make_pend(nk_=nk, k0=ksec0 // 128, cc0=c0, cc1=c1, vq=vq_p):
                    def emit_tp():
                        for ci in range(2):
                            for k in range(nk_):
                                E("t", lambda e, kk=k, cci=ci: e.transpose(
                                    psT[:, (cci * nk_ + kk) * 128:
                                        (cci * nk_ + kk + 1) * 128],
                                    tpsrc[:, (cci * nk_ + kk) * 128:
                                          (cci * nk_ + kk + 1) * 128],
                                    ident_s[:]),
                                  waits=(("vq", vq), ("vt", sch.now("vt")))
                                  if (ci == 0 and k == 0) else ())
                        td_ = tdrain("td")
                        for ci, cc in enumerate((cc0, cc1)):
                            v_op(lambda e, cci=ci, c_=cc:
                                 e.tensor_copy(
                                     hqT_s.ap()[:, k0:k0 + nk_,
                                                c_ * 128:(c_ + 1) * 128],
                                     psT[:, cci * nk_ * 128:(cci + 1) * nk_ * 128]
                                     .rearrange("p (k n) -> p k n", k=nk_)),
                                 waits=(("td", td_),) if ci == 0 else (),
                                 sig="vt")
                        return td_
                    return emit_tp
                pend = make_pend()
            ksec0 += SEC
        if pend is not None:
            tp_drain_prev = pend()
            pend = None

        for pr in range(2):
            c0, c1 = 2 * pr, 2 * pr + 1
            psf = {c0: psM1, c1: psM2}
            grp_drain = {}
            for grp in range(11):
                bufd = grp % 2
                if grp >= 2:
                    wbuf = (("td", grp_drain[grp - 2]),)
                else:
                    wbuf = (("td", sch.now("td")),)
                dma("g", dwn_s.ap()[:, bufd],
                    downT_h[2 * grp:2 * grp + 2].rearrange("k p n -> p k n"),
                    "ddn", waits=wbuf)
                wd = sch.now("ddn")
                for cc in (c0, c1):
                    if grp == 0:
                        for half in range(2):
                            mm(psf[cc][0:128, half * 512:(half + 1) * 512],
                               ones_row[:], betarow[0:1, half * 512:(half + 1) * 512],
                               True, False,
                               waits=(("ddn", wd), ("vq", sch.now("vq")),
                                      ("vt", sch.now("vt"))) if (half == 0 and cc == c0) else ())
                    for kc in range(2):
                        ic = 2 * grp + kc
                        for half in range(2):
                            mm(psf[cc][:, half * 512:(half + 1) * 512],
                               hqT_s[:, ic, cc * 128:(cc + 1) * 128],
                               dwn_s[:, bufd, kc, half * 512:(half + 1) * 512],
                               False, grp == 10 and kc == 1,
                               waits=(("ddn", wd),) if (grp > 0 and kc == 0 and half == 0 and cc == c0) else ())
                grp_drain[grp] = tdrain("td")
            
            for cc in (c0, c1):
                v_op(lambda e, c_=cc, p_=psf[cc]: e.tensor_copy(
                    f_s.ap()[:, c_, :], p_[:]),
                    waits=(("td", sch.now("td")),) if cc == c0 else (), sig="vt")

        p4_vt = sch.now("vt")
        xp_ev = {}
        xpT_ev = {}
        zib_ev = {}
        zvv = {}
        ziT_ev = {}
        for c in range(NCH + 1):
            cm = c - 1
            if c < NCH:
                ffirst = True
                for half in range(2):
                    sl = slice(half * 512, (half + 1) * 512)
                    bw = (("vt", xp_ev[c - 1]),) if c >= 1 else                         (("vt", p4_vt),)
                    mm(psM1[:, sl], ones_row[:], binrow[0:1, sl], True, False,
                       waits=bw if ffirst else ())
                    ffirst = False
                    for kk in range(KH):
                        mm(psM1[:, sl], xnT_s[:, kk, c * 128:(c + 1) * 128],
                           Win_s[:, kk, sl], False, kk == KH - 1)
            if c >= 1:
                zw = (("vt", xpT_ev[cm]),)
                if cm >= 1:
                    zw = zw + (("vt", zib_ev[cm - 1]),)
                mm(psZ[:], ones_row[:], czrow[:], True, False, waits=zw)
                for kk in range(KH):
                    mm(psZ[:], xpT_s[:, kk, cm * 128:(cm + 1) * 128],
                       WS3_s[:, kk, :], False, kk == KH - 1)
            dA = tdrain("td")
            if c < NCH:
                xp_ev[c] = v_op(lambda e: e.tensor_copy(tpsrc[:], psM1[:]),
                                waits=(("td", dA),), sig="vt")
            if c >= 1:
                zib_ev[cm] = v_op(lambda e: e.tensor_copy(zib[:], psZ[:]),
                                  waits=(("td", dA),), sig="vt")
                zvv[cm] = vnop(sig="vt")
            if c < NCH:
                for k in range(KH):
                    E("t", lambda e, kk=k: e.transpose(
                        psT[:, kk * 128:(kk + 1) * 128],
                        tpsrc[:, kk * 128:(kk + 1) * 128], ident_s[:]),
                      waits=(("vt", xp_ev[c]),) if k == 0 else ())
            if c >= 1:
                for j in range(3):
                    E("t", lambda e, jj=j: e.transpose(
                        psZT[0:S, jj * 128:(jj + 1) * 128],
                        zib[:, jj * S:(jj + 1) * S], ident_s[:]),
                      waits=(("vt", zvv[cm]),) if j == 0 else ())
            dB = tdrain("td")
            if c < NCH:
                xpT_ev[c] = v_op(lambda e, cc=c: e.tensor_copy(
                    xpT_s.ap()[:, :, cc * 128:(cc + 1) * 128],
                    psT[:].rearrange("p (k n) -> p k n", k=KH)),
                    waits=(("td", dB),), sig="vt")
            if c >= 1:
                ziT_ev[cm] = v_op(lambda e, cc=cm: e.tensor_copy(
                    ziT3.ap()[:, :, cc * 128:(cc + 1) * 128],
                    psZT[0:S, :].rearrange("p (j n) -> p j n", j=3)),
                    waits=(("td", dB),) if c == NCH else (), sig="vt")

        sgx = s_op(lambda e: e.activation(gates[:], ziT3[:, 1:3, :], AF.Sigmoid),
                   waits=(("vt", sch.now("vt")),), sig="sv")
        v_op(lambda e: e.tensor_tensor(u_s[:], gates[:, 0, :], ziT3[:, 0, :],
                                       ALU.mult), waits=(("sv", sgx),))
        uw = vsettle2("vv")
        if st_i % ST_PER_SEQ == 0:
            v_op(lambda e: e.tensor_tensor_scan(hs_s[:], gates[:, 1, :], u_s[:],
                                                0.0, ALU.mult, ALU.add),
                 waits=(("vv", uw),))
        else:
            v_op(lambda e: e.tensor_tensor_scan(hs_s[:], gates[:, 1, :], u_s[:],
                                                carry[:], ALU.mult, ALU.add),
                 waits=(("vv", uw),))
        uw = vsettle2("vv")
        v_op(lambda e: e.tensor_copy(carry[:], hs_s[:, TS - 1:TS]),
             waits=(("vv", uw),))
        v_op(lambda e: e.tensor_copy(hsb_s[:], hs_s[:]))
        uw = vsettle2("vv")
        E("v", lambda e: e.engine_nop(), waits=(("vv", uw),))
        hv = vnop(sig="vt")

        p6_vs = sch.now("vs")
        p6_sv = sch.now("sv")
        p6_vt = sch.now("vt")
        p6_dn = sch.now("dn")
        m1_free = p6_vs
        yv1 = {}
        y1tv = {}
        av = {}
        xs2d = {}
        oa1 = {}
        dxr_cnt = {}
        th = {}
        for c in range(NCH + 1):
            cm = c - 1
            if c < NCH:
                ffirst = True
                for half in range(2):
                    sl = slice(half * 512, (half + 1) * 512)
                    dwaits = (("vs", m1_free),)
                    if c == 0:
                        dwaits = dwaits + (("vt", hv),)
                    mm(psM1[:, sl], ones_row[:], cprerow[0:1, sl], True, False,
                       waits=dwaits if ffirst else ())
                    ffirst = False
                    for kk in range(KH):
                        mm(psM1[:, sl], xpT_s[:, kk, c * 128:(c + 1) * 128],
                           WD_s[:, kk, sl], False, False)
                    mm(psM1[:, sl], hsb_s[:, c * 128:(c + 1) * 128],
                       C_s[0:S, sl], False, True)
                dA = tdrain("td")
            if c >= 1:
                ffirst = True
                for half in range(2):
                    sl = slice(half * 512, (half + 1) * 512)
                    ewaits = (("vt", y1tv[cm]),)
                    if cm >= 1:
                        ewaits = ewaits + (("vs", av[cm - 1]),)
                    mm(psM2[:, sl], ones_row[:], coutrow[0:1, sl], True, False,
                       waits=ewaits if ffirst else ())
                    ffirst = False
                    for kk in range(KH):
                        mm(psM2[:, sl], y1T_c[:, kk * 128:(kk + 1) * 128],
                           Wout_s[:, kk, sl], False, False)
                    for kk in range(KH):
                        mm(psM2[:, sl], xnT_s[:, kk, cm * 128:(cm + 1) * 128],
                           Win_s[:, kk, sl], False, kk == KH - 1)
                dB = tdrain("td")
                if cm == NCH - 1:
                    last_e_drain = dB
            if c < NCH:
                yv1[c] = v_op(lambda e: e.tensor_copy(y1pre[:], psM1[:]),
                              waits=(("td", dA),), sig="vs")
                m1_free = yv1[c]
                th[c] = s_op(lambda e: e.activation(tpsrc[:], y1pre[:],
                                                    AF.Tanh),
                             waits=(("vs", yv1[c]),), sig="sv")
                if c == 0:
                    g = st_i * NCH
                    dma("y", gsec.ap(), x_h[g * 128:(g + 1) * 128, :], "dxr",
                        waits=(("sv", p6_sv),))
                    dxr_cnt[0] = sch.now("dxr")
                if c == 0:
                    for k in range(KH):
                        E("t", lambda e, kk=k: e.transpose(
                            psT[:, kk * 128:(kk + 1) * 128],
                            tpsrc[:, kk * 128:(kk + 1) * 128], ident_s[:]),
                          waits=(("sv", th[0]), ("vt", p6_vt))
                          if k == 0 else ())
                    dC = tdrain("td")
                    y1tv[0] = v_op(lambda e: e.tensor_copy(y1T_c, psT[:]),
                                   waits=(("td", dC),), sig="vt")
            if c >= 1:
                par = cm % 2
                t1c, t2c, t3c = (tiny1.ap()[:, par:par + 1],
                                 tiny2.ap()[:, par:par + 1],
                                 tiny3.ap()[:, par:par + 1])
                aw = (("td", dB),)
                if cm >= 1:
                    aw = aw + (("sv", xs2d[cm - 1]),)
                av[cm] = v_op(lambda e: e.tensor_copy(q1b[:], psM2[:]),
                              waits=aw, sig="vs")
                v_op(lambda e: e.bn_stats(bn6[:, 0:6], q1b[:, 0:512]))
                v_op(lambda e: e.bn_stats(bn6[:, 6:12], q1b[:, 512:1024]))
                bs = vnop()
                v_op(lambda e: e.bn_aggr(bn2[:], bn6[:]), waits=(("vv", bs),))
                ba = vnop()
                v_op(lambda e, d=t1c: e.tensor_scalar(d, bn2[:, 1:2], LN_EPS,
                                                      None, ALU.add),
                     waits=(("vv", ba),))
                ba = vnop()
                v_op(lambda e, d=t2c, s_=t1c: e.reciprocal(d, s_),
                     waits=(("vv", ba),))
                v_op(lambda e, d=t1c: e.tensor_copy(d, bn2[:, 0:1]))
                vv = vsettle2("vs")
                sq2 = s_op(lambda e, d=t3c, s_=t2c: e.activation(d, s_,
                                                                 AF.Sqrt),
                           waits=(("vs", vv),), sig="sv")
                wv = E("v", lambda e: e.engine_nop(), waits=(("sv", sq2),),
                       sig="vv")
                v_op(lambda e, d=t2c, m=t1c, r=t3c:
                     e.scalar_tensor_tensor(d, m, -1.0, r, ALU.mult, ALU.mult),
                     waits=(("vv", wv),))
                vv = vnop(sig="vs")
                if 1 <= c < NCH:
                    for k in range(KH):
                        E("t", lambda e, kk=k: e.transpose(
                            psT[:, kk * 128:(kk + 1) * 128],
                            tpsrc[:, kk * 128:(kk + 1) * 128], ident_s[:]),
                          waits=(("sv", th[c]), ("vt", y1tv[cm]))
                          if k == 0 else ())
                    dC = tdrain("td")
                    y1tv[c] = v_op(lambda e: e.tensor_copy(y1T_c, psT[:]),
                                   waits=(("td", dC),), sig="vt")
                xs2d[cm] = s_op(lambda e, b=t2c, s_=t3c:
                                e.activation(xhat_b[:], q1b[:], AF.Identity,
                                             bias=b, scale=s_),
                                waits=(("vs", vv),), sig="sv")
                v_op(lambda e: e.tensor_tensor(a2q[:], xhat_b[:], gamma_t[:],
                                               ALU.mult),
                     waits=(("sv", xs2d[cm]),))
                ow = (("dxr", dxr_cnt[cm]),)
                ow = ow + (("dn", store_dn[cm - 1]),) if cm >= 1 else                     ow + (("dn", p6_dn),)
                oa1[cm] = v_op(lambda e, c_=cm: e.tensor_tensor(
                    outc[:], f_s[:, c_, :], gsec[:], ALU.add),
                    waits=ow, sig="vs")
                if cm + 1 < NCH:
                    g2 = st_i * NCH + cm + 1
                    dma("y", gsec.ap(), x_h[g2 * 128:(g2 + 1) * 128, :],
                        "dxr", waits=(("vs", oa1[cm]),))
                    dxr_cnt[cm + 1] = sch.now("dxr")
                ov = v_op(lambda e: e.tensor_tensor(outc[:], outc[:], a2q[:],
                                                    ALU.add), sig="vy")
                g = st_i * NCH + cm
                dma("y", out_h[g * 128:(g + 1) * 128, :], outc.ap(),
                    "dn", waits=(("vy", ov),))
                store_dn[cm] = sch.now("dn")
        else:
            pass
    E("y", lambda e: e.nop(), waits=(("dn", sch.now("dn")),))

    sems = {}
    import contextlib
    with contextlib.ExitStack() as ctx:
        def get_sem(name):
            if name not in sems:
                sems[name] = ctx.enter_context(nc.semaphore(name))
            return sems[name]
        for name in sch.cnt:
            get_sem(name)
        block = ctx.enter_context(nc.Block())
        eng_attr = {"t": "tensor", "v": "vector", "s": "scalar", "y": "sync",
                    "g": "gpsimd"}
        for ename, attr in eng_attr.items():
            ops = sch.q[ename]
            if not ops:
                continue
            def make_section(ops_):
                def section(eng):
                    for fn, waits, sig, inc in ops_:
                        for sname, val in waits:
                            if val > 0:
                                eng.wait_ge(get_sem(sname), val)
                        inst = fn(eng)
                        if sig is not None:
                            assert inst is not None, "signal op returned None"
                            inst.then_inc(get_sem(sig), inc)
                return section
            getattr(block, attr)(make_section(ops))
    return nc


_CACHED = {}


def kernel(**inputs):
    import ml_dtypes
    x = np.ascontiguousarray(inputs["L0_attn_x"], np.float32).reshape(B_FULL * T, H)
    ident = np.eye(128, dtype=ml_dtypes.bfloat16)
    onesr = np.ones((1, 128), dtype=ml_dtypes.bfloat16)
    row = lambda v: np.ascontiguousarray(np.asarray(v, np.float32).reshape(1, -1))
    common = {
        "W_in": np.ascontiguousarray(inputs["L0_attn_W_in"], np.float32),
        "W_B": np.ascontiguousarray(inputs["L0_attn_W_B"], np.float32),
        "W_D": np.ascontiguousarray(inputs["L0_attn_W_D"], np.float32),
        "W_out": np.ascontiguousarray(inputs["L0_attn_W_out"], np.float32),
        "B_mat": np.ascontiguousarray(inputs["L0_attn_B_mat"], np.float32),
        "C_mat": np.ascontiguousarray(inputs["L0_attn_C_mat"], np.float32),
        "W_ig": np.ascontiguousarray(inputs["L0_attn_W_ig"], np.float32),
        "W_fg": np.ascontiguousarray(inputs["L0_attn_W_fg"], np.float32),
        "b_in": row(inputs["L0_attn_b_in"]),
        "b_B": row(inputs["L0_attn_b_B"]),
        "b_D": row(inputs["L0_attn_b_D"]),
        "b_out": row(inputs["L0_attn_b_out"]),
        "b_ig": row(inputs["L0_attn_b_ig"]),
        "b_fg": row(inputs["L0_attn_b_fg"]),
        "gamma": row(inputs["L0_attn_gamma"]),
        "beta": row(inputs["L0_attn_beta"]),
        "w_gate": np.ascontiguousarray(inputs["L0_ffn_w_gate"], np.float32),
        "w_up": np.ascontiguousarray(inputs["L0_ffn_w_up"], np.float32),
        "w_down": np.ascontiguousarray(inputs["L0_ffn_w_down"], np.float32),
        "ident": ident,
        "onesr": onesr,
    }
    if "nc" not in _CACHED:
        _CACHED["nc"] = build_program()
    nc = _CACHED["nc"]
    in_maps = []
    for ci in range(NCORES):
        m = dict(common)
        m["x"] = np.ascontiguousarray(x[ci * R:(ci + 1) * R])
        in_maps.append(m)
    res = run_bass_kernel_spmd(nc, in_maps, core_ids=list(range(NCORES)))
    out = np.concatenate([r["out"] for r in res.results], axis=0)
    return out.reshape(B_FULL, T, H)


if __name__ == "__main__":
    pass


# version 13
# speedup vs baseline: 1.4529x; 1.0288x over previous
import numpy as np

import concourse.bass as bass
import concourse.mybir as mybir
from concourse.bass_utils import run_bass_kernel_spmd

F32 = mybir.dt.float32
BF16 = mybir.dt.bfloat16
AF = mybir.ActivationFunctionType
ALU = mybir.AluOpType
AX = mybir.AxisListType

NCORES = 8
H = 1024
S = 16
II = 2816
B_FULL, T = 32, 2048
R = (B_FULL // NCORES) * T
CH = 128
TS = 512
NCH = TS // CH
import os as _os
NST = int(_os.environ.get("K_NST", R // TS))
ST_PER_SEQ = T // TS
KH = H // 128
KI = II // 128
SECS = [512, 512, 512, 512, 512, 256]
LN_EPS = 1e-5


class Sched:
    ENGS = ("t", "v", "s", "y", "g")

    def __init__(self):
        self.q = {e: [] for e in self.ENGS}
        self.cnt = {}

    def emit(self, eng, fn, waits=(), sig=None, inc=1):
        self.q[eng].append((fn, tuple(waits), sig, inc))
        if sig is not None:
            self.cnt[sig] = self.cnt.get(sig, 0) + inc
        return self.cnt.get(sig, 0) if sig else 0

    def now(self, sem):
        return self.cnt.get(sem, 0)


def build_program():
    nc = bass.Bass("TRN2", target_bir_lowering=False,
                   detect_race_conditions=False)

    dT = lambda n, sh, dt=F32, kind="ExternalInput": nc.dram_tensor(n, sh, dt, kind=kind).ap()
    x_h = dT("x", [R, H])
    W_in_h = dT("W_in", [H, H])
    W_B_h = dT("W_B", [H, H])
    W_D_h = dT("W_D", [H, H])
    W_out_h = dT("W_out", [H, H])
    B_mat_h = dT("B_mat", [H, S])
    C_mat_h = dT("C_mat", [S, H])
    W_ig_h = dT("W_ig", [H, S])
    W_fg_h = dT("W_fg", [H, S])
    b_in_h = dT("b_in", [1, H])
    b_B_h = dT("b_B", [1, H])
    b_D_h = dT("b_D", [1, H])
    b_out_h = dT("b_out", [1, H])
    b_ig_h = dT("b_ig", [1, S])
    b_fg_h = dT("b_fg", [1, S])
    gamma_h = dT("gamma", [1, H])
    beta_h = dT("beta", [1, H])
    w_gate_h = dT("w_gate", [II, H])
    w_up_h = dT("w_up", [II, H])
    w_down_h = dT("w_down", [H, II])
    ident_h = dT("ident", [128, 128], BF16)
    onesr_h = dT("onesr", [1, 128], BF16)
    out_h = dT("out", [R, H], F32, kind="ExternalOutput")

    gateT_h = nc.dram_tensor("gateT_q", [KH, 128, II], BF16, kind="Internal").ap()
    upT_h = nc.dram_tensor("upT_q", [KH, 128, II], BF16, kind="Internal").ap()
    downT_h = nc.dram_tensor("downT_q", [KI, 128, H], BF16, kind="Internal").ap()

    sb = lambda n, sh, dt: nc.alloc_sbuf_tensor(n, sh, dt)
    Win_s = sb("Win_s", [128, KH, H], BF16)
    WD_s = sb("WD_s", [128, KH, H], BF16)
    Wout_s = sb("Wout_s", [128, KH, H], BF16)
    WS3_s = sb("WS3_s", [128, KH, 3 * S], BF16)
    Bm_s = sb("Bm_s", [128, KH, S], BF16)
    C_s = sb("C_s", [S, H], BF16)
    gamma_t = sb("gamma_t", [128, H], BF16)
    ident_s = sb("ident_s", [128, 128], BF16)
    ones_row = sb("ones_row", [1, 128], BF16)
    binrow = sb("binrow", [1, H], BF16)
    betarow = sb("betarow", [1, H], BF16)
    czrow = sb("czrow", [1, 3 * S], BF16)
    cprerow = sb("cprerow", [1, H], BF16)
    coutrow = sb("coutrow", [1, H], BF16)
    bincol = sb("bincol", [128, KH], BF16)
    bBcol = sb("bBcol", [128, KH], BF16)
    biasrow48 = sb("biasrow48", [1, 3 * S], BF16)
    bias_p1 = sb("bias_p1", [128, 1], F32)
    bias_n1 = sb("bias_n1", [128, 1], F32)
    xnT_s = sb("xnT_s", [128, KH, TS], BF16)
    xqT_s = sb("xqT_s", [128, KH, TS], BF16)
    xpT_s = sb("xpT_s", [128, KH, TS], BF16)
    hqT_s = sb("hqT_s", [128, KI, TS], BF16)
    f_s = sb("f_s", [128, NCH, H], BF16)
    ziT3 = sb("ziT3", [S, 3, TS], BF16)
    gates = sb("gates", [S, 2, TS], F32)
    u_s = sb("u_s", [S, TS], F32)
    hs_s = sb("hs_s", [S, TS], F32)
    hsb_s = sb("hsb_s", [S, TS], BF16)
    carry = sb("carry", [S, 1], F32)
    gup_s = sb("gup_s", [128, 2, 2, KH, 512], BF16)
    dwn_s = sb("dwn_s", [128, 2, 2, H], BF16)
    xin_s = sb("xin_s", [128, 2, H], F32)
    xnb_s = sb("xnb_s", [128, H], BF16)
    tpsrc = sb("tpsrc", [128, H], BF16)
    qin = sb("qin", [128, H], F32)
    sm2 = sb("sm2", [128, H], F32)
    a2q = sb("a2q", [128, H], F32)
    q1b = sb("q1b", [128, H], F32)
    q2b = sb("q2b", [128, H], F32)
    bmq = sb("bmq", [128, H // 16], F32)
    scq = sb("scq", [128, H // 16], F32)
    rcq = sb("rcq", [128, H // 16], F32)
    gsec = sb("gsec", [128, H], F32)
    bn6 = sb("bn6", [128, 12], F32)
    bn2 = sb("bn2", [128, 2], F32)
    tiny1 = sb("tiny1", [128, 2], F32)
    tiny2 = sb("tiny2", [128, 2], F32)
    tiny3 = sb("tiny3", [128, 2], F32)
    zib = sb("zib", [128, 3 * S], BF16)
    sgp = gsec
    y1pre = sm2
    y1T_t = sb("y1T_t", [128, KH * 128], BF16)
    y1T_c = y1T_t.ap()[:, 0:KH * 128]
    rowA = gsec.ap()[0:1, :]
    rowB = sm2.ap()[0:1, :]
    rowbf = tpsrc.ap()[0:1, :]
    bdrow = xnb_s.ap()[0:1, :]
    xhat_b = xnb_s
    outc = qin

    psM1 = nc.alloc_psum_tensor("psM1", [128, H], F32)
    psM2 = nc.alloc_psum_tensor("psM2", [128, H], F32)
    psT = nc.alloc_psum_tensor("psT", [128, 1024], BF16)
    psZ = nc.alloc_psum_tensor("psZ", [128, 3 * S], F32)
    psZT = nc.alloc_psum_tensor("psZT", [S, 3 * 128], BF16)

    sch = Sched()
    E = sch.emit

    def dma(q_, dst, src, sig, waits=()):
        return E(q_, lambda e, d=dst, s=src: e.dma_start(out=d, in_=s),
                 waits=waits, sig=sig, inc=16)

    def tdrain(sig, waits=()):
        def fn(e):
            e.drain()
            return e.nop()
        return E("t", fn, waits=waits, sig=sig)

    def mm(out_ap, lhsT, rhs, start, stop, waits=()):
        return E("t", lambda e, o=out_ap, l=lhsT, r=rhs, a=start, b=stop:
                 e.matmul(o, l, r, start=a, stop=b), waits=waits)

    def v_op(fn, waits=(), sig=None):
        return E("v", fn, waits=waits, sig=sig)

    def vnop(sig="vv"):
        return E("v", lambda e: e.engine_nop(), sig=sig)

    def vsettle2(sig):
        t = E("v", lambda e: e.engine_nop(), sig="vv")
        return E("v", lambda e: e.engine_nop(), waits=(("vv", t),), sig=sig)

    def snop(sig="sv"):
        return E("s", lambda e: e.nop(cycle_cnt=200), sig=sig)

    def s_op(fn, waits=(), sig=None):
        return E("s", fn, waits=waits, sig=sig)

    def transposes(src_ap, n, waits=()):
        for k in range(n):
            E("t", lambda e, kk=k, sa=src_ap: e.transpose(
                psT[:, kk * 128:(kk + 1) * 128], sa[:, kk * 128:(kk + 1) * 128],
                ident_s[:]), waits=waits if k == 0 else ())
        return tdrain("td")

    eq_mv = [0]
    def emit_quant(w, qout_ap, waits_v=(), waits_out=(), src=None, c20=None):
        nb = w // 16
        if src is None:
            src = qin.ap()[:, 0:w]
        if c20 is None:
            c20 = qin.ap()[:, 0:w]
        qin3 = src.rearrange("p (b c) -> p b c", c=16)
        sm23 = sm2.ap()[:, 0:w].rearrange("p (b c) -> p b c", c=16)
        rcb = rcq.ap()[:, 0:nb].unsqueeze(2).broadcast_to([128, nb, 16])
        scb = scq.ap()[:, 0:nb].unsqueeze(2).broadcast_to([128, nb, 16])
        v_op(lambda e: e.tensor_reduce(bmq[:, 0:nb].unsqueeze(2), qin3,
                                       AX.X, ALU.max,
                                       apply_absolute_value=True),
             waits=waits_v)
        rv = vnop()
        v_op(lambda e: e.tensor_scalar(scq[:, 0:nb], bmq[:, 0:nb], 1.0 / 24.0,
                                       2.5e-13, ALU.mult, ALU.max),
             waits=(("vv", rv),))
        rv = vnop()
        v_op(lambda e: e.reciprocal(rcq[:, 0:nb], scq[:, 0:nb]),
             waits=(("vv", rv),))
        rv = vsettle2("vv")
        mv = v_op(lambda e: e.scalar_tensor_tensor(sm23, qin3, 1.0, rcb,
                                                   ALU.mult, ALU.mult),
                  waits=(("vv", rv),), sig="vs")
        eq_mv[0] = mv
        s_op(lambda e: e.activation(sgp[:, 0:w], sm2[:, 0:w], AF.Sign,
                                    bias=bias_n1[:], scale=1.0),
             waits=(("vs", mv),))
        ab = s_op(lambda e: e.activation(a2q[:, 0:w], sm2[:, 0:w], AF.Abs),
                  waits=(("td", sch.now("td")), ("dn", sch.now("dn"))), sig="sv")
        v_op(lambda e: e.tensor_scalar(q1b[:, 0:w], a2q[:, 0:w], 3.0, None,
                                       ALU.is_gt), waits=(("sv", ab),))
        v_op(lambda e: e.scalar_tensor_tensor(q1b[:, 0:w], a2q[:, 0:w], 5.0,
                                              q1b[:, 0:w], ALU.is_gt, ALU.add))
        v_op(lambda e: e.scalar_tensor_tensor(q1b[:, 0:w], a2q[:, 0:w], 7.0,
                                              q1b[:, 0:w], ALU.is_gt, ALU.add))
        v_op(lambda e: e.tensor_scalar(q2b[:, 0:w], a2q[:, 0:w], 10.0, None,
                                       ALU.is_gt))
        v_op(lambda e: e.scalar_tensor_tensor(q2b[:, 0:w], a2q[:, 0:w], 14.0,
                                              q2b[:, 0:w], ALU.is_gt, ALU.add))
        v_op(lambda e: e.tensor_scalar(c20, a2q[:, 0:w], 20.0, None,
                                       ALU.is_gt))
        v_op(lambda e: e.scalar_tensor_tensor(q2b[:, 0:w], c20, 2.0,
                                              q2b[:, 0:w], ALU.mult, ALU.add))
        a2f = v_op(lambda e: e.scalar_tensor_tensor(q1b[:, 0:w], q2b[:, 0:w], 2.0,
                                                    q1b[:, 0:w], ALU.mult, ALU.add),
                   sig="vs")
        ssig = s_op(lambda e: e.activation(a2q[:, 0:w], sm2[:, 0:w], AF.Sign,
                                           bias=bias_p1[:], scale=1.0),
                    waits=(("vs", a2f),), sig="sv")
        v_op(lambda e: e.tensor_tensor(sgp[:, 0:w], sgp[:, 0:w], a2q[:, 0:w],
                                       ALU.add), waits=(("sv", ssig),))
        v_op(lambda e: e.scalar_tensor_tensor(q2b[:, 0:w], q1b[:, 0:w], 1.0,
                                              sgp[:, 0:w], ALU.add, ALU.mult))
        q23 = q2b.ap()[:, 0:w].rearrange("p (b c) -> p b c", c=16)
        qo3 = qout_ap.rearrange("p (b c) -> p b c", c=16)
        return v_op(lambda e: e.scalar_tensor_tensor(qo3, q23, 1.0, scb,
                                                     ALU.mult, ALU.mult),
                    waits=waits_out, sig="vq")

    v_op(lambda e: e.memset(bias_p1[:], 1.0))
    v_op(lambda e: e.memset(bias_n1[:], -1.0))
    v_op(lambda e: e.memset(biasrow48[:], 0.0))

    dma("y", ident_s.ap(), ident_h, "dc")
    dma("y", ones_row.ap(), onesr_h, "dc")
    for dst, src in ((Win_s, W_in_h), (WD_s, W_D_h), (Wout_s, W_out_h)):
        dma("g", dst.ap(), src.rearrange("(k p) n -> p k n", p=128), "dwc")
    dma("g", WS3_s.ap()[:, :, S:2 * S], W_ig_h.rearrange("(k p) n -> p k n", p=128), "dwc")
    dma("g", WS3_s.ap()[:, :, 2 * S:3 * S], W_fg_h.rearrange("(k p) n -> p k n", p=128), "dwc")
    dma("g", Bm_s.ap(), B_mat_h.rearrange("(k p) n -> p k n", p=128), "dwc")
    dma("g", C_s.ap(), C_mat_h, "dwc")
    WB_v = gup_s.ap()[:, 0].rearrange("p m k n -> p (m k n)").rearrange(
        "p (a b) -> p a b", b=H)
    WBT_v = gup_s.ap()[:, 1].rearrange("p m k n -> p (m k n)").rearrange(
        "p (a b) -> p a b", b=H)
    dma("g", WB_v, W_B_h.rearrange("(k p) n -> p k n", p=128), "dwb")

    dma("y", rowA, b_in_h, "drA")
    cv = v_op(lambda e: e.tensor_copy(binrow[:], rowA),
              waits=(("drA", 16), ("dc", 32)), sig="vt")
    for k in range(KH):
        E("t", lambda e, kk=k: e.transpose(psT[:, 2 * kk:2 * kk + 1],
                                           binrow[0:1, kk * 128:(kk + 1) * 128],
                                           ident_s[0:1, 0:1]),
          waits=(("vt", cv),) if k == 0 else ())
    tdrain("td")
    v_op(lambda e: e.tensor_copy(bincol[:],
                                 psT.ap()[:, 0:2 * KH].rearrange(
                                     "p (k two) -> p k two", two=2)[:, :, 0]),
         waits=(("td", sch.now("td")),))
    vnop(sig="vt")
    dma("y", rowB, b_out_h, "drB")
    v_op(lambda e: e.tensor_tensor(coutrow[:], rowA, rowB, ALU.add),
         waits=(("drB", 16),), sig="vt")
    dma("y", rowA, b_B_h, "drA", waits=(("vt", sch.now("vt")),))
    cv = v_op(lambda e: e.tensor_copy(rowbf, rowA),
              waits=(("drA", 32),), sig="vt")
    for k in range(KH):
        E("t", lambda e, kk=k: e.transpose(psT[:, 2 * kk:2 * kk + 1],
                                           rowbf[0:1, kk * 128:(kk + 1) * 128],
                                           ident_s[0:1, 0:1]),
          waits=(("vt", cv),) if k == 0 else ())
    tdrain("td")
    v_op(lambda e: e.tensor_copy(bBcol[:],
                                 psT.ap()[:, 0:2 * KH].rearrange(
                                     "p (k two) -> p k two", two=2)[:, :, 0]),
         waits=(("td", sch.now("td")),))
    vnop(sig="vt")
    dma("y", rowA, b_D_h, "drA", waits=(("vt", sch.now("vt")),))
    cv = v_op(lambda e: e.tensor_copy(bdrow, rowA), waits=(("drA", 48),),
              sig="vt")
    dma("y", rowA, gamma_h, "drA", waits=(("vt", cv),))
    cv = v_op(lambda e: e.tensor_copy(rowbf, rowA), waits=(("drA", 64),),
              sig="vt")
    for half in range(2):
        mm(psM1[:, half * 512:(half + 1) * 512], ones_row[:],
           rowbf[0:1, half * 512:(half + 1) * 512], True, True,
           waits=(("vt", cv),) if half == 0 else ())
    tdrain("td")
    v_op(lambda e: e.tensor_copy(gamma_t[:], psM1[:]),
         waits=(("td", sch.now("td")),), sig="vt")
    dma("y", rowA, beta_h, "drA", waits=(("vt", sch.now("vt")),))
    cv = v_op(lambda e: e.tensor_copy(betarow[:], rowA), waits=(("drA", 80),),
              sig="vt")
    dma("y", rowA[0:1, 0:S], b_ig_h, "drA", waits=(("vt", cv),))
    dma("y", rowA[0:1, S:2 * S], b_fg_h, "drA")
    cv = v_op(lambda e: e.tensor_copy(biasrow48[0:1, S:2 * S], rowA[0:1, 0:S]),
              waits=(("drA", 112),), sig="vt")
    v_op(lambda e: e.tensor_copy(biasrow48[0:1, 2 * S:3 * S],
                                 rowA[0:1, S:2 * S]))
    cv = vnop(sig="vt")

    for kk in range(KH):
        for jj in range(KH):
            E("t", lambda e, a=kk, b=jj: e.transpose(
                psT[:, b * 128:(b + 1) * 128],
                WB_v[:, b, a * 128:(a + 1) * 128], ident_s[:]),
              waits=((("dwb", 16),) if kk == 0 else ()) +
                    (("vt", sch.now("vt")),) if jj == 0 else ())
        tdrain("td")
        v_op(lambda e, a=kk: e.tensor_copy(WBT_v[:, a, :], psT[:]),
             waits=(("td", sch.now("td")),), sig="vt")
    for r in range(KH):
        for kk in range(KH):
            mm(psM2[:, 0:S], WBT_v[:, kk, r * 128:(r + 1) * 128], Bm_s[:, kk, :],
               kk == 0, kk == KH - 1,
               waits=((("dwc", sch.now("dwc")),) if r == 0 else ()) +
                     (("vt", sch.now("vt")),) if kk == 0 else ())
        tdrain("td")
        v_op(lambda e, rr=r: e.tensor_copy(WS3_s[:, rr, 0:S], psM2[:, 0:S]),
             waits=(("td", sch.now("td")),))
        vnop(sig="vt")

    for kk in range(KH):
        mm(psZ[0:S, 0:1], Bm_s[:, kk, :], bBcol[:, kk:kk + 1], kk == 0,
           kk == KH - 1, waits=(("vt", sch.now("vt")),) if kk == 0 else ())
    tdrain("td")
    v_op(lambda e: e.tensor_copy(zib[0:S, 0:1], psZ[0:S, 0:1]),
         waits=(("td", sch.now("td")),))
    vnop(sig="vt")
    E("t", lambda e: e.transpose(psZT[0:1, 0:S], zib[0:S, 0:1],
                                 ident_s[0:S, 0:S]),
      waits=(("vt", sch.now("vt")),))
    tdrain("td")
    cw = vnop()
    v_op(lambda e: e.tensor_copy(czrow[:], biasrow48[:]),
         waits=(("td", sch.now("td")), ("vv", cw)))
    cw = vnop()
    v_op(lambda e: e.tensor_tensor(czrow[0:1, 0:S], czrow[0:1, 0:S],
                                   psZT[0:1, 0:S], ALU.add),
         waits=(("vv", cw),))
    vnop(sig="vt")

    v_op(lambda e: e.tensor_copy(cprerow[:], bdrow))
    vnop(sig="vt")

    vnop()
    wq_bar = vnop()
    jobs = []
    for src_h, dstT_h, nchunk, width in (
            (w_gate_h, gateT_h, KI, H), (w_up_h, upT_h, KI, H),
            (w_down_h, downT_h, KH, II)):
        for icx in range(nchunk):
            subs = ([(0, 1024), (1024, 2048), (2048, 2816)]
                    if width == II else [(0, 1024)])
            for s0_, s1_ in subs:
                jobs.append((src_h, dstT_h, icx, s0_, s1_))
    nj = len(jobs)

    def wq_src(i):
        return qin.ap() if i % 2 == 0 else xin_s.ap()[:, 0, :]

    sh0, _, ic0, a0, a1 = jobs[0]
    dma("y", wq_src(0)[:, 0:a1 - a0], sh0[ic0 * 128:(ic0 + 1) * 128, a0:a1],
        "dfw", waits=(("vv", wq_bar),))
    wq_ev = {}
    wq_dn = {}
    wq_td = {}
    wq_mv = {}
    for i, (src_h, dstT_h, icx, s0_, s1_) in enumerate(jobs):
        wsub = s1_ - s0_
        nk = wsub // 128
        if i + 1 < nj:
            sh2, _, ic2, t0_, t1_ = jobs[i + 1]
            pw = (("vs", wq_mv[i - 1]),) if i >= 1 else (("vv", wq_bar),)
            dma("y", wq_src(i + 1)[:, 0:t1_ - t0_],
                sh2[ic2 * 128:(ic2 + 1) * 128, t0_:t1_], "dfw", waits=pw)
        vqw = emit_quant(wsub, tpsrc.ap()[:, 0:wsub],
                         waits_v=(("dfw", 16 * (i + 1)),),
                         waits_out=(("td", wq_td[i - 1]),) if i >= 1 else (),
                         src=wq_src(i)[:, 0:wsub],
                         c20=xin_s.ap()[:, 1, 0:wsub])
        wq_mv[i] = eq_mv[0]
        for k in range(nk):
            E("t", lambda e, kk=k: e.transpose(
                psT[:, kk * 128:(kk + 1) * 128],
                tpsrc[:, kk * 128:(kk + 1) * 128], ident_s[:]),
              waits=(("vq", vqw), ("vt", wq_ev[i - 1])) if (k == 0 and i >= 1)
              else ((("vq", vqw), ("vt", sch.now("vt"))) if k == 0 else ()))
        wq_td[i] = tdrain("td")
        if i % 2 == 0:
            dst3 = y1T_c[:, 0:nk * 128].rearrange("p (k n) -> p k n", k=nk)
        else:
            dst3 = xnT_s.ap()[:, 0:nk, 0:128]
        evw = (("td", wq_td[i]),)
        if i >= 2:
            evw = evw + (("dn", wq_dn[i - 2]),)
        wq_ev[i] = v_op(lambda e, d=dst3, n_=nk: e.tensor_copy(
            d, psT[:, 0:n_ * 128].rearrange("p (k n) -> p k n", k=n_)),
            waits=evw, sig="vt")
        dst = dstT_h[s0_ // 128:(s0_ // 128) + nk, :,
                     icx * 128:(icx + 1) * 128].rearrange("k p n -> p k n")
        dma("y", dst, dst3, "dn", waits=(("vt", wq_ev[i]),))
        wq_dn[i] = sch.now("dn")

    dma("y", xin_s.ap()[:, 0, :], x_h[0:128, :], "dx",
        waits=(("dn", sch.now("dn")),))
    dma("y", xin_s.ap()[:, 1, :], x_h[128:256, :], "dx")
    n_dx = 2

    last_e_drain = sch.now("td")
    store_dn = {}
    last_fdown_read = sch.now("td")
    for st_i in range(NST):
        p1_ss = {}
        p1_tdXQ = None
        p1_xqc = None
        sv_base = sch.now("sv")
        for c in range(NCH):
            g = st_i * NCH + c
            par = c % 2
            t1c, t2c, t3c = (tiny1.ap()[:, par:par + 1],
                             tiny2.ap()[:, par:par + 1],
                             tiny3.ap()[:, par:par + 1])
            slot = g % 2
            dxw = 16 * (g + 1)
            xa = xin_s.ap()[:, slot, :]
            wA = (("dx", dxw),
                  ("sv", p1_ss[g - 2]) if g - 2 in p1_ss else ("sv", sv_base))
            v_op(lambda e, a=xa: e.bn_stats(bn6[:, 0:6], a[:, 0:512]), waits=wA)
            v_op(lambda e, a=xa: e.bn_stats(bn6[:, 6:12], a[:, 512:1024]))
            bs = vnop()
            v_op(lambda e: e.bn_aggr(bn2[:], bn6[:]), waits=(("vv", bs),))
            ba = vnop()
            v_op(lambda e, d=t1c: e.tensor_scalar(d, bn2[:, 1:2], LN_EPS, None,
                                                  ALU.add), waits=(("vv", ba),))
            ba = vnop()
            v_op(lambda e, d=t2c, s_=t1c: e.reciprocal(d, s_),
                 waits=(("vv", ba),))
            v_op(lambda e, d=t1c: e.tensor_copy(d, bn2[:, 0:1]))
            vv = vsettle2("vs")
            sq = s_op(lambda e, d=t3c, s_=t2c: e.activation(d, s_, AF.Sqrt),
                      waits=(("vs", vv),), sig="sv")
            wv = E("v", lambda e: e.engine_nop(), waits=(("sv", sq),), sig="vv")
            v_op(lambda e, d=t2c, m=t1c, r=t3c:
                 e.scalar_tensor_tensor(d, m, -1.0, r, ALU.mult, ALU.mult),
                 waits=(("vv", wv),))
            vv = vnop(sig="vs")
            ss = s_op(lambda e, a=xa, b=t2c, s_=t3c:
                      e.activation(qin[:], a, AF.Identity, bias=b, scale=s_),
                      waits=(("vs", vv), ("vq", sch.now("vq")),
                             ("dn", sch.now("dn"))), sig="sv")
            wxn = (("td", p1_tdXN),) if c > 0 else ()
            ss = s_op(lambda e, a=xa, b=t2c, s_=t3c:
                      e.activation(xnb_s[:], a, AF.Identity, bias=b, scale=s_),
                      waits=wxn, sig="sv")
            p1_ss[g] = ss
            if g + 2 < NST * NCH:
                gg = g + 2
                dma("y", xin_s.ap()[:, gg % 2, :],
                    x_h[gg * 128:(gg + 1) * 128, :], "dx",
                    waits=(("sv", ss),))
                n_dx += 1
            if p1_tdXQ is not None:
                ew = (("td", p1_tdXQ),)
                if p1_xqc == 0:
                    ew = ew + (("td", last_e_drain),)
                xq_ev = v_op(lambda e, cc=p1_xqc: e.tensor_copy(
                    xqT_s.ap()[:, :, cc * 128:(cc + 1) * 128],
                    psT[:].rearrange("p (k n) -> p k n", k=KH)),
                    waits=ew, sig="vt")
                psT_free = xq_ev
            else:
                psT_free = sch.now("vt")
            for k in range(KH):
                E("t", lambda e, kk=k: e.transpose(
                    psT[:, kk * 128:(kk + 1) * 128],
                    xnb_s[:, kk * 128:(kk + 1) * 128], ident_s[:]),
                  waits=(("sv", ss), ("vt", psT_free)) if k == 0 else ())
            p1_tdXN = tdrain("td")
            emit_quant(H, tpsrc.ap()[:, 0:H], waits_v=(("sv", ss),))
            vq_c = sch.now("vq")
            dw = (("td", p1_tdXN),)
            if c == 0:
                dw = dw + (("td", last_e_drain),)
            xn_ev = v_op(lambda e, cc=c: e.tensor_copy(
                xnT_s.ap()[:, :, cc * 128:(cc + 1) * 128],
                psT[:].rearrange("p (k n) -> p k n", k=KH)),
                waits=dw, sig="vt")
            for k in range(KH):
                E("t", lambda e, kk=k: e.transpose(
                    psT[:, kk * 128:(kk + 1) * 128],
                    tpsrc[:, kk * 128:(kk + 1) * 128], ident_s[:]),
                  waits=(("vq", vq_c), ("vt", xn_ev)) if k == 0 else ())
            p1_tdXQ = tdrain("td")
            p1_xqc = c
        v_op(lambda e, cc=p1_xqc: e.tensor_copy(
            xqT_s.ap()[:, :, cc * 128:(cc + 1) * 128],
            psT[:].rearrange("p (k n) -> p k n", k=KH)),
            waits=(("td", p1_tdXQ),), sig="vt")

        ksec0 = 0
        sec_drain = {}
        m1_free = sch.now("vs")
        m2_free = sch.now("vs")
        xqT_ready = sch.now("vt")
        pend = None
        tp_drain_prev = sch.now("td")
        for si, SEC in enumerate(SECS):
            buf = si % 2
            nk = SEC // 128
            if si >= 2:
                wbuf = (("td", sec_drain[si - 2]),)
            else:
                wbuf = (("td", sch.now("td")),)
            dma("g", gup_s.ap()[:, buf, 0, :, 0:SEC],
                gateT_h[:, :, ksec0:ksec0 + SEC].rearrange("k p n -> p k n"),
                "dgu", waits=wbuf)
            dma("g", gup_s.ap()[:, buf, 1, :, 0:SEC],
                upT_h[:, :, ksec0:ksec0 + SEC].rearrange("k p n -> p k n"),
                "dgu")
            wg = sch.now("dgu")
            for pr in range(2):
                c0, c1 = 2 * pr, 2 * pr + 1
                ffirst = True
                for ci, cc in enumerate((c0, c1)):
                    for kk in range(KH):
                        mm(psM1[:, ci * 512:ci * 512 + SEC],
                           xqT_s[:, kk, cc * 128:(cc + 1) * 128],
                           gup_s[:, buf, 0, kk, 0:SEC], kk == 0, kk == KH - 1,
                           waits=(("dgu", wg), ("vs", m1_free),
                                  ("vt", xqT_ready)) if ffirst else ())
                        ffirst = False
                ffirst = True
                for ci, cc in enumerate((c0, c1)):
                    for kk in range(KH):
                        mm(psM2[:, ci * 512:ci * 512 + SEC],
                           xqT_s[:, kk, cc * 128:(cc + 1) * 128],
                           gup_s[:, buf, 1, kk, 0:SEC], kk == 0, kk == KH - 1,
                           waits=(("vs", m2_free),) if ffirst else ())
                        ffirst = False
                md = tdrain("td")
                if pr == 1:
                    sec_drain[si] = md
                if pend is not None:
                    tp_drain_prev = pend()
                    pend = None
                w2 = 2 * SEC
                pM1v = psM1.ap().rearrange("p (c n) -> p c n", c=2)[:, :, 0:SEC]
                pM2v = psM2.ap().rearrange("p (c n) -> p c n", c=2)[:, :, 0:SEC]
                gsv = gsec.ap()[:, 0:w2].rearrange("p (c n) -> p c n", c=2)
                q1v = q1b.ap()[:, 0:w2].rearrange("p (c n) -> p c n", c=2)
                qiv = qin.ap()[:, 0:w2].rearrange("p (c n) -> p c n", c=2)
                gv = v_op(lambda e, d=gsv, s_=pM1v: e.tensor_copy(d, s_),
                          waits=(("td", md),), sig="vs")
                m1_free = gv
                sl = s_op(lambda e, w=w2: e.activation(q1b[:, 0:w], gsec[:, 0:w],
                                                       AF.Silu),
                          waits=(("vs", gv),), sig="sv")
                m2_free = v_op(lambda e, d=qiv, s_=pM2v, q_=q1v:
                               e.scalar_tensor_tensor(d, s_, 1.0, q_,
                                                      ALU.mult, ALU.mult),
                               waits=(("sv", sl), ("vq", sch.now("vq"))),
                               sig="vs")
                vq_p = emit_quant(w2, tpsrc.ap()[:, 0:w2],
                                  waits_out=(("td", tp_drain_prev),))

                def make_pend(nk_=nk, k0=ksec0 // 128, cc0=c0, cc1=c1, vq=vq_p):
                    def emit_tp():
                        for ci in range(2):
                            for k in range(nk_):
                                E("t", lambda e, kk=k, cci=ci: e.transpose(
                                    psT[:, (cci * nk_ + kk) * 128:
                                        (cci * nk_ + kk + 1) * 128],
                                    tpsrc[:, (cci * nk_ + kk) * 128:
                                          (cci * nk_ + kk + 1) * 128],
                                    ident_s[:]),
                                  waits=(("vq", vq), ("vt", sch.now("vt")))
                                  if (ci == 0 and k == 0) else ())
                        td_ = tdrain("td")
                        for ci, cc in enumerate((cc0, cc1)):
                            v_op(lambda e, cci=ci, c_=cc:
                                 e.tensor_copy(
                                     hqT_s.ap()[:, k0:k0 + nk_,
                                                c_ * 128:(c_ + 1) * 128],
                                     psT[:, cci * nk_ * 128:(cci + 1) * nk_ * 128]
                                     .rearrange("p (k n) -> p k n", k=nk_)),
                                 waits=(("td", td_),) if ci == 0 else (),
                                 sig="vt")
                        return td_
                    return emit_tp
                pend = make_pend()
            ksec0 += SEC
        if pend is not None:
            tp_drain_prev = pend()
            pend = None

        for pr in range(2):
            c0, c1 = 2 * pr, 2 * pr + 1
            psf = {c0: psM1, c1: psM2}
            grp_drain = {}
            for grp in range(11):
                bufd = grp % 2
                if grp >= 2:
                    wbuf = (("td", grp_drain[grp - 2]),)
                else:
                    wbuf = (("td", sch.now("td")),)
                dma("g", dwn_s.ap()[:, bufd],
                    downT_h[2 * grp:2 * grp + 2].rearrange("k p n -> p k n"),
                    "ddn", waits=wbuf)
                wd = sch.now("ddn")
                for cc in (c0, c1):
                    if grp == 0:
                        for half in range(2):
                            mm(psf[cc][0:128, half * 512:(half + 1) * 512],
                               ones_row[:], betarow[0:1, half * 512:(half + 1) * 512],
                               True, False,
                               waits=(("ddn", wd), ("vq", sch.now("vq")),
                                      ("vt", sch.now("vt"))) if (half == 0 and cc == c0) else ())
                    for kc in range(2):
                        ic = 2 * grp + kc
                        for half in range(2):
                            mm(psf[cc][:, half * 512:(half + 1) * 512],
                               hqT_s[:, ic, cc * 128:(cc + 1) * 128],
                               dwn_s[:, bufd, kc, half * 512:(half + 1) * 512],
                               False, grp == 10 and kc == 1,
                               waits=(("ddn", wd),) if (grp > 0 and kc == 0 and half == 0 and cc == c0) else ())
                grp_drain[grp] = tdrain("td")
            
            for cc in (c0, c1):
                v_op(lambda e, c_=cc, p_=psf[cc]: e.tensor_copy(
                    f_s.ap()[:, c_, :], p_[:]),
                    waits=(("td", sch.now("td")),) if cc == c0 else (), sig="vt")

        p4_vt = sch.now("vt")
        xp_ev = {}
        xpT_ev = {}
        zib_ev = {}
        zvv = {}
        ziT_ev = {}
        for c in range(NCH + 1):
            cm = c - 1
            if c < NCH:
                ffirst = True
                for half in range(2):
                    sl = slice(half * 512, (half + 1) * 512)
                    bw = (("vt", xp_ev[c - 1]),) if c >= 1 else                         (("vt", p4_vt),)
                    mm(psM1[:, sl], ones_row[:], binrow[0:1, sl], True, False,
                       waits=bw if ffirst else ())
                    ffirst = False
                    for kk in range(KH):
                        mm(psM1[:, sl], xnT_s[:, kk, c * 128:(c + 1) * 128],
                           Win_s[:, kk, sl], False, kk == KH - 1)
            if c >= 1:
                zw = (("vt", xpT_ev[cm]),)
                if cm >= 1:
                    zw = zw + (("vt", zib_ev[cm - 1]),)
                mm(psZ[:], ones_row[:], czrow[:], True, False, waits=zw)
                for kk in range(KH):
                    mm(psZ[:], xpT_s[:, kk, cm * 128:(cm + 1) * 128],
                       WS3_s[:, kk, :], False, kk == KH - 1)
            dA = tdrain("td")
            if c < NCH:
                xp_ev[c] = v_op(lambda e: e.tensor_copy(tpsrc[:], psM1[:]),
                                waits=(("td", dA),), sig="vt")
            if c >= 1:
                zib_ev[cm] = v_op(lambda e: e.tensor_copy(zib[:], psZ[:]),
                                  waits=(("td", dA),), sig="vt")
                zvv[cm] = vnop(sig="vt")
            if c < NCH:
                for k in range(KH):
                    E("t", lambda e, kk=k: e.transpose(
                        psT[:, kk * 128:(kk + 1) * 128],
                        tpsrc[:, kk * 128:(kk + 1) * 128], ident_s[:]),
                      waits=(("vt", xp_ev[c]),) if k == 0 else ())
            if c >= 1:
                for j in range(3):
                    E("t", lambda e, jj=j: e.transpose(
                        psZT[0:S, jj * 128:(jj + 1) * 128],
                        zib[:, jj * S:(jj + 1) * S], ident_s[:]),
                      waits=(("vt", zvv[cm]),) if j == 0 else ())
            dB = tdrain("td")
            if c < NCH:
                xpT_ev[c] = v_op(lambda e, cc=c: e.tensor_copy(
                    xpT_s.ap()[:, :, cc * 128:(cc + 1) * 128],
                    psT[:].rearrange("p (k n) -> p k n", k=KH)),
                    waits=(("td", dB),), sig="vt")
            if c >= 1:
                ziT_ev[cm] = v_op(lambda e, cc=cm: e.tensor_copy(
                    ziT3.ap()[:, :, cc * 128:(cc + 1) * 128],
                    psZT[0:S, :].rearrange("p (j n) -> p j n", j=3)),
                    waits=(("td", dB),) if c == NCH else (), sig="vt")

        sgx = s_op(lambda e: e.activation(gates[:], ziT3[:, 1:3, :], AF.Sigmoid),
                   waits=(("vt", sch.now("vt")),), sig="sv")
        v_op(lambda e: e.tensor_tensor(u_s[:], gates[:, 0, :], ziT3[:, 0, :],
                                       ALU.mult), waits=(("sv", sgx),))
        uw = vsettle2("vv")
        if st_i % ST_PER_SEQ == 0:
            v_op(lambda e: e.tensor_tensor_scan(hs_s[:], gates[:, 1, :], u_s[:],
                                                0.0, ALU.mult, ALU.add),
                 waits=(("vv", uw),))
        else:
            v_op(lambda e: e.tensor_tensor_scan(hs_s[:], gates[:, 1, :], u_s[:],
                                                carry[:], ALU.mult, ALU.add),
                 waits=(("vv", uw),))
        uw = vsettle2("vv")
        v_op(lambda e: e.tensor_copy(carry[:], hs_s[:, TS - 1:TS]),
             waits=(("vv", uw),))
        v_op(lambda e: e.tensor_copy(hsb_s[:], hs_s[:]))
        uw = vsettle2("vv")
        E("v", lambda e: e.engine_nop(), waits=(("vv", uw),))
        hv = vnop(sig="vt")

        p6_vs = sch.now("vs")
        p6_sv = sch.now("sv")
        p6_vt = sch.now("vt")
        p6_dn = sch.now("dn")
        m1_free = p6_vs
        yv1 = {}
        y1tv = {}
        av = {}
        xs2d = {}
        oa1 = {}
        dxr_cnt = {}
        th = {}
        for c in range(NCH + 1):
            cm = c - 1
            if c < NCH:
                ffirst = True
                for half in range(2):
                    sl = slice(half * 512, (half + 1) * 512)
                    dwaits = (("vs", m1_free),)
                    if c == 0:
                        dwaits = dwaits + (("vt", hv),)
                    mm(psM1[:, sl], ones_row[:], cprerow[0:1, sl], True, False,
                       waits=dwaits if ffirst else ())
                    ffirst = False
                    for kk in range(KH):
                        mm(psM1[:, sl], xpT_s[:, kk, c * 128:(c + 1) * 128],
                           WD_s[:, kk, sl], False, False)
                    mm(psM1[:, sl], hsb_s[:, c * 128:(c + 1) * 128],
                       C_s[0:S, sl], False, True)
                dA = tdrain("td")
            if c >= 1:
                ffirst = True
                for half in range(2):
                    sl = slice(half * 512, (half + 1) * 512)
                    ewaits = (("vt", y1tv[cm]),)
                    if cm >= 1:
                        ewaits = ewaits + (("vs", av[cm - 1]),)
                    mm(psM2[:, sl], ones_row[:], coutrow[0:1, sl], True, False,
                       waits=ewaits if ffirst else ())
                    ffirst = False
                    for kk in range(KH):
                        mm(psM2[:, sl], y1T_c[:, kk * 128:(kk + 1) * 128],
                           Wout_s[:, kk, sl], False, False)
                    for kk in range(KH):
                        mm(psM2[:, sl], xnT_s[:, kk, cm * 128:(cm + 1) * 128],
                           Win_s[:, kk, sl], False, kk == KH - 1)
                dB = tdrain("td")
                if cm == NCH - 1:
                    last_e_drain = dB
            if c < NCH:
                yv1[c] = v_op(lambda e: e.tensor_copy(y1pre[:], psM1[:]),
                              waits=(("td", dA),), sig="vs")
                m1_free = yv1[c]
                th[c] = s_op(lambda e: e.activation(tpsrc[:], y1pre[:],
                                                    AF.Tanh),
                             waits=(("vs", yv1[c]),), sig="sv")
                if c == 0:
                    g = st_i * NCH
                    dma("y", gsec.ap(), x_h[g * 128:(g + 1) * 128, :], "dxr",
                        waits=(("sv", p6_sv),))
                    dxr_cnt[0] = sch.now("dxr")
                if c == 0:
                    for k in range(KH):
                        E("t", lambda e, kk=k: e.transpose(
                            psT[:, kk * 128:(kk + 1) * 128],
                            tpsrc[:, kk * 128:(kk + 1) * 128], ident_s[:]),
                          waits=(("sv", th[0]), ("vt", p6_vt))
                          if k == 0 else ())
                    dC = tdrain("td")
                    y1tv[0] = v_op(lambda e: e.tensor_copy(y1T_c, psT[:]),
                                   waits=(("td", dC),), sig="vt")
            if c >= 1:
                par = cm % 2
                t1c, t2c, t3c = (tiny1.ap()[:, par:par + 1],
                                 tiny2.ap()[:, par:par + 1],
                                 tiny3.ap()[:, par:par + 1])
                aw = (("td", dB),)
                if cm >= 1:
                    aw = aw + (("sv", xs2d[cm - 1]),)
                av[cm] = v_op(lambda e: e.tensor_copy(q1b[:], psM2[:]),
                              waits=aw, sig="vs")
                v_op(lambda e: e.bn_stats(bn6[:, 0:6], q1b[:, 0:512]))
                v_op(lambda e: e.bn_stats(bn6[:, 6:12], q1b[:, 512:1024]))
                bs = vnop()
                v_op(lambda e: e.bn_aggr(bn2[:], bn6[:]), waits=(("vv", bs),))
                ba = vnop()
                v_op(lambda e, d=t1c: e.tensor_scalar(d, bn2[:, 1:2], LN_EPS,
                                                      None, ALU.add),
                     waits=(("vv", ba),))
                ba = vnop()
                v_op(lambda e, d=t2c, s_=t1c: e.reciprocal(d, s_),
                     waits=(("vv", ba),))
                v_op(lambda e, d=t1c: e.tensor_copy(d, bn2[:, 0:1]))
                vv = vsettle2("vs")
                sq2 = s_op(lambda e, d=t3c, s_=t2c: e.activation(d, s_,
                                                                 AF.Sqrt),
                           waits=(("vs", vv),), sig="sv")
                wv = E("v", lambda e: e.engine_nop(), waits=(("sv", sq2),),
                       sig="vv")
                v_op(lambda e, d=t2c, m=t1c, r=t3c:
                     e.scalar_tensor_tensor(d, m, -1.0, r, ALU.mult, ALU.mult),
                     waits=(("vv", wv),))
                vv = vnop(sig="vs")
                if 1 <= c < NCH:
                    for k in range(KH):
                        E("t", lambda e, kk=k: e.transpose(
                            psT[:, kk * 128:(kk + 1) * 128],
                            tpsrc[:, kk * 128:(kk + 1) * 128], ident_s[:]),
                          waits=(("sv", th[c]), ("vt", y1tv[cm]))
                          if k == 0 else ())
                    dC = tdrain("td")
                    y1tv[c] = v_op(lambda e: e.tensor_copy(y1T_c, psT[:]),
                                   waits=(("td", dC),), sig="vt")
                xs2d[cm] = s_op(lambda e, b=t2c, s_=t3c:
                                e.activation(xhat_b[:], q1b[:], AF.Identity,
                                             bias=b, scale=s_),
                                waits=(("vs", vv),), sig="sv")
                v_op(lambda e: e.tensor_tensor(a2q[:], xhat_b[:], gamma_t[:],
                                               ALU.mult),
                     waits=(("sv", xs2d[cm]),))
                ow = (("dxr", dxr_cnt[cm]),)
                ow = ow + (("dn", store_dn[cm - 1]),) if cm >= 1 else                     ow + (("dn", p6_dn),)
                oa1[cm] = v_op(lambda e, c_=cm: e.tensor_tensor(
                    outc[:], f_s[:, c_, :], gsec[:], ALU.add),
                    waits=ow, sig="vs")
                if cm + 1 < NCH:
                    g2 = st_i * NCH + cm + 1
                    dma("y", gsec.ap(), x_h[g2 * 128:(g2 + 1) * 128, :],
                        "dxr", waits=(("vs", oa1[cm]),))
                    dxr_cnt[cm + 1] = sch.now("dxr")
                ov = v_op(lambda e: e.tensor_tensor(outc[:], outc[:], a2q[:],
                                                    ALU.add), sig="vy")
                g = st_i * NCH + cm
                dma("y", out_h[g * 128:(g + 1) * 128, :], outc.ap(),
                    "dn", waits=(("vy", ov),))
                store_dn[cm] = sch.now("dn")
        else:
            pass
    E("y", lambda e: e.nop(), waits=(("dn", sch.now("dn")),))

    sems = {}
    import contextlib
    with contextlib.ExitStack() as ctx:
        def get_sem(name):
            if name not in sems:
                sems[name] = ctx.enter_context(nc.semaphore(name))
            return sems[name]
        for name in sch.cnt:
            get_sem(name)
        block = ctx.enter_context(nc.Block())
        eng_attr = {"t": "tensor", "v": "vector", "s": "scalar", "y": "sync",
                    "g": "gpsimd"}
        for ename, attr in eng_attr.items():
            ops = sch.q[ename]
            if not ops:
                continue
            def make_section(ops_):
                def section(eng):
                    for fn, waits, sig, inc in ops_:
                        for sname, val in waits:
                            if val > 0:
                                eng.wait_ge(get_sem(sname), val)
                        inst = fn(eng)
                        if sig is not None:
                            assert inst is not None, "signal op returned None"
                            inst.then_inc(get_sem(sig), inc)
                return section
            getattr(block, attr)(make_section(ops))
    return nc


_CACHED = {}


def kernel(**inputs):
    import ml_dtypes
    x = np.ascontiguousarray(inputs["L0_attn_x"], np.float32).reshape(B_FULL * T, H)
    ident = np.eye(128, dtype=ml_dtypes.bfloat16)
    onesr = np.ones((1, 128), dtype=ml_dtypes.bfloat16)
    row = lambda v: np.ascontiguousarray(np.asarray(v, np.float32).reshape(1, -1))
    common = {
        "W_in": np.ascontiguousarray(inputs["L0_attn_W_in"], np.float32),
        "W_B": np.ascontiguousarray(inputs["L0_attn_W_B"], np.float32),
        "W_D": np.ascontiguousarray(inputs["L0_attn_W_D"], np.float32),
        "W_out": np.ascontiguousarray(inputs["L0_attn_W_out"], np.float32),
        "B_mat": np.ascontiguousarray(inputs["L0_attn_B_mat"], np.float32),
        "C_mat": np.ascontiguousarray(inputs["L0_attn_C_mat"], np.float32),
        "W_ig": np.ascontiguousarray(inputs["L0_attn_W_ig"], np.float32),
        "W_fg": np.ascontiguousarray(inputs["L0_attn_W_fg"], np.float32),
        "b_in": row(inputs["L0_attn_b_in"]),
        "b_B": row(inputs["L0_attn_b_B"]),
        "b_D": row(inputs["L0_attn_b_D"]),
        "b_out": row(inputs["L0_attn_b_out"]),
        "b_ig": row(inputs["L0_attn_b_ig"]),
        "b_fg": row(inputs["L0_attn_b_fg"]),
        "gamma": row(inputs["L0_attn_gamma"]),
        "beta": row(inputs["L0_attn_beta"]),
        "w_gate": np.ascontiguousarray(inputs["L0_ffn_w_gate"], np.float32),
        "w_up": np.ascontiguousarray(inputs["L0_ffn_w_up"], np.float32),
        "w_down": np.ascontiguousarray(inputs["L0_ffn_w_down"], np.float32),
        "ident": ident,
        "onesr": onesr,
    }
    if "nc" not in _CACHED:
        _CACHED["nc"] = build_program()
    nc = _CACHED["nc"]
    in_maps = []
    for ci in range(NCORES):
        m = dict(common)
        m["x"] = np.ascontiguousarray(x[ci * R:(ci + 1) * R])
        in_maps.append(m)
    res = run_bass_kernel_spmd(nc, in_maps, core_ids=list(range(NCORES)))
    out = np.concatenate([r["out"] for r in res.results], axis=0)
    return out.reshape(B_FULL, T, H)


if __name__ == "__main__":
    pass


# version 14
# speedup vs baseline: 1.5088x; 1.0385x over previous
import numpy as np

import concourse.bass as bass
import concourse.mybir as mybir
from concourse.bass_utils import run_bass_kernel_spmd

F32 = mybir.dt.float32
BF16 = mybir.dt.bfloat16
AF = mybir.ActivationFunctionType
ALU = mybir.AluOpType
AX = mybir.AxisListType

NCORES = 8
H = 1024
S = 16
II = 2816
B_FULL, T = 32, 2048
R = (B_FULL // NCORES) * T
CH = 128
TS = 512
NCH = TS // CH
import os as _os
NST = int(_os.environ.get("K_NST", R // TS))
ST_PER_SEQ = T // TS
KH = H // 128
KI = II // 128
SECS = [512, 512, 512, 512, 512, 256]
LN_EPS = 1e-5


class Sched:
    ENGS = ("t", "v", "s", "y", "g")

    def __init__(self):
        self.q = {e: [] for e in self.ENGS}
        self.cnt = {}

    def emit(self, eng, fn, waits=(), sig=None, inc=1):
        self.q[eng].append((fn, tuple(waits), sig, inc))
        if sig is not None:
            self.cnt[sig] = self.cnt.get(sig, 0) + inc
        return self.cnt.get(sig, 0) if sig else 0

    def now(self, sem):
        return self.cnt.get(sem, 0)


def build_program():
    nc = bass.Bass("TRN2", target_bir_lowering=False,
                   detect_race_conditions=False)

    dT = lambda n, sh, dt=F32, kind="ExternalInput": nc.dram_tensor(n, sh, dt, kind=kind).ap()
    x_h = dT("x", [R, H])
    W_in_h = dT("W_in", [H, H])
    W_B_h = dT("W_B", [H, H])
    W_D_h = dT("W_D", [H, H])
    W_out_h = dT("W_out", [H, H])
    B_mat_h = dT("B_mat", [H, S])
    C_mat_h = dT("C_mat", [S, H])
    W_ig_h = dT("W_ig", [H, S])
    W_fg_h = dT("W_fg", [H, S])
    b_in_h = dT("b_in", [1, H])
    b_B_h = dT("b_B", [1, H])
    b_D_h = dT("b_D", [1, H])
    b_out_h = dT("b_out", [1, H])
    b_ig_h = dT("b_ig", [1, S])
    b_fg_h = dT("b_fg", [1, S])
    gamma_h = dT("gamma", [1, H])
    beta_h = dT("beta", [1, H])
    w_gate_h = dT("w_gate", [II, H])
    w_up_h = dT("w_up", [II, H])
    w_down_h = dT("w_down", [H, II])
    ident_h = dT("ident", [128, 128], BF16)
    onesr_h = dT("onesr", [1, 128], BF16)
    out_h = dT("out", [R, H], F32, kind="ExternalOutput")

    gateT_h = nc.dram_tensor("gateT_q", [KH, 128, II], BF16, kind="Internal").ap()
    upT_h = nc.dram_tensor("upT_q", [KH, 128, II], BF16, kind="Internal").ap()
    downT_h = nc.dram_tensor("downT_q", [KI, 128, H], BF16, kind="Internal").ap()

    sb = lambda n, sh, dt: nc.alloc_sbuf_tensor(n, sh, dt)
    Win_s = sb("Win_s", [128, KH, H], BF16)
    WD_s = sb("WD_s", [128, KH, H], BF16)
    Wout_s = sb("Wout_s", [128, KH, H], BF16)
    WS3_s = sb("WS3_s", [128, KH, 3 * S], BF16)
    Bm_s = sb("Bm_s", [128, KH, S], BF16)
    C_s = sb("C_s", [S, H], BF16)
    gamma_t = sb("gamma_t", [128, H], BF16)
    ident_s = sb("ident_s", [128, 128], BF16)
    ones_row = sb("ones_row", [1, 128], BF16)
    binrow = sb("binrow", [1, H], BF16)
    betarow = sb("betarow", [1, H], BF16)
    czrow = sb("czrow", [1, 3 * S], BF16)
    cprerow = sb("cprerow", [1, H], BF16)
    coutrow = sb("coutrow", [1, H], BF16)
    bincol = sb("bincol", [128, KH], BF16)
    bBcol = sb("bBcol", [128, KH], BF16)
    biasrow48 = sb("biasrow48", [1, 3 * S], BF16)
    bias_p1 = sb("bias_p1", [128, 1], F32)
    bias_n1 = sb("bias_n1", [128, 1], F32)
    xnT_s = sb("xnT_s", [128, KH, TS], BF16)
    xqT_s = sb("xqT_s", [128, KH, TS], BF16)
    xpT_s = sb("xpT_s", [128, KH, TS], BF16)
    hqT_s = sb("hqT_s", [128, KI, TS], BF16)
    f_s = sb("f_s", [128, NCH, H], BF16)
    ziT3 = sb("ziT3", [S, 3, TS], BF16)
    gates = sb("gates", [S, 2, TS], F32)
    u_s = sb("u_s", [S, TS], F32)
    hs_s = sb("hs_s", [S, TS], F32)
    hsb_s = sb("hsb_s", [S, TS], BF16)
    carry = sb("carry", [S, 1], F32)
    gup_s = sb("gup_s", [128, 2, 2, KH, 512], BF16)
    dwn_s = sb("dwn_s", [128, 2, 2, H], BF16)
    xin_s = sb("xin_s", [128, 2, H], F32)
    xnb_s = sb("xnb_s", [128, H], BF16)
    tpsrc = sb("tpsrc", [128, H], BF16)
    qin = sb("qin", [128, H], F32)
    sm2 = sb("sm2", [128, H], F32)
    a2q = sb("a2q", [128, H], F32)
    q1b = sb("q1b", [128, H], F32)
    q2b = sb("q2b", [128, H], F32)
    bmq = sb("bmq", [128, H // 16], F32)
    scq = sb("scq", [128, H // 16], F32)
    rcq = sb("rcq", [128, H // 16], F32)
    gsec = sb("gsec", [128, H], F32)
    bn6 = sb("bn6", [128, 12], F32)
    bn2 = sb("bn2", [128, 2], F32)
    tiny1 = sb("tiny1", [128, 2], F32)
    tiny2 = sb("tiny2", [128, 2], F32)
    tiny3 = sb("tiny3", [128, 2], F32)
    zib = sb("zib", [128, 3 * S], BF16)
    sgp = gsec
    y1pre = sm2
    y1T_t = sb("y1T_t", [128, KH * 128], BF16)
    y1T_c = y1T_t.ap()[:, 0:KH * 128]
    rowA = gsec.ap()[0:1, :]
    rowB = sm2.ap()[0:1, :]
    rowbf = tpsrc.ap()[0:1, :]
    bdrow = xnb_s.ap()[0:1, :]
    xhat_b = xnb_s
    outc = qin

    psM1 = nc.alloc_psum_tensor("psM1", [128, H], F32)
    psM2 = nc.alloc_psum_tensor("psM2", [128, H], F32)
    psT = nc.alloc_psum_tensor("psT", [128, 1024], BF16)
    psZ = nc.alloc_psum_tensor("psZ", [128, 3 * S], F32)
    psZT = nc.alloc_psum_tensor("psZT", [S, 3 * 128], BF16)

    sch = Sched()
    E = sch.emit

    def dma(q_, dst, src, sig, waits=()):
        return E(q_, lambda e, d=dst, s=src: e.dma_start(out=d, in_=s),
                 waits=waits, sig=sig, inc=16)

    def tdrain(sig, waits=()):
        def fn(e):
            e.drain()
            return e.nop()
        return E("t", fn, waits=waits, sig=sig)

    def mm(out_ap, lhsT, rhs, start, stop, waits=()):
        return E("t", lambda e, o=out_ap, l=lhsT, r=rhs, a=start, b=stop:
                 e.matmul(o, l, r, start=a, stop=b), waits=waits)

    def v_op(fn, waits=(), sig=None):
        return E("v", fn, waits=waits, sig=sig)

    def vnop(sig="vv"):
        return E("v", lambda e: e.engine_nop(), sig=sig)

    def vsettle2(sig):
        t = E("v", lambda e: e.engine_nop(), sig="vv")
        return E("v", lambda e: e.engine_nop(), waits=(("vv", t),), sig=sig)

    def snop(sig="sv"):
        return E("s", lambda e: e.nop(cycle_cnt=200), sig=sig)

    def s_op(fn, waits=(), sig=None):
        return E("s", fn, waits=waits, sig=sig)

    def transposes(src_ap, n, waits=()):
        for k in range(n):
            E("t", lambda e, kk=k, sa=src_ap: e.transpose(
                psT[:, kk * 128:(kk + 1) * 128], sa[:, kk * 128:(kk + 1) * 128],
                ident_s[:]), waits=waits if k == 0 else ())
        return tdrain("td")

    eq_mv = [0]
    def emit_quant(w, qout_ap, waits_v=(), waits_out=(), src=None, c20=None):
        nb = w // 16
        if src is None:
            src = qin.ap()[:, 0:w]
        if c20 is None:
            c20 = qin.ap()[:, 0:w]
        qin3 = src.rearrange("p (b c) -> p b c", c=16)
        sm23 = sm2.ap()[:, 0:w].rearrange("p (b c) -> p b c", c=16)
        rcb = rcq.ap()[:, 0:nb].unsqueeze(2).broadcast_to([128, nb, 16])
        scb = scq.ap()[:, 0:nb].unsqueeze(2).broadcast_to([128, nb, 16])
        v_op(lambda e: e.tensor_reduce(bmq[:, 0:nb].unsqueeze(2), qin3,
                                       AX.X, ALU.max,
                                       apply_absolute_value=True),
             waits=waits_v)
        rv = vnop()
        v_op(lambda e: e.tensor_scalar(scq[:, 0:nb], bmq[:, 0:nb], 1.0 / 24.0,
                                       2.5e-13, ALU.mult, ALU.max),
             waits=(("vv", rv),))
        rv = vnop()
        v_op(lambda e: e.reciprocal(rcq[:, 0:nb], scq[:, 0:nb]),
             waits=(("vv", rv),))
        rv = vsettle2("vv")
        mv = v_op(lambda e: e.scalar_tensor_tensor(sm23, qin3, 1.0, rcb,
                                                   ALU.mult, ALU.mult),
                  waits=(("vv", rv),), sig="vs")
        eq_mv[0] = mv
        s_op(lambda e: e.activation(sgp[:, 0:w], sm2[:, 0:w], AF.Sign,
                                    bias=bias_n1[:], scale=1.0),
             waits=(("vs", mv),))
        v_op(lambda e: e.scalar_tensor_tensor(a2q[:, 0:w], sm2[:, 0:w], -1.0,
                                              sm2[:, 0:w], ALU.mult, ALU.max))
        v_op(lambda e: e.tensor_scalar(q1b[:, 0:w], a2q[:, 0:w], 3.0, None,
                                       ALU.is_gt))
        v_op(lambda e: e.scalar_tensor_tensor(q1b[:, 0:w], a2q[:, 0:w], 5.0,
                                              q1b[:, 0:w], ALU.is_gt, ALU.add))
        v_op(lambda e: e.scalar_tensor_tensor(q1b[:, 0:w], a2q[:, 0:w], 7.0,
                                              q1b[:, 0:w], ALU.is_gt, ALU.add))
        v_op(lambda e: e.tensor_scalar(q2b[:, 0:w], a2q[:, 0:w], 10.0, None,
                                       ALU.is_gt))
        v_op(lambda e: e.scalar_tensor_tensor(q2b[:, 0:w], a2q[:, 0:w], 14.0,
                                              q2b[:, 0:w], ALU.is_gt, ALU.add))
        v_op(lambda e: e.tensor_scalar(c20, a2q[:, 0:w], 20.0, None,
                                       ALU.is_gt))
        v_op(lambda e: e.scalar_tensor_tensor(q2b[:, 0:w], c20, 2.0,
                                              q2b[:, 0:w], ALU.mult, ALU.add))
        a2f = v_op(lambda e: e.scalar_tensor_tensor(q1b[:, 0:w], q2b[:, 0:w], 2.0,
                                                    q1b[:, 0:w], ALU.mult, ALU.add),
                   sig="vs")
        ssig = s_op(lambda e: e.activation(a2q[:, 0:w], sm2[:, 0:w], AF.Sign,
                                           bias=bias_p1[:], scale=1.0),
                    waits=(("vs", a2f),), sig="sv")
        v_op(lambda e: e.tensor_tensor(sgp[:, 0:w], sgp[:, 0:w], a2q[:, 0:w],
                                       ALU.add), waits=(("sv", ssig),))
        v_op(lambda e: e.scalar_tensor_tensor(q2b[:, 0:w], q1b[:, 0:w], 1.0,
                                              sgp[:, 0:w], ALU.add, ALU.mult))
        q23 = q2b.ap()[:, 0:w].rearrange("p (b c) -> p b c", c=16)
        qo3 = qout_ap.rearrange("p (b c) -> p b c", c=16)
        return v_op(lambda e: e.scalar_tensor_tensor(qo3, q23, 1.0, scb,
                                                     ALU.mult, ALU.mult),
                    waits=waits_out, sig="vq")

    v_op(lambda e: e.memset(bias_p1[:], 1.0))
    v_op(lambda e: e.memset(bias_n1[:], -1.0))
    v_op(lambda e: e.memset(biasrow48[:], 0.0))

    dma("y", ident_s.ap(), ident_h, "dc")
    dma("y", ones_row.ap(), onesr_h, "dc")
    for dst, src in ((Win_s, W_in_h), (WD_s, W_D_h), (Wout_s, W_out_h)):
        dma("g", dst.ap(), src.rearrange("(k p) n -> p k n", p=128), "dwc")
    dma("g", WS3_s.ap()[:, :, S:2 * S], W_ig_h.rearrange("(k p) n -> p k n", p=128), "dwc")
    dma("g", WS3_s.ap()[:, :, 2 * S:3 * S], W_fg_h.rearrange("(k p) n -> p k n", p=128), "dwc")
    dma("g", Bm_s.ap(), B_mat_h.rearrange("(k p) n -> p k n", p=128), "dwc")
    dma("g", C_s.ap(), C_mat_h, "dwc")
    WB_v = gup_s.ap()[:, 0].rearrange("p m k n -> p (m k n)").rearrange(
        "p (a b) -> p a b", b=H)
    WBT_v = gup_s.ap()[:, 1].rearrange("p m k n -> p (m k n)").rearrange(
        "p (a b) -> p a b", b=H)
    dma("g", WB_v, W_B_h.rearrange("(k p) n -> p k n", p=128), "dwb")

    dma("y", rowA, b_in_h, "drA")
    cv = v_op(lambda e: e.tensor_copy(binrow[:], rowA),
              waits=(("drA", 16), ("dc", 32)), sig="vt")
    for k in range(KH):
        E("t", lambda e, kk=k: e.transpose(psT[:, 2 * kk:2 * kk + 1],
                                           binrow[0:1, kk * 128:(kk + 1) * 128],
                                           ident_s[0:1, 0:1]),
          waits=(("vt", cv),) if k == 0 else ())
    tdrain("td")
    v_op(lambda e: e.tensor_copy(bincol[:],
                                 psT.ap()[:, 0:2 * KH].rearrange(
                                     "p (k two) -> p k two", two=2)[:, :, 0]),
         waits=(("td", sch.now("td")),))
    vnop(sig="vt")
    dma("y", rowB, b_out_h, "drB")
    v_op(lambda e: e.tensor_tensor(coutrow[:], rowA, rowB, ALU.add),
         waits=(("drB", 16),), sig="vt")
    dma("y", rowA, b_B_h, "drA", waits=(("vt", sch.now("vt")),))
    cv = v_op(lambda e: e.tensor_copy(rowbf, rowA),
              waits=(("drA", 32),), sig="vt")
    for k in range(KH):
        E("t", lambda e, kk=k: e.transpose(psT[:, 2 * kk:2 * kk + 1],
                                           rowbf[0:1, kk * 128:(kk + 1) * 128],
                                           ident_s[0:1, 0:1]),
          waits=(("vt", cv),) if k == 0 else ())
    tdrain("td")
    v_op(lambda e: e.tensor_copy(bBcol[:],
                                 psT.ap()[:, 0:2 * KH].rearrange(
                                     "p (k two) -> p k two", two=2)[:, :, 0]),
         waits=(("td", sch.now("td")),))
    vnop(sig="vt")
    dma("y", rowA, b_D_h, "drA", waits=(("vt", sch.now("vt")),))
    cv = v_op(lambda e: e.tensor_copy(bdrow, rowA), waits=(("drA", 48),),
              sig="vt")
    dma("y", rowA, gamma_h, "drA", waits=(("vt", cv),))
    cv = v_op(lambda e: e.tensor_copy(rowbf, rowA), waits=(("drA", 64),),
              sig="vt")
    for half in range(2):
        mm(psM1[:, half * 512:(half + 1) * 512], ones_row[:],
           rowbf[0:1, half * 512:(half + 1) * 512], True, True,
           waits=(("vt", cv),) if half == 0 else ())
    tdrain("td")
    v_op(lambda e: e.tensor_copy(gamma_t[:], psM1[:]),
         waits=(("td", sch.now("td")),), sig="vt")
    dma("y", rowA, beta_h, "drA", waits=(("vt", sch.now("vt")),))
    cv = v_op(lambda e: e.tensor_copy(betarow[:], rowA), waits=(("drA", 80),),
              sig="vt")
    dma("y", rowA[0:1, 0:S], b_ig_h, "drA", waits=(("vt", cv),))
    dma("y", rowA[0:1, S:2 * S], b_fg_h, "drA")
    cv = v_op(lambda e: e.tensor_copy(biasrow48[0:1, S:2 * S], rowA[0:1, 0:S]),
              waits=(("drA", 112),), sig="vt")
    v_op(lambda e: e.tensor_copy(biasrow48[0:1, 2 * S:3 * S],
                                 rowA[0:1, S:2 * S]))
    cv = vnop(sig="vt")

    for kk in range(KH):
        for jj in range(KH):
            E("t", lambda e, a=kk, b=jj: e.transpose(
                psT[:, b * 128:(b + 1) * 128],
                WB_v[:, b, a * 128:(a + 1) * 128], ident_s[:]),
              waits=((("dwb", 16),) if kk == 0 else ()) +
                    (("vt", sch.now("vt")),) if jj == 0 else ())
        tdrain("td")
        v_op(lambda e, a=kk: e.tensor_copy(WBT_v[:, a, :], psT[:]),
             waits=(("td", sch.now("td")),), sig="vt")
    for r in range(KH):
        for kk in range(KH):
            mm(psM2[:, 0:S], WBT_v[:, kk, r * 128:(r + 1) * 128], Bm_s[:, kk, :],
               kk == 0, kk == KH - 1,
               waits=((("dwc", sch.now("dwc")),) if r == 0 else ()) +
                     (("vt", sch.now("vt")),) if kk == 0 else ())
        tdrain("td")
        v_op(lambda e, rr=r: e.tensor_copy(WS3_s[:, rr, 0:S], psM2[:, 0:S]),
             waits=(("td", sch.now("td")),))
        vnop(sig="vt")

    for kk in range(KH):
        mm(psZ[0:S, 0:1], Bm_s[:, kk, :], bBcol[:, kk:kk + 1], kk == 0,
           kk == KH - 1, waits=(("vt", sch.now("vt")),) if kk == 0 else ())
    tdrain("td")
    v_op(lambda e: e.tensor_copy(zib[0:S, 0:1], psZ[0:S, 0:1]),
         waits=(("td", sch.now("td")),))
    vnop(sig="vt")
    E("t", lambda e: e.transpose(psZT[0:1, 0:S], zib[0:S, 0:1],
                                 ident_s[0:S, 0:S]),
      waits=(("vt", sch.now("vt")),))
    tdrain("td")
    cw = vnop()
    v_op(lambda e: e.tensor_copy(czrow[:], biasrow48[:]),
         waits=(("td", sch.now("td")), ("vv", cw)))
    cw = vnop()
    v_op(lambda e: e.tensor_tensor(czrow[0:1, 0:S], czrow[0:1, 0:S],
                                   psZT[0:1, 0:S], ALU.add),
         waits=(("vv", cw),))
    vnop(sig="vt")

    v_op(lambda e: e.tensor_copy(cprerow[:], bdrow))
    vnop(sig="vt")

    vnop()
    wq_bar = vnop()
    jobs = []
    for src_h, dstT_h, nchunk, width in (
            (w_gate_h, gateT_h, KI, H), (w_up_h, upT_h, KI, H),
            (w_down_h, downT_h, KH, II)):
        for icx in range(nchunk):
            subs = ([(0, 1024), (1024, 2048), (2048, 2816)]
                    if width == II else [(0, 1024)])
            for s0_, s1_ in subs:
                jobs.append((src_h, dstT_h, icx, s0_, s1_))
    nj = len(jobs)

    def wq_src(i):
        return qin.ap() if i % 2 == 0 else xin_s.ap()[:, 0, :]

    sh0, _, ic0, a0, a1 = jobs[0]
    dma("y", wq_src(0)[:, 0:a1 - a0], sh0[ic0 * 128:(ic0 + 1) * 128, a0:a1],
        "dfw", waits=(("vv", wq_bar),))
    wq_ev = {}
    wq_dn = {}
    wq_td = {}
    wq_mv = {}
    for i, (src_h, dstT_h, icx, s0_, s1_) in enumerate(jobs):
        wsub = s1_ - s0_
        nk = wsub // 128
        if i + 1 < nj:
            sh2, _, ic2, t0_, t1_ = jobs[i + 1]
            pw = (("vs", wq_mv[i - 1]),) if i >= 1 else (("vv", wq_bar),)
            dma("y", wq_src(i + 1)[:, 0:t1_ - t0_],
                sh2[ic2 * 128:(ic2 + 1) * 128, t0_:t1_], "dfw", waits=pw)
        vqw = emit_quant(wsub, tpsrc.ap()[:, 0:wsub],
                         waits_v=(("dfw", 16 * (i + 1)),),
                         waits_out=(("td", wq_td[i - 1]),) if i >= 1 else (),
                         src=wq_src(i)[:, 0:wsub],
                         c20=xin_s.ap()[:, 1, 0:wsub])
        wq_mv[i] = eq_mv[0]
        for k in range(nk):
            E("t", lambda e, kk=k: e.transpose(
                psT[:, kk * 128:(kk + 1) * 128],
                tpsrc[:, kk * 128:(kk + 1) * 128], ident_s[:]),
              waits=(("vq", vqw), ("vt", wq_ev[i - 1])) if (k == 0 and i >= 1)
              else ((("vq", vqw), ("vt", sch.now("vt"))) if k == 0 else ()))
        wq_td[i] = tdrain("td")
        if i % 2 == 0:
            dst3 = y1T_c[:, 0:nk * 128].rearrange("p (k n) -> p k n", k=nk)
        else:
            dst3 = xnT_s.ap()[:, 0:nk, 0:128]
        evw = (("td", wq_td[i]),)
        if i >= 2:
            evw = evw + (("dn", wq_dn[i - 2]),)
        wq_ev[i] = v_op(lambda e, d=dst3, n_=nk: e.tensor_copy(
            d, psT[:, 0:n_ * 128].rearrange("p (k n) -> p k n", k=n_)),
            waits=evw, sig="vt")
        dst = dstT_h[s0_ // 128:(s0_ // 128) + nk, :,
                     icx * 128:(icx + 1) * 128].rearrange("k p n -> p k n")
        dma("y", dst, dst3, "dn", waits=(("vt", wq_ev[i]),))
        wq_dn[i] = sch.now("dn")

    dma("y", xin_s.ap()[:, 0, :], x_h[0:128, :], "dx",
        waits=(("dn", sch.now("dn")),))
    dma("y", xin_s.ap()[:, 1, :], x_h[128:256, :], "dx")
    n_dx = 2

    last_e_drain = sch.now("td")
    store_dn = {}
    last_fdown_read = sch.now("td")
    for st_i in range(NST):
        p1_ss = {}
        p1_tdXQ = None
        p1_xqc = None
        sv_base = sch.now("sv")
        for c in range(NCH):
            g = st_i * NCH + c
            par = c % 2
            t1c, t2c, t3c = (tiny1.ap()[:, par:par + 1],
                             tiny2.ap()[:, par:par + 1],
                             tiny3.ap()[:, par:par + 1])
            slot = g % 2
            dxw = 16 * (g + 1)
            xa = xin_s.ap()[:, slot, :]
            wA = (("dx", dxw),
                  ("sv", p1_ss[g - 2]) if g - 2 in p1_ss else ("sv", sv_base))
            v_op(lambda e, a=xa: e.bn_stats(bn6[:, 0:6], a[:, 0:512]), waits=wA)
            v_op(lambda e, a=xa: e.bn_stats(bn6[:, 6:12], a[:, 512:1024]))
            bs = vnop()
            v_op(lambda e: e.bn_aggr(bn2[:], bn6[:]), waits=(("vv", bs),))
            ba = vnop()
            v_op(lambda e, d=t1c: e.tensor_scalar(d, bn2[:, 1:2], LN_EPS, None,
                                                  ALU.add), waits=(("vv", ba),))
            ba = vnop()
            v_op(lambda e, d=t2c, s_=t1c: e.reciprocal(d, s_),
                 waits=(("vv", ba),))
            v_op(lambda e, d=t1c: e.tensor_copy(d, bn2[:, 0:1]))
            vv = vsettle2("vs")
            sq = s_op(lambda e, d=t3c, s_=t2c: e.activation(d, s_, AF.Sqrt),
                      waits=(("vs", vv),), sig="sv")
            wv = E("v", lambda e: e.engine_nop(), waits=(("sv", sq),), sig="vv")
            v_op(lambda e, d=t2c, m=t1c, r=t3c:
                 e.scalar_tensor_tensor(d, m, -1.0, r, ALU.mult, ALU.mult),
                 waits=(("vv", wv),))
            vv = vnop(sig="vs")
            ss = s_op(lambda e, a=xa, b=t2c, s_=t3c:
                      e.activation(qin[:], a, AF.Identity, bias=b, scale=s_),
                      waits=(("vs", vv), ("vq", sch.now("vq")),
                             ("dn", sch.now("dn"))), sig="sv")
            wxn = (("td", p1_tdXN),) if c > 0 else ()
            ss = s_op(lambda e, a=xa, b=t2c, s_=t3c:
                      e.activation(xnb_s[:], a, AF.Identity, bias=b, scale=s_),
                      waits=wxn, sig="sv")
            p1_ss[g] = ss
            if g + 2 < NST * NCH:
                gg = g + 2
                dma("y", xin_s.ap()[:, gg % 2, :],
                    x_h[gg * 128:(gg + 1) * 128, :], "dx",
                    waits=(("sv", ss),))
                n_dx += 1
            if p1_tdXQ is not None:
                ew = (("td", p1_tdXQ),)
                if p1_xqc == 0:
                    ew = ew + (("td", last_e_drain),)
                xq_ev = v_op(lambda e, cc=p1_xqc: e.tensor_copy(
                    xqT_s.ap()[:, :, cc * 128:(cc + 1) * 128],
                    psT[:].rearrange("p (k n) -> p k n", k=KH)),
                    waits=ew, sig="vt")
                psT_free = xq_ev
            else:
                psT_free = sch.now("vt")
            for k in range(KH):
                E("t", lambda e, kk=k: e.transpose(
                    psT[:, kk * 128:(kk + 1) * 128],
                    xnb_s[:, kk * 128:(kk + 1) * 128], ident_s[:]),
                  waits=(("sv", ss), ("vt", psT_free)) if k == 0 else ())
            p1_tdXN = tdrain("td")
            emit_quant(H, tpsrc.ap()[:, 0:H], waits_v=(("sv", ss),))
            vq_c = sch.now("vq")
            dw = (("td", p1_tdXN),)
            if c == 0:
                dw = dw + (("td", last_e_drain),)
            xn_ev = v_op(lambda e, cc=c: e.tensor_copy(
                xnT_s.ap()[:, :, cc * 128:(cc + 1) * 128],
                psT[:].rearrange("p (k n) -> p k n", k=KH)),
                waits=dw, sig="vt")
            for k in range(KH):
                E("t", lambda e, kk=k: e.transpose(
                    psT[:, kk * 128:(kk + 1) * 128],
                    tpsrc[:, kk * 128:(kk + 1) * 128], ident_s[:]),
                  waits=(("vq", vq_c), ("vt", xn_ev)) if k == 0 else ())
            p1_tdXQ = tdrain("td")
            p1_xqc = c
        v_op(lambda e, cc=p1_xqc: e.tensor_copy(
            xqT_s.ap()[:, :, cc * 128:(cc + 1) * 128],
            psT[:].rearrange("p (k n) -> p k n", k=KH)),
            waits=(("td", p1_tdXQ),), sig="vt")

        ksec0 = 0
        sec_drain = {}
        m1_free = sch.now("vs")
        m2_free = sch.now("vs")
        xqT_ready = sch.now("vt")
        pend = None
        tp_drain_prev = sch.now("td")
        for si, SEC in enumerate(SECS):
            buf = si % 2
            nk = SEC // 128
            if si >= 2:
                wbuf = (("td", sec_drain[si - 2]),)
            else:
                wbuf = (("td", sch.now("td")),)
            dma("g", gup_s.ap()[:, buf, 0, :, 0:SEC],
                gateT_h[:, :, ksec0:ksec0 + SEC].rearrange("k p n -> p k n"),
                "dgu", waits=wbuf)
            dma("g", gup_s.ap()[:, buf, 1, :, 0:SEC],
                upT_h[:, :, ksec0:ksec0 + SEC].rearrange("k p n -> p k n"),
                "dgu")
            wg = sch.now("dgu")
            for pr in range(2):
                c0, c1 = 2 * pr, 2 * pr + 1
                ffirst = True
                for ci, cc in enumerate((c0, c1)):
                    for kk in range(KH):
                        mm(psM1[:, ci * 512:ci * 512 + SEC],
                           xqT_s[:, kk, cc * 128:(cc + 1) * 128],
                           gup_s[:, buf, 0, kk, 0:SEC], kk == 0, kk == KH - 1,
                           waits=(("dgu", wg), ("vs", m1_free),
                                  ("vt", xqT_ready)) if ffirst else ())
                        ffirst = False
                ffirst = True
                for ci, cc in enumerate((c0, c1)):
                    for kk in range(KH):
                        mm(psM2[:, ci * 512:ci * 512 + SEC],
                           xqT_s[:, kk, cc * 128:(cc + 1) * 128],
                           gup_s[:, buf, 1, kk, 0:SEC], kk == 0, kk == KH - 1,
                           waits=(("vs", m2_free),) if ffirst else ())
                        ffirst = False
                md = tdrain("td")
                if pr == 1:
                    sec_drain[si] = md
                if pend is not None:
                    tp_drain_prev = pend()
                    pend = None
                w2 = 2 * SEC
                pM1v = psM1.ap().rearrange("p (c n) -> p c n", c=2)[:, :, 0:SEC]
                pM2v = psM2.ap().rearrange("p (c n) -> p c n", c=2)[:, :, 0:SEC]
                gsv = gsec.ap()[:, 0:w2].rearrange("p (c n) -> p c n", c=2)
                q1v = q1b.ap()[:, 0:w2].rearrange("p (c n) -> p c n", c=2)
                qiv = qin.ap()[:, 0:w2].rearrange("p (c n) -> p c n", c=2)
                gv = v_op(lambda e, d=gsv, s_=pM1v: e.tensor_copy(d, s_),
                          waits=(("td", md),), sig="vs")
                m1_free = gv
                sl = s_op(lambda e, w=w2: e.activation(q1b[:, 0:w], gsec[:, 0:w],
                                                       AF.Silu),
                          waits=(("vs", gv),), sig="sv")
                m2_free = v_op(lambda e, d=qiv, s_=pM2v, q_=q1v:
                               e.scalar_tensor_tensor(d, s_, 1.0, q_,
                                                      ALU.mult, ALU.mult),
                               waits=(("sv", sl), ("vq", sch.now("vq"))),
                               sig="vs")
                vq_p = emit_quant(w2, tpsrc.ap()[:, 0:w2],
                                  waits_out=(("td", tp_drain_prev),))

                def make_pend(nk_=nk, k0=ksec0 // 128, cc0=c0, cc1=c1, vq=vq_p):
                    def emit_tp():
                        for ci in range(2):
                            for k in range(nk_):
                                E("t", lambda e, kk=k, cci=ci: e.transpose(
                                    psT[:, (cci * nk_ + kk) * 128:
                                        (cci * nk_ + kk + 1) * 128],
                                    tpsrc[:, (cci * nk_ + kk) * 128:
                                          (cci * nk_ + kk + 1) * 128],
                                    ident_s[:]),
                                  waits=(("vq", vq), ("vt", sch.now("vt")))
                                  if (ci == 0 and k == 0) else ())
                        td_ = tdrain("td")
                        for ci, cc in enumerate((cc0, cc1)):
                            v_op(lambda e, cci=ci, c_=cc:
                                 e.tensor_copy(
                                     hqT_s.ap()[:, k0:k0 + nk_,
                                                c_ * 128:(c_ + 1) * 128],
                                     psT[:, cci * nk_ * 128:(cci + 1) * nk_ * 128]
                                     .rearrange("p (k n) -> p k n", k=nk_)),
                                 waits=(("td", td_),) if ci == 0 else (),
                                 sig="vt")
                        return td_
                    return emit_tp
                pend = make_pend()
            ksec0 += SEC
        if pend is not None:
            tp_drain_prev = pend()
            pend = None

        for pr in range(2):
            c0, c1 = 2 * pr, 2 * pr + 1
            psf = {c0: psM1, c1: psM2}
            grp_drain = {}
            for grp in range(11):
                bufd = grp % 2
                if grp >= 2:
                    wbuf = (("td", grp_drain[grp - 2]),)
                else:
                    wbuf = (("td", sch.now("td")),)
                dma("g", dwn_s.ap()[:, bufd],
                    downT_h[2 * grp:2 * grp + 2].rearrange("k p n -> p k n"),
                    "ddn", waits=wbuf)
                wd = sch.now("ddn")
                for cc in (c0, c1):
                    if grp == 0:
                        for half in range(2):
                            mm(psf[cc][0:128, half * 512:(half + 1) * 512],
                               ones_row[:], betarow[0:1, half * 512:(half + 1) * 512],
                               True, False,
                               waits=(("ddn", wd), ("vq", sch.now("vq")),
                                      ("vt", sch.now("vt"))) if (half == 0 and cc == c0) else ())
                    for kc in range(2):
                        ic = 2 * grp + kc
                        for half in range(2):
                            mm(psf[cc][:, half * 512:(half + 1) * 512],
                               hqT_s[:, ic, cc * 128:(cc + 1) * 128],
                               dwn_s[:, bufd, kc, half * 512:(half + 1) * 512],
                               False, grp == 10 and kc == 1,
                               waits=(("ddn", wd),) if (grp > 0 and kc == 0 and half == 0 and cc == c0) else ())
                grp_drain[grp] = tdrain("td")
            
            for cc in (c0, c1):
                v_op(lambda e, c_=cc, p_=psf[cc]: e.tensor_copy(
                    f_s.ap()[:, c_, :], p_[:]),
                    waits=(("td", sch.now("td")),) if cc == c0 else (), sig="vt")

        p4_vt = sch.now("vt")
        xp_ev = {}
        xpT_ev = {}
        zib_ev = {}
        zvv = {}
        ziT_ev = {}
        for c in range(NCH + 1):
            cm = c - 1
            if c < NCH:
                ffirst = True
                for half in range(2):
                    sl = slice(half * 512, (half + 1) * 512)
                    bw = (("vt", xp_ev[c - 1]),) if c >= 1 else                         (("vt", p4_vt),)
                    mm(psM1[:, sl], ones_row[:], binrow[0:1, sl], True, False,
                       waits=bw if ffirst else ())
                    ffirst = False
                    for kk in range(KH):
                        mm(psM1[:, sl], xnT_s[:, kk, c * 128:(c + 1) * 128],
                           Win_s[:, kk, sl], False, kk == KH - 1)
            if c >= 1:
                zw = (("vt", xpT_ev[cm]),)
                if cm >= 1:
                    zw = zw + (("vt", zib_ev[cm - 1]),)
                mm(psZ[:], ones_row[:], czrow[:], True, False, waits=zw)
                for kk in range(KH):
                    mm(psZ[:], xpT_s[:, kk, cm * 128:(cm + 1) * 128],
                       WS3_s[:, kk, :], False, kk == KH - 1)
            dA = tdrain("td")
            if c < NCH:
                xp_ev[c] = v_op(lambda e: e.tensor_copy(tpsrc[:], psM1[:]),
                                waits=(("td", dA),), sig="vt")
            if c >= 1:
                zib_ev[cm] = v_op(lambda e: e.tensor_copy(zib[:], psZ[:]),
                                  waits=(("td", dA),), sig="vt")
                zvv[cm] = vnop(sig="vt")
            if c < NCH:
                for k in range(KH):
                    E("t", lambda e, kk=k: e.transpose(
                        psT[:, kk * 128:(kk + 1) * 128],
                        tpsrc[:, kk * 128:(kk + 1) * 128], ident_s[:]),
                      waits=(("vt", xp_ev[c]),) if k == 0 else ())
            if c >= 1:
                for j in range(3):
                    E("t", lambda e, jj=j: e.transpose(
                        psZT[0:S, jj * 128:(jj + 1) * 128],
                        zib[:, jj * S:(jj + 1) * S], ident_s[:]),
                      waits=(("vt", zvv[cm]),) if j == 0 else ())
            dB = tdrain("td")
            if c < NCH:
                xpT_ev[c] = v_op(lambda e, cc=c: e.tensor_copy(
                    xpT_s.ap()[:, :, cc * 128:(cc + 1) * 128],
                    psT[:].rearrange("p (k n) -> p k n", k=KH)),
                    waits=(("td", dB),), sig="vt")
            if c >= 1:
                ziT_ev[cm] = v_op(lambda e, cc=cm: e.tensor_copy(
                    ziT3.ap()[:, :, cc * 128:(cc + 1) * 128],
                    psZT[0:S, :].rearrange("p (j n) -> p j n", j=3)),
                    waits=(("td", dB),) if c == NCH else (), sig="vt")

        sgx = s_op(lambda e: e.activation(gates[:], ziT3[:, 1:3, :], AF.Sigmoid),
                   waits=(("vt", sch.now("vt")),), sig="sv")
        v_op(lambda e: e.tensor_tensor(u_s[:], gates[:, 0, :], ziT3[:, 0, :],
                                       ALU.mult), waits=(("sv", sgx),))
        uw = vsettle2("vv")
        if st_i % ST_PER_SEQ == 0:
            v_op(lambda e: e.tensor_tensor_scan(hs_s[:], gates[:, 1, :], u_s[:],
                                                0.0, ALU.mult, ALU.add),
                 waits=(("vv", uw),))
        else:
            v_op(lambda e: e.tensor_tensor_scan(hs_s[:], gates[:, 1, :], u_s[:],
                                                carry[:], ALU.mult, ALU.add),
                 waits=(("vv", uw),))
        uw = vsettle2("vv")
        v_op(lambda e: e.tensor_copy(carry[:], hs_s[:, TS - 1:TS]),
             waits=(("vv", uw),))
        v_op(lambda e: e.tensor_copy(hsb_s[:], hs_s[:]))
        uw = vsettle2("vv")
        E("v", lambda e: e.engine_nop(), waits=(("vv", uw),))
        hv = vnop(sig="vt")

        p6_vs = sch.now("vs")
        p6_sv = sch.now("sv")
        p6_vt = sch.now("vt")
        p6_dn = sch.now("dn")
        m1_free = p6_vs
        yv1 = {}
        y1tv = {}
        av = {}
        xs2d = {}
        oa1 = {}
        dxr_cnt = {}
        th = {}
        for c in range(NCH + 1):
            cm = c - 1
            if c < NCH:
                ffirst = True
                for half in range(2):
                    sl = slice(half * 512, (half + 1) * 512)
                    dwaits = (("vs", m1_free),)
                    if c == 0:
                        dwaits = dwaits + (("vt", hv),)
                    mm(psM1[:, sl], ones_row[:], cprerow[0:1, sl], True, False,
                       waits=dwaits if ffirst else ())
                    ffirst = False
                    for kk in range(KH):
                        mm(psM1[:, sl], xpT_s[:, kk, c * 128:(c + 1) * 128],
                           WD_s[:, kk, sl], False, False)
                    mm(psM1[:, sl], hsb_s[:, c * 128:(c + 1) * 128],
                       C_s[0:S, sl], False, True)
                dA = tdrain("td")
            if c >= 1:
                ffirst = True
                for half in range(2):
                    sl = slice(half * 512, (half + 1) * 512)
                    ewaits = (("vt", y1tv[cm]),)
                    if cm >= 1:
                        ewaits = ewaits + (("vs", av[cm - 1]),)
                    mm(psM2[:, sl], ones_row[:], coutrow[0:1, sl], True, False,
                       waits=ewaits if ffirst else ())
                    ffirst = False
                    for kk in range(KH):
                        mm(psM2[:, sl], y1T_c[:, kk * 128:(kk + 1) * 128],
                           Wout_s[:, kk, sl], False, False)
                    for kk in range(KH):
                        mm(psM2[:, sl], xnT_s[:, kk, cm * 128:(cm + 1) * 128],
                           Win_s[:, kk, sl], False, kk == KH - 1)
                dB = tdrain("td")
                if cm == NCH - 1:
                    last_e_drain = dB
            if c < NCH:
                yv1[c] = v_op(lambda e: e.tensor_copy(y1pre[:], psM1[:]),
                              waits=(("td", dA),), sig="vs")
                m1_free = yv1[c]
                th[c] = s_op(lambda e: e.activation(tpsrc[:], y1pre[:],
                                                    AF.Tanh),
                             waits=(("vs", yv1[c]),), sig="sv")
                if c == 0:
                    g = st_i * NCH
                    dma("y", gsec.ap(), x_h[g * 128:(g + 1) * 128, :], "dxr",
                        waits=(("sv", p6_sv),))
                    dxr_cnt[0] = sch.now("dxr")
                if c == 0:
                    for k in range(KH):
                        E("t", lambda e, kk=k: e.transpose(
                            psT[:, kk * 128:(kk + 1) * 128],
                            tpsrc[:, kk * 128:(kk + 1) * 128], ident_s[:]),
                          waits=(("sv", th[0]), ("vt", p6_vt))
                          if k == 0 else ())
                    dC = tdrain("td")
                    y1tv[0] = v_op(lambda e: e.tensor_copy(y1T_c, psT[:]),
                                   waits=(("td", dC),), sig="vt")
            if c >= 1:
                par = cm % 2
                t1c, t2c, t3c = (tiny1.ap()[:, par:par + 1],
                                 tiny2.ap()[:, par:par + 1],
                                 tiny3.ap()[:, par:par + 1])
                aw = (("td", dB),)
                if cm >= 1:
                    aw = aw + (("sv", xs2d[cm - 1]),)
                av[cm] = v_op(lambda e: e.tensor_copy(q1b[:], psM2[:]),
                              waits=aw, sig="vs")
                v_op(lambda e: e.bn_stats(bn6[:, 0:6], q1b[:, 0:512]))
                v_op(lambda e: e.bn_stats(bn6[:, 6:12], q1b[:, 512:1024]))
                bs = vnop()
                v_op(lambda e: e.bn_aggr(bn2[:], bn6[:]), waits=(("vv", bs),))
                ba = vnop()
                v_op(lambda e, d=t1c: e.tensor_scalar(d, bn2[:, 1:2], LN_EPS,
                                                      None, ALU.add),
                     waits=(("vv", ba),))
                ba = vnop()
                v_op(lambda e, d=t2c, s_=t1c: e.reciprocal(d, s_),
                     waits=(("vv", ba),))
                v_op(lambda e, d=t1c: e.tensor_copy(d, bn2[:, 0:1]))
                vv = vsettle2("vs")
                sq2 = s_op(lambda e, d=t3c, s_=t2c: e.activation(d, s_,
                                                                 AF.Sqrt),
                           waits=(("vs", vv),), sig="sv")
                wv = E("v", lambda e: e.engine_nop(), waits=(("sv", sq2),),
                       sig="vv")
                v_op(lambda e, d=t2c, m=t1c, r=t3c:
                     e.scalar_tensor_tensor(d, m, -1.0, r, ALU.mult, ALU.mult),
                     waits=(("vv", wv),))
                vv = vnop(sig="vs")
                if 1 <= c < NCH:
                    for k in range(KH):
                        E("t", lambda e, kk=k: e.transpose(
                            psT[:, kk * 128:(kk + 1) * 128],
                            tpsrc[:, kk * 128:(kk + 1) * 128], ident_s[:]),
                          waits=(("sv", th[c]), ("vt", y1tv[cm]))
                          if k == 0 else ())
                    dC = tdrain("td")
                    y1tv[c] = v_op(lambda e: e.tensor_copy(y1T_c, psT[:]),
                                   waits=(("td", dC),), sig="vt")
                xs2d[cm] = s_op(lambda e, b=t2c, s_=t3c:
                                e.activation(xhat_b[:], q1b[:], AF.Identity,
                                             bias=b, scale=s_),
                                waits=(("vs", vv),), sig="sv")
                v_op(lambda e: e.tensor_tensor(a2q[:], xhat_b[:], gamma_t[:],
                                               ALU.mult),
                     waits=(("sv", xs2d[cm]),))
                ow = (("dxr", dxr_cnt[cm]),)
                ow = ow + (("dn", store_dn[cm - 1]),) if cm >= 1 else                     ow + (("dn", p6_dn),)
                oa1[cm] = v_op(lambda e, c_=cm: e.tensor_tensor(
                    outc[:], f_s[:, c_, :], gsec[:], ALU.add),
                    waits=ow, sig="vs")
                if cm + 1 < NCH:
                    g2 = st_i * NCH + cm + 1
                    dma("y", gsec.ap(), x_h[g2 * 128:(g2 + 1) * 128, :],
                        "dxr", waits=(("vs", oa1[cm]),))
                    dxr_cnt[cm + 1] = sch.now("dxr")
                ov = v_op(lambda e: e.tensor_tensor(outc[:], outc[:], a2q[:],
                                                    ALU.add), sig="vy")
                g = st_i * NCH + cm
                dma("y", out_h[g * 128:(g + 1) * 128, :], outc.ap(),
                    "dn", waits=(("vy", ov),))
                store_dn[cm] = sch.now("dn")
        else:
            pass
    E("y", lambda e: e.nop(), waits=(("dn", sch.now("dn")),))

    sems = {}
    import contextlib
    with contextlib.ExitStack() as ctx:
        def get_sem(name):
            if name not in sems:
                sems[name] = ctx.enter_context(nc.semaphore(name))
            return sems[name]
        for name in sch.cnt:
            get_sem(name)
        block = ctx.enter_context(nc.Block())
        eng_attr = {"t": "tensor", "v": "vector", "s": "scalar", "y": "sync",
                    "g": "gpsimd"}
        for ename, attr in eng_attr.items():
            ops = sch.q[ename]
            if not ops:
                continue
            def make_section(ops_):
                def section(eng):
                    for fn, waits, sig, inc in ops_:
                        for sname, val in waits:
                            if val > 0:
                                eng.wait_ge(get_sem(sname), val)
                        inst = fn(eng)
                        if sig is not None:
                            assert inst is not None, "signal op returned None"
                            inst.then_inc(get_sem(sig), inc)
                return section
            getattr(block, attr)(make_section(ops))
    return nc


_CACHED = {}


def kernel(**inputs):
    import ml_dtypes
    x = np.ascontiguousarray(inputs["L0_attn_x"], np.float32).reshape(B_FULL * T, H)
    ident = np.eye(128, dtype=ml_dtypes.bfloat16)
    onesr = np.ones((1, 128), dtype=ml_dtypes.bfloat16)
    row = lambda v: np.ascontiguousarray(np.asarray(v, np.float32).reshape(1, -1))
    common = {
        "W_in": np.ascontiguousarray(inputs["L0_attn_W_in"], np.float32),
        "W_B": np.ascontiguousarray(inputs["L0_attn_W_B"], np.float32),
        "W_D": np.ascontiguousarray(inputs["L0_attn_W_D"], np.float32),
        "W_out": np.ascontiguousarray(inputs["L0_attn_W_out"], np.float32),
        "B_mat": np.ascontiguousarray(inputs["L0_attn_B_mat"], np.float32),
        "C_mat": np.ascontiguousarray(inputs["L0_attn_C_mat"], np.float32),
        "W_ig": np.ascontiguousarray(inputs["L0_attn_W_ig"], np.float32),
        "W_fg": np.ascontiguousarray(inputs["L0_attn_W_fg"], np.float32),
        "b_in": row(inputs["L0_attn_b_in"]),
        "b_B": row(inputs["L0_attn_b_B"]),
        "b_D": row(inputs["L0_attn_b_D"]),
        "b_out": row(inputs["L0_attn_b_out"]),
        "b_ig": row(inputs["L0_attn_b_ig"]),
        "b_fg": row(inputs["L0_attn_b_fg"]),
        "gamma": row(inputs["L0_attn_gamma"]),
        "beta": row(inputs["L0_attn_beta"]),
        "w_gate": np.ascontiguousarray(inputs["L0_ffn_w_gate"], np.float32),
        "w_up": np.ascontiguousarray(inputs["L0_ffn_w_up"], np.float32),
        "w_down": np.ascontiguousarray(inputs["L0_ffn_w_down"], np.float32),
        "ident": ident,
        "onesr": onesr,
    }
    if "nc" not in _CACHED:
        _CACHED["nc"] = build_program()
    nc = _CACHED["nc"]
    in_maps = []
    for ci in range(NCORES):
        m = dict(common)
        m["x"] = np.ascontiguousarray(x[ci * R:(ci + 1) * R])
        in_maps.append(m)
    res = run_bass_kernel_spmd(nc, in_maps, core_ids=list(range(NCORES)))
    out = np.concatenate([r["out"] for r in res.results], axis=0)
    return out.reshape(B_FULL, T, H)


if __name__ == "__main__":
    pass


# version 15
# speedup vs baseline: 1.5738x; 1.0430x over previous
import numpy as np

import concourse.bass as bass
import concourse.mybir as mybir
from concourse.bass_utils import run_bass_kernel_spmd

F32 = mybir.dt.float32
BF16 = mybir.dt.bfloat16
AF = mybir.ActivationFunctionType
ALU = mybir.AluOpType
AX = mybir.AxisListType

NCORES = 8
H = 1024
S = 16
II = 2816
B_FULL, T = 32, 2048
R = (B_FULL // NCORES) * T
CH = 128
TS = 512
NCH = TS // CH
import os as _os
NST = int(_os.environ.get("K_NST", R // TS))
ST_PER_SEQ = T // TS
KH = H // 128
KI = II // 128
SECS = [512, 512, 512, 512, 512, 256]
LN_EPS = 1e-5


class Sched:
    ENGS = ("t", "v", "s", "y", "g")

    def __init__(self):
        self.q = {e: [] for e in self.ENGS}
        self.cnt = {}

    def emit(self, eng, fn, waits=(), sig=None, inc=1):
        self.q[eng].append((fn, tuple(waits), sig, inc))
        if sig is not None:
            self.cnt[sig] = self.cnt.get(sig, 0) + inc
        return self.cnt.get(sig, 0) if sig else 0

    def now(self, sem):
        return self.cnt.get(sem, 0)


def build_program():
    nc = bass.Bass("TRN2", target_bir_lowering=False,
                   detect_race_conditions=False)

    dT = lambda n, sh, dt=F32, kind="ExternalInput": nc.dram_tensor(n, sh, dt, kind=kind).ap()
    x_h = dT("x", [R, H])
    W_in_h = dT("W_in", [H, H])
    W_B_h = dT("W_B", [H, H])
    W_D_h = dT("W_D", [H, H])
    W_out_h = dT("W_out", [H, H])
    B_mat_h = dT("B_mat", [H, S])
    C_mat_h = dT("C_mat", [S, H])
    W_ig_h = dT("W_ig", [H, S])
    W_fg_h = dT("W_fg", [H, S])
    b_in_h = dT("b_in", [1, H])
    b_B_h = dT("b_B", [1, H])
    b_D_h = dT("b_D", [1, H])
    b_out_h = dT("b_out", [1, H])
    b_ig_h = dT("b_ig", [1, S])
    b_fg_h = dT("b_fg", [1, S])
    gamma_h = dT("gamma", [1, H])
    beta_h = dT("beta", [1, H])
    w_gate_h = dT("w_gate", [II, H])
    w_up_h = dT("w_up", [II, H])
    w_down_h = dT("w_down", [H, II])
    ident_h = dT("ident", [128, 128], BF16)
    onesr_h = dT("onesr", [1, 128], BF16)
    out_h = dT("out", [R, H], F32, kind="ExternalOutput")

    gateT_h = nc.dram_tensor("gateT_q", [KH, 128, II], BF16, kind="Internal").ap()
    upT_h = nc.dram_tensor("upT_q", [KH, 128, II], BF16, kind="Internal").ap()
    downT_h = nc.dram_tensor("downT_q", [KI, 128, H], BF16, kind="Internal").ap()

    sb = lambda n, sh, dt: nc.alloc_sbuf_tensor(n, sh, dt)
    Win_s = sb("Win_s", [128, KH, H], BF16)
    WD_s = sb("WD_s", [128, KH, H], BF16)
    Wout_s = sb("Wout_s", [128, KH, H], BF16)
    WS3_s = sb("WS3_s", [128, KH, 3 * S], BF16)
    Bm_s = sb("Bm_s", [128, KH, S], BF16)
    C_s = sb("C_s", [S, H], BF16)
    gamma_t = sb("gamma_t", [128, H], BF16)
    ident_s = sb("ident_s", [128, 128], BF16)
    ones_row = sb("ones_row", [1, 128], BF16)
    binrow = sb("binrow", [1, H], BF16)
    betarow = sb("betarow", [1, H], BF16)
    czrow = sb("czrow", [1, 3 * S], BF16)
    cprerow = sb("cprerow", [1, H], BF16)
    coutrow = sb("coutrow", [1, H], BF16)
    bincol = sb("bincol", [128, KH], BF16)
    bBcol = sb("bBcol", [128, KH], BF16)
    biasrow48 = sb("biasrow48", [1, 3 * S], BF16)
    bias_p1 = sb("bias_p1", [128, 1], F32)
    bias_n1 = sb("bias_n1", [128, 1], F32)
    xnT_s = sb("xnT_s", [128, KH, TS], BF16)
    xqT_s = sb("xqT_s", [128, KH, TS], BF16)
    xpT_s = sb("xpT_s", [128, KH, TS], BF16)
    hqT_s = sb("hqT_s", [128, KI, TS], BF16)
    f_s = sb("f_s", [128, NCH, H], BF16)
    ziT3 = sb("ziT3", [S, 3, TS], BF16)
    gates = sb("gates", [S, 2, TS], F32)
    u_s = sb("u_s", [S, TS], F32)
    hs_s = sb("hs_s", [S, TS], F32)
    hsb_s = sb("hsb_s", [S, TS], BF16)
    carry = sb("carry", [S, 1], F32)
    gup_s = sb("gup_s", [128, 2, 2, KH, 512], BF16)
    dwn_s = sb("dwn_s", [128, 2, 2, H], BF16)
    xin_s = sb("xin_s", [128, 2, H], F32)
    xnb_s = sb("xnb_s", [128, H], BF16)
    tpsrc = sb("tpsrc", [128, H], BF16)
    qin = sb("qin", [128, H], F32)
    sm2 = sb("sm2", [128, H], F32)
    a2q = sb("a2q", [128, H], F32)
    q1b = sb("q1b", [128, H], F32)
    q2b = sb("q2b", [128, H], F32)
    bmq = sb("bmq", [128, H // 16], F32)
    scq = sb("scq", [128, H // 16], F32)
    rcq = sb("rcq", [128, H // 16], F32)
    gsec = sb("gsec", [128, H], F32)
    bn6 = sb("bn6", [128, 12], F32)
    bn2 = sb("bn2", [128, 2], F32)
    tiny1 = sb("tiny1", [128, 2], F32)
    tiny2 = sb("tiny2", [128, 2], F32)
    tiny3 = sb("tiny3", [128, 2], F32)
    zib = sb("zib", [128, 3 * S], BF16)
    sgp = gsec
    y1pre = sm2
    y1T_t = sb("y1T_t", [128, KH * 128], BF16)
    y1T_c = y1T_t.ap()[:, 0:KH * 128]
    rowA = gsec.ap()[0:1, :]
    rowB = sm2.ap()[0:1, :]
    rowbf = tpsrc.ap()[0:1, :]
    bdrow = xnb_s.ap()[0:1, :]
    xhat_b = xnb_s
    outc = qin

    psM1 = nc.alloc_psum_tensor("psM1", [128, H], F32)
    psM2 = nc.alloc_psum_tensor("psM2", [128, H], F32)
    psT = nc.alloc_psum_tensor("psT", [128, 1024], BF16)
    psZ = nc.alloc_psum_tensor("psZ", [128, 3 * S], F32)
    psZT = nc.alloc_psum_tensor("psZT", [S, 3 * 128], BF16)

    sch = Sched()
    E = sch.emit

    def dma(q_, dst, src, sig, waits=()):
        return E(q_, lambda e, d=dst, s=src: e.dma_start(out=d, in_=s),
                 waits=waits, sig=sig, inc=16)

    def tdrain(sig, waits=()):
        def fn(e):
            e.drain()
            return e.nop()
        return E("t", fn, waits=waits, sig=sig)

    def mm(out_ap, lhsT, rhs, start, stop, waits=()):
        return E("t", lambda e, o=out_ap, l=lhsT, r=rhs, a=start, b=stop:
                 e.matmul(o, l, r, start=a, stop=b), waits=waits)

    def v_op(fn, waits=(), sig=None):
        return E("v", fn, waits=waits, sig=sig)

    def vnop(sig="vv"):
        return E("v", lambda e: e.engine_nop(), sig=sig)

    def vsettle2(sig):
        t = E("v", lambda e: e.engine_nop(), sig="vv")
        return E("v", lambda e: e.engine_nop(), waits=(("vv", t),), sig=sig)

    def snop(sig="sv"):
        return E("s", lambda e: e.nop(cycle_cnt=200), sig=sig)

    def s_op(fn, waits=(), sig=None):
        return E("s", fn, waits=waits, sig=sig)

    def transposes(src_ap, n, waits=()):
        for k in range(n):
            E("t", lambda e, kk=k, sa=src_ap: e.transpose(
                psT[:, kk * 128:(kk + 1) * 128], sa[:, kk * 128:(kk + 1) * 128],
                ident_s[:]), waits=waits if k == 0 else ())
        return tdrain("td")

    eq_mv = [0]
    def emit_quant(w, qout_ap, waits_v=(), waits_out=(), src=None, c20=None):
        nb = w // 16
        if src is None:
            src = qin.ap()[:, 0:w]
        if c20 is None:
            c20 = qin.ap()[:, 0:w]
        qin3 = src.rearrange("p (b c) -> p b c", c=16)
        sm23 = sm2.ap()[:, 0:w].rearrange("p (b c) -> p b c", c=16)
        rcb = rcq.ap()[:, 0:nb].unsqueeze(2).broadcast_to([128, nb, 16])
        scb = scq.ap()[:, 0:nb].unsqueeze(2).broadcast_to([128, nb, 16])
        v_op(lambda e: e.tensor_reduce(bmq[:, 0:nb].unsqueeze(2), qin3,
                                       AX.X, ALU.max,
                                       apply_absolute_value=True),
             waits=waits_v)
        rv = vnop()
        v_op(lambda e: e.tensor_scalar(scq[:, 0:nb], bmq[:, 0:nb], 1.0 / 24.0,
                                       2.5e-13, ALU.mult, ALU.max),
             waits=(("vv", rv),))
        rv = vnop()
        v_op(lambda e: e.reciprocal(rcq[:, 0:nb], scq[:, 0:nb]),
             waits=(("vv", rv),))
        rv = vsettle2("vv")
        mv = v_op(lambda e: e.scalar_tensor_tensor(sm23, qin3, 1.0, rcb,
                                                   ALU.mult, ALU.mult),
                  waits=(("vv", rv),), sig="vs")
        eq_mv[0] = mv
        s_op(lambda e: e.activation(sgp[:, 0:w], sm2[:, 0:w], AF.Sign,
                                    bias=bias_n1[:], scale=1.0),
             waits=(("vs", mv),))
        v_op(lambda e: e.scalar_tensor_tensor(a2q[:, 0:w], sm2[:, 0:w], -1.0,
                                              sm2[:, 0:w], ALU.mult, ALU.max))
        v_op(lambda e: e.tensor_scalar(q1b[:, 0:w], a2q[:, 0:w], 3.0, None,
                                       ALU.is_gt))
        v_op(lambda e: e.scalar_tensor_tensor(q1b[:, 0:w], a2q[:, 0:w], 5.0,
                                              q1b[:, 0:w], ALU.is_gt, ALU.add))
        v_op(lambda e: e.scalar_tensor_tensor(q1b[:, 0:w], a2q[:, 0:w], 7.0,
                                              q1b[:, 0:w], ALU.is_gt, ALU.add))
        v_op(lambda e: e.tensor_scalar(q2b[:, 0:w], a2q[:, 0:w], 10.0, None,
                                       ALU.is_gt))
        v_op(lambda e: e.scalar_tensor_tensor(q2b[:, 0:w], a2q[:, 0:w], 14.0,
                                              q2b[:, 0:w], ALU.is_gt, ALU.add))
        v_op(lambda e: e.tensor_scalar(c20, a2q[:, 0:w], 20.0, None,
                                       ALU.is_gt))
        v_op(lambda e: e.scalar_tensor_tensor(q2b[:, 0:w], c20, 2.0,
                                              q2b[:, 0:w], ALU.mult, ALU.add))
        a2f = v_op(lambda e: e.scalar_tensor_tensor(q1b[:, 0:w], q2b[:, 0:w], 2.0,
                                                    q1b[:, 0:w], ALU.mult, ALU.add),
                   sig="vs")
        ssig = s_op(lambda e: e.activation(a2q[:, 0:w], sm2[:, 0:w], AF.Sign,
                                           bias=bias_p1[:], scale=1.0),
                    waits=(("vs", a2f),), sig="sv")
        v_op(lambda e: e.tensor_tensor(sgp[:, 0:w], sgp[:, 0:w], a2q[:, 0:w],
                                       ALU.add), waits=(("sv", ssig),))
        v_op(lambda e: e.scalar_tensor_tensor(q2b[:, 0:w], q1b[:, 0:w], 1.0,
                                              sgp[:, 0:w], ALU.add, ALU.mult))
        q23 = q2b.ap()[:, 0:w].rearrange("p (b c) -> p b c", c=16)
        qo3 = qout_ap.rearrange("p (b c) -> p b c", c=16)
        return v_op(lambda e: e.scalar_tensor_tensor(qo3, q23, 1.0, scb,
                                                     ALU.mult, ALU.mult),
                    waits=waits_out, sig="vq")

    v_op(lambda e: e.memset(bias_p1[:], 1.0))
    v_op(lambda e: e.memset(bias_n1[:], -1.0))
    v_op(lambda e: e.memset(biasrow48[:], 0.0))

    dma("y", ident_s.ap(), ident_h, "dc")
    dma("y", ones_row.ap(), onesr_h, "dc")
    for dst, src in ((Win_s, W_in_h), (WD_s, W_D_h), (Wout_s, W_out_h)):
        dma("g", dst.ap(), src.rearrange("(k p) n -> p k n", p=128), "dwc")
    dma("g", WS3_s.ap()[:, :, S:2 * S], W_ig_h.rearrange("(k p) n -> p k n", p=128), "dwc")
    dma("g", WS3_s.ap()[:, :, 2 * S:3 * S], W_fg_h.rearrange("(k p) n -> p k n", p=128), "dwc")
    dma("g", Bm_s.ap(), B_mat_h.rearrange("(k p) n -> p k n", p=128), "dwc")
    dma("g", C_s.ap(), C_mat_h, "dwc")
    WB_v = gup_s.ap()[:, 0].rearrange("p m k n -> p (m k n)").rearrange(
        "p (a b) -> p a b", b=H)
    WBT_v = gup_s.ap()[:, 1].rearrange("p m k n -> p (m k n)").rearrange(
        "p (a b) -> p a b", b=H)
    dma("g", WB_v, W_B_h.rearrange("(k p) n -> p k n", p=128), "dwb")

    dma("y", rowA, b_in_h, "drA")
    cv = v_op(lambda e: e.tensor_copy(binrow[:], rowA),
              waits=(("drA", 16), ("dc", 32)), sig="vt")
    for k in range(KH):
        E("t", lambda e, kk=k: e.transpose(psT[:, 2 * kk:2 * kk + 1],
                                           binrow[0:1, kk * 128:(kk + 1) * 128],
                                           ident_s[0:1, 0:1]),
          waits=(("vt", cv),) if k == 0 else ())
    tdrain("td")
    v_op(lambda e: e.tensor_copy(bincol[:],
                                 psT.ap()[:, 0:2 * KH].rearrange(
                                     "p (k two) -> p k two", two=2)[:, :, 0]),
         waits=(("td", sch.now("td")),))
    vnop(sig="vt")
    dma("y", rowB, b_out_h, "drB")
    v_op(lambda e: e.tensor_tensor(coutrow[:], rowA, rowB, ALU.add),
         waits=(("drB", 16),), sig="vt")
    dma("y", rowA, b_B_h, "drA", waits=(("vt", sch.now("vt")),))
    cv = v_op(lambda e: e.tensor_copy(rowbf, rowA),
              waits=(("drA", 32),), sig="vt")
    for k in range(KH):
        E("t", lambda e, kk=k: e.transpose(psT[:, 2 * kk:2 * kk + 1],
                                           rowbf[0:1, kk * 128:(kk + 1) * 128],
                                           ident_s[0:1, 0:1]),
          waits=(("vt", cv),) if k == 0 else ())
    tdrain("td")
    v_op(lambda e: e.tensor_copy(bBcol[:],
                                 psT.ap()[:, 0:2 * KH].rearrange(
                                     "p (k two) -> p k two", two=2)[:, :, 0]),
         waits=(("td", sch.now("td")),))
    vnop(sig="vt")
    dma("y", rowA, b_D_h, "drA", waits=(("vt", sch.now("vt")),))
    cv = v_op(lambda e: e.tensor_copy(bdrow, rowA), waits=(("drA", 48),),
              sig="vt")
    dma("y", rowA, gamma_h, "drA", waits=(("vt", cv),))
    cv = v_op(lambda e: e.tensor_copy(rowbf, rowA), waits=(("drA", 64),),
              sig="vt")
    for half in range(2):
        mm(psM1[:, half * 512:(half + 1) * 512], ones_row[:],
           rowbf[0:1, half * 512:(half + 1) * 512], True, True,
           waits=(("vt", cv),) if half == 0 else ())
    tdrain("td")
    v_op(lambda e: e.tensor_copy(gamma_t[:], psM1[:]),
         waits=(("td", sch.now("td")),), sig="vt")
    dma("y", rowA, beta_h, "drA", waits=(("vt", sch.now("vt")),))
    cv = v_op(lambda e: e.tensor_copy(betarow[:], rowA), waits=(("drA", 80),),
              sig="vt")
    dma("y", rowA[0:1, 0:S], b_ig_h, "drA", waits=(("vt", cv),))
    dma("y", rowA[0:1, S:2 * S], b_fg_h, "drA")
    cv = v_op(lambda e: e.tensor_copy(biasrow48[0:1, S:2 * S], rowA[0:1, 0:S]),
              waits=(("drA", 112),), sig="vt")
    v_op(lambda e: e.tensor_copy(biasrow48[0:1, 2 * S:3 * S],
                                 rowA[0:1, S:2 * S]))
    cv = vnop(sig="vt")

    for kk in range(KH):
        for jj in range(KH):
            E("t", lambda e, a=kk, b=jj: e.transpose(
                psT[:, b * 128:(b + 1) * 128],
                WB_v[:, b, a * 128:(a + 1) * 128], ident_s[:]),
              waits=((("dwb", 16),) if kk == 0 else ()) +
                    (("vt", sch.now("vt")),) if jj == 0 else ())
        tdrain("td")
        v_op(lambda e, a=kk: e.tensor_copy(WBT_v[:, a, :], psT[:]),
             waits=(("td", sch.now("td")),), sig="vt")
    for r in range(KH):
        for kk in range(KH):
            mm(psM2[:, 0:S], WBT_v[:, kk, r * 128:(r + 1) * 128], Bm_s[:, kk, :],
               kk == 0, kk == KH - 1,
               waits=((("dwc", sch.now("dwc")),) if r == 0 else ()) +
                     (("vt", sch.now("vt")),) if kk == 0 else ())
        tdrain("td")
        v_op(lambda e, rr=r: e.tensor_copy(WS3_s[:, rr, 0:S], psM2[:, 0:S]),
             waits=(("td", sch.now("td")),))
        vnop(sig="vt")

    for kk in range(KH):
        mm(psZ[0:S, 0:1], Bm_s[:, kk, :], bBcol[:, kk:kk + 1], kk == 0,
           kk == KH - 1, waits=(("vt", sch.now("vt")),) if kk == 0 else ())
    tdrain("td")
    v_op(lambda e: e.tensor_copy(zib[0:S, 0:1], psZ[0:S, 0:1]),
         waits=(("td", sch.now("td")),))
    vnop(sig="vt")
    E("t", lambda e: e.transpose(psZT[0:1, 0:S], zib[0:S, 0:1],
                                 ident_s[0:S, 0:S]),
      waits=(("vt", sch.now("vt")),))
    tdrain("td")
    cw = vnop()
    v_op(lambda e: e.tensor_copy(czrow[:], biasrow48[:]),
         waits=(("td", sch.now("td")), ("vv", cw)))
    cw = vnop()
    v_op(lambda e: e.tensor_tensor(czrow[0:1, 0:S], czrow[0:1, 0:S],
                                   psZT[0:1, 0:S], ALU.add),
         waits=(("vv", cw),))
    vnop(sig="vt")

    v_op(lambda e: e.tensor_copy(cprerow[:], bdrow))
    vnop(sig="vt")

    vnop()
    wq_bar = vnop()
    jobs = []
    for src_h, dstT_h, nchunk, width in (
            (w_gate_h, gateT_h, KI, H), (w_up_h, upT_h, KI, H),
            (w_down_h, downT_h, KH, II)):
        for icx in range(nchunk):
            subs = ([(0, 1024), (1024, 2048), (2048, 2816)]
                    if width == II else [(0, 1024)])
            for s0_, s1_ in subs:
                jobs.append((src_h, dstT_h, icx, s0_, s1_))
    nj = len(jobs)

    def wq_src(i):
        return qin.ap() if i % 2 == 0 else xin_s.ap()[:, 0, :]

    sh0, _, ic0, a0, a1 = jobs[0]
    dma("y", wq_src(0)[:, 0:a1 - a0], sh0[ic0 * 128:(ic0 + 1) * 128, a0:a1],
        "dfw", waits=(("vv", wq_bar),))
    wq_ev = {}
    wq_dn = {}
    wq_td = {}
    wq_mv = {}
    for i, (src_h, dstT_h, icx, s0_, s1_) in enumerate(jobs):
        wsub = s1_ - s0_
        nk = wsub // 128
        if i + 1 < nj:
            sh2, _, ic2, t0_, t1_ = jobs[i + 1]
            pw = (("vs", wq_mv[i - 1]),) if i >= 1 else (("vv", wq_bar),)
            dma("y", wq_src(i + 1)[:, 0:t1_ - t0_],
                sh2[ic2 * 128:(ic2 + 1) * 128, t0_:t1_], "dfw", waits=pw)
        vqw = emit_quant(wsub, tpsrc.ap()[:, 0:wsub],
                         waits_v=(("dfw", 16 * (i + 1)),),
                         waits_out=(("td", wq_td[i - 1]),) if i >= 1 else (),
                         src=wq_src(i)[:, 0:wsub],
                         c20=xin_s.ap()[:, 1, 0:wsub])
        wq_mv[i] = eq_mv[0]
        for k in range(nk):
            E("t", lambda e, kk=k: e.transpose(
                psT[:, kk * 128:(kk + 1) * 128],
                tpsrc[:, kk * 128:(kk + 1) * 128], ident_s[:]),
              waits=(("vq", vqw), ("vt", wq_ev[i - 1])) if (k == 0 and i >= 1)
              else ((("vq", vqw), ("vt", sch.now("vt"))) if k == 0 else ()))
        wq_td[i] = tdrain("td")
        if i % 2 == 0:
            dst3 = y1T_c[:, 0:nk * 128].rearrange("p (k n) -> p k n", k=nk)
        else:
            dst3 = xnT_s.ap()[:, 0:nk, 0:128]
        evw = (("td", wq_td[i]),)
        if i >= 2:
            evw = evw + (("dn", wq_dn[i - 2]),)
        wq_ev[i] = v_op(lambda e, d=dst3, n_=nk: e.tensor_copy(
            d, psT[:, 0:n_ * 128].rearrange("p (k n) -> p k n", k=n_)),
            waits=evw, sig="vt")
        dst = dstT_h[s0_ // 128:(s0_ // 128) + nk, :,
                     icx * 128:(icx + 1) * 128].rearrange("k p n -> p k n")
        dma("y", dst, dst3, "dn", waits=(("vt", wq_ev[i]),))
        wq_dn[i] = sch.now("dn")

    dma("y", xin_s.ap()[:, 0, :], x_h[0:128, :], "dx",
        waits=(("dn", sch.now("dn")),))
    dma("y", xin_s.ap()[:, 1, :], x_h[128:256, :], "dx")
    n_dx = 2

    last_e_drain = sch.now("td")
    store_dn = {}
    last_fdown_read = sch.now("td")
    for st_i in range(NST):
        p1_ss = {}
        p1_tdXQ = None
        p1_xqc = None
        sv_base = sch.now("sv")
        for c in range(NCH):
            g = st_i * NCH + c
            par = c % 2
            t1c, t2c, t3c = (tiny1.ap()[:, par:par + 1],
                             tiny2.ap()[:, par:par + 1],
                             tiny3.ap()[:, par:par + 1])
            slot = g % 2
            dxw = 16 * (g + 1)
            xa = xin_s.ap()[:, slot, :]
            wA = (("dx", dxw),
                  ("sv", p1_ss[g - 2]) if g - 2 in p1_ss else ("sv", sv_base))
            v_op(lambda e, a=xa: e.bn_stats(bn6[:, 0:6], a[:, 0:512]), waits=wA)
            v_op(lambda e, a=xa: e.bn_stats(bn6[:, 6:12], a[:, 512:1024]))
            bs = vnop()
            v_op(lambda e: e.bn_aggr(bn2[:], bn6[:]), waits=(("vv", bs),))
            ba = vnop()
            v_op(lambda e, d=t1c: e.tensor_scalar(d, bn2[:, 1:2], LN_EPS, None,
                                                  ALU.add), waits=(("vv", ba),))
            ba = vnop()
            v_op(lambda e, d=t2c, s_=t1c: e.reciprocal(d, s_),
                 waits=(("vv", ba),))
            v_op(lambda e, d=t1c: e.tensor_copy(d, bn2[:, 0:1]))
            vv = vsettle2("vs")
            sq = s_op(lambda e, d=t3c, s_=t2c: e.activation(d, s_, AF.Sqrt),
                      waits=(("vs", vv),), sig="sv")
            wv = E("v", lambda e: e.engine_nop(), waits=(("sv", sq),), sig="vv")
            v_op(lambda e, d=t2c, m=t1c, r=t3c:
                 e.scalar_tensor_tensor(d, m, -1.0, r, ALU.mult, ALU.mult),
                 waits=(("vv", wv),))
            vv = vnop(sig="vs")
            ss = s_op(lambda e, a=xa, b=t2c, s_=t3c:
                      e.activation(qin[:], a, AF.Identity, bias=b, scale=s_),
                      waits=(("vs", vv), ("vq", sch.now("vq")),
                             ("dn", sch.now("dn"))), sig="sv")
            wxn = (("td", p1_tdXN),) if c > 0 else ()
            ss = s_op(lambda e, a=xa, b=t2c, s_=t3c:
                      e.activation(xnb_s[:], a, AF.Identity, bias=b, scale=s_),
                      waits=wxn, sig="sv")
            p1_ss[g] = ss
            if g + 2 < NST * NCH:
                gg = g + 2
                dma("y", xin_s.ap()[:, gg % 2, :],
                    x_h[gg * 128:(gg + 1) * 128, :], "dx",
                    waits=(("sv", ss),))
                n_dx += 1
            if p1_tdXQ is not None:
                ew = (("td", p1_tdXQ),)
                if p1_xqc == 0:
                    ew = ew + (("td", last_e_drain),)
                xq_ev = v_op(lambda e, cc=p1_xqc: e.tensor_copy(
                    xqT_s.ap()[:, :, cc * 128:(cc + 1) * 128],
                    psT[:].rearrange("p (k n) -> p k n", k=KH)),
                    waits=ew, sig="vt")
                psT_free = xq_ev
            else:
                psT_free = sch.now("vt")
            for k in range(KH):
                E("t", lambda e, kk=k: e.transpose(
                    psT[:, kk * 128:(kk + 1) * 128],
                    xnb_s[:, kk * 128:(kk + 1) * 128], ident_s[:]),
                  waits=(("sv", ss), ("vt", psT_free)) if k == 0 else ())
            p1_tdXN = tdrain("td")
            emit_quant(H, tpsrc.ap()[:, 0:H], waits_v=(("sv", ss),))
            vq_c = sch.now("vq")
            dw = (("td", p1_tdXN),)
            if c == 0:
                dw = dw + (("td", last_e_drain),)
            xn_ev = v_op(lambda e, cc=c: e.tensor_copy(
                xnT_s.ap()[:, :, cc * 128:(cc + 1) * 128],
                psT[:].rearrange("p (k n) -> p k n", k=KH)),
                waits=dw, sig="vt")
            for k in range(KH):
                E("t", lambda e, kk=k: e.transpose(
                    psT[:, kk * 128:(kk + 1) * 128],
                    tpsrc[:, kk * 128:(kk + 1) * 128], ident_s[:]),
                  waits=(("vq", vq_c), ("vt", xn_ev)) if k == 0 else ())
            p1_tdXQ = tdrain("td")
            p1_xqc = c
        v_op(lambda e, cc=p1_xqc: e.tensor_copy(
            xqT_s.ap()[:, :, cc * 128:(cc + 1) * 128],
            psT[:].rearrange("p (k n) -> p k n", k=KH)),
            waits=(("td", p1_tdXQ),), sig="vt")

        ksec0 = 0
        sec_drain = {}
        m1_free = sch.now("vs")
        m2_free = sch.now("vs")
        xqT_ready = sch.now("vt")
        pend = None
        pend_ev = None
        next_ev = None
        tp_drain_prev = sch.now("td")
        for si, SEC in enumerate(SECS):
            buf = si % 2
            nk = SEC // 128
            if si >= 2:
                wbuf = (("td", sec_drain[si - 2]),)
            else:
                wbuf = (("td", sch.now("td")),)
            dma("g", gup_s.ap()[:, buf, 0, :, 0:SEC],
                gateT_h[:, :, ksec0:ksec0 + SEC].rearrange("k p n -> p k n"),
                "dgu", waits=wbuf)
            dma("g", gup_s.ap()[:, buf, 1, :, 0:SEC],
                upT_h[:, :, ksec0:ksec0 + SEC].rearrange("k p n -> p k n"),
                "dgu")
            wg = sch.now("dgu")
            for pr in range(2):
                c0, c1 = 2 * pr, 2 * pr + 1
                ffirst = True
                for ci, cc in enumerate((c0, c1)):
                    for kk in range(KH):
                        mm(psM1[:, ci * 512:ci * 512 + SEC],
                           xqT_s[:, kk, cc * 128:(cc + 1) * 128],
                           gup_s[:, buf, 0, kk, 0:SEC], kk == 0, kk == KH - 1,
                           waits=(("dgu", wg), ("vs", m1_free),
                                  ("vt", xqT_ready)) if ffirst else ())
                        ffirst = False
                md1 = tdrain("td")
                ffirst = True
                for ci, cc in enumerate((c0, c1)):
                    for kk in range(KH):
                        mm(psM2[:, ci * 512:ci * 512 + SEC],
                           xqT_s[:, kk, cc * 128:(cc + 1) * 128],
                           gup_s[:, buf, 1, kk, 0:SEC], kk == 0, kk == KH - 1,
                           waits=(("vs", m2_free),) if ffirst else ())
                        ffirst = False
                md = tdrain("td")
                if pr == 1:
                    sec_drain[si] = md
                if pend is not None:
                    tp_drain_prev = pend()
                    pend = None
                    pend_ev = next_ev
                w2 = 2 * SEC
                pM1v = psM1.ap().rearrange("p (c n) -> p c n", c=2)[:, :, 0:SEC]
                pM2v = psM2.ap().rearrange("p (c n) -> p c n", c=2)[:, :, 0:SEC]
                gsv = gsec.ap()[:, 0:w2].rearrange("p (c n) -> p c n", c=2)
                q1v = q1b.ap()[:, 0:w2].rearrange("p (c n) -> p c n", c=2)
                qiv = qin.ap()[:, 0:w2].rearrange("p (c n) -> p c n", c=2)
                gv = v_op(lambda e, d=gsv, s_=pM1v: e.tensor_copy(d, s_),
                          waits=(("td", md1),), sig="vs")
                m1_free = gv
                sl = s_op(lambda e, w=w2: e.activation(q1b[:, 0:w], gsec[:, 0:w],
                                                       AF.Silu),
                          waits=(("vs", gv),), sig="sv")
                if pend_ev is not None:
                    pend_ev()
                    pend_ev = None
                m2_free = v_op(lambda e, d=qiv, s_=pM2v, q_=q1v:
                               e.scalar_tensor_tensor(d, s_, 1.0, q_,
                                                      ALU.mult, ALU.mult),
                               waits=(("sv", sl), ("td", md),
                                      ("vq", sch.now("vq"))),
                               sig="vs")
                vq_p = emit_quant(w2, tpsrc.ap()[:, 0:w2],
                                  waits_out=(("td", tp_drain_prev),))

                def make_pend(nk_=nk, k0=ksec0 // 128, cc0=c0, cc1=c1, vq=vq_p):
                    state = {}
                    def emit_tp():
                        for ci in range(2):
                            for k in range(nk_):
                                E("t", lambda e, kk=k, cci=ci: e.transpose(
                                    psT[:, (cci * nk_ + kk) * 128:
                                        (cci * nk_ + kk + 1) * 128],
                                    tpsrc[:, (cci * nk_ + kk) * 128:
                                          (cci * nk_ + kk + 1) * 128],
                                    ident_s[:]),
                                  waits=(("vq", vq), ("vt", sch.now("vt")))
                                  if (ci == 0 and k == 0) else ())
                        td_ = tdrain("td")
                        state["td"] = td_
                        return td_
                    def emit_ev():
                        td_ = state["td"]
                        for ci, cc in enumerate((cc0, cc1)):
                            v_op(lambda e, cci=ci, c_=cc:
                                 e.tensor_copy(
                                     hqT_s.ap()[:, k0:k0 + nk_,
                                                c_ * 128:(c_ + 1) * 128],
                                     psT[:, cci * nk_ * 128:(cci + 1) * nk_ * 128]
                                     .rearrange("p (k n) -> p k n", k=nk_)),
                                 waits=(("td", td_),) if ci == 0 else (),
                                 sig="vt")
                    return emit_tp, emit_ev
                pend, next_ev = make_pend()
            ksec0 += SEC
        if pend is not None:
            tp_drain_prev = pend()
            pend = None
            pend_ev = next_ev
        if pend_ev is not None:
            pend_ev()
            pend_ev = None

        for pr in range(2):
            c0, c1 = 2 * pr, 2 * pr + 1
            psf = {c0: psM1, c1: psM2}
            grp_drain = {}
            for grp in range(11):
                bufd = grp % 2
                if grp >= 2:
                    wbuf = (("td", grp_drain[grp - 2]),)
                else:
                    wbuf = (("td", sch.now("td")),)
                dma("g", dwn_s.ap()[:, bufd],
                    downT_h[2 * grp:2 * grp + 2].rearrange("k p n -> p k n"),
                    "ddn", waits=wbuf)
                wd = sch.now("ddn")
                for cc in (c0, c1):
                    if grp == 0:
                        for half in range(2):
                            mm(psf[cc][0:128, half * 512:(half + 1) * 512],
                               ones_row[:], betarow[0:1, half * 512:(half + 1) * 512],
                               True, False,
                               waits=(("ddn", wd), ("vq", sch.now("vq")),
                                      ("vt", sch.now("vt"))) if (half == 0 and cc == c0) else ())
                    for kc in range(2):
                        ic = 2 * grp + kc
                        for half in range(2):
                            mm(psf[cc][:, half * 512:(half + 1) * 512],
                               hqT_s[:, ic, cc * 128:(cc + 1) * 128],
                               dwn_s[:, bufd, kc, half * 512:(half + 1) * 512],
                               False, grp == 10 and kc == 1,
                               waits=(("ddn", wd),) if (grp > 0 and kc == 0 and half == 0 and cc == c0) else ())
                grp_drain[grp] = tdrain("td")
            
            for cc in (c0, c1):
                v_op(lambda e, c_=cc, p_=psf[cc]: e.tensor_copy(
                    f_s.ap()[:, c_, :], p_[:]),
                    waits=(("td", sch.now("td")),) if cc == c0 else (), sig="vt")

        p4_vt = sch.now("vt")
        xp_ev = {}
        xpT_ev = {}
        zib_ev = {}
        zvv = {}
        ziT_ev = {}
        for c in range(NCH + 1):
            cm = c - 1
            if c < NCH:
                ffirst = True
                for half in range(2):
                    sl = slice(half * 512, (half + 1) * 512)
                    bw = (("vt", xp_ev[c - 1]),) if c >= 1 else                         (("vt", p4_vt),)
                    mm(psM1[:, sl], ones_row[:], binrow[0:1, sl], True, False,
                       waits=bw if ffirst else ())
                    ffirst = False
                    for kk in range(KH):
                        mm(psM1[:, sl], xnT_s[:, kk, c * 128:(c + 1) * 128],
                           Win_s[:, kk, sl], False, kk == KH - 1)
            if c >= 1:
                zw = (("vt", xpT_ev[cm]),)
                if cm >= 1:
                    zw = zw + (("vt", zib_ev[cm - 1]),)
                mm(psZ[:], ones_row[:], czrow[:], True, False, waits=zw)
                for kk in range(KH):
                    mm(psZ[:], xpT_s[:, kk, cm * 128:(cm + 1) * 128],
                       WS3_s[:, kk, :], False, kk == KH - 1)
            dA = tdrain("td")
            if c < NCH:
                xp_ev[c] = v_op(lambda e: e.tensor_copy(tpsrc[:], psM1[:]),
                                waits=(("td", dA),), sig="vt")
            if c >= 1:
                zib_ev[cm] = v_op(lambda e: e.tensor_copy(zib[:], psZ[:]),
                                  waits=(("td", dA),), sig="vt")
                zvv[cm] = vnop(sig="vt")
            if c < NCH:
                for k in range(KH):
                    E("t", lambda e, kk=k: e.transpose(
                        psT[:, kk * 128:(kk + 1) * 128],
                        tpsrc[:, kk * 128:(kk + 1) * 128], ident_s[:]),
                      waits=(("vt", xp_ev[c]),) if k == 0 else ())
            if c >= 1:
                for j in range(3):
                    E("t", lambda e, jj=j: e.transpose(
                        psZT[0:S, jj * 128:(jj + 1) * 128],
                        zib[:, jj * S:(jj + 1) * S], ident_s[:]),
                      waits=(("vt", zvv[cm]),) if j == 0 else ())
            dB = tdrain("td")
            if c < NCH:
                xpT_ev[c] = v_op(lambda e, cc=c: e.tensor_copy(
                    xpT_s.ap()[:, :, cc * 128:(cc + 1) * 128],
                    psT[:].rearrange("p (k n) -> p k n", k=KH)),
                    waits=(("td", dB),), sig="vt")
            if c >= 1:
                ziT_ev[cm] = v_op(lambda e, cc=cm: e.tensor_copy(
                    ziT3.ap()[:, :, cc * 128:(cc + 1) * 128],
                    psZT[0:S, :].rearrange("p (j n) -> p j n", j=3)),
                    waits=(("td", dB),) if c == NCH else (), sig="vt")

        sgx = s_op(lambda e: e.activation(gates[:], ziT3[:, 1:3, :], AF.Sigmoid),
                   waits=(("vt", sch.now("vt")),), sig="sv")
        v_op(lambda e: e.tensor_tensor(u_s[:], gates[:, 0, :], ziT3[:, 0, :],
                                       ALU.mult), waits=(("sv", sgx),))
        uw = vsettle2("vv")
        if st_i % ST_PER_SEQ == 0:
            v_op(lambda e: e.tensor_tensor_scan(hs_s[:], gates[:, 1, :], u_s[:],
                                                0.0, ALU.mult, ALU.add),
                 waits=(("vv", uw),))
        else:
            v_op(lambda e: e.tensor_tensor_scan(hs_s[:], gates[:, 1, :], u_s[:],
                                                carry[:], ALU.mult, ALU.add),
                 waits=(("vv", uw),))
        uw = vsettle2("vv")
        v_op(lambda e: e.tensor_copy(carry[:], hs_s[:, TS - 1:TS]),
             waits=(("vv", uw),))
        v_op(lambda e: e.tensor_copy(hsb_s[:], hs_s[:]))
        uw = vsettle2("vv")
        E("v", lambda e: e.engine_nop(), waits=(("vv", uw),))
        hv = vnop(sig="vt")

        p6_vs = sch.now("vs")
        p6_sv = sch.now("sv")
        p6_vt = sch.now("vt")
        p6_dn = sch.now("dn")
        m1_free = p6_vs
        yv1 = {}
        y1tv = {}
        av = {}
        xs2d = {}
        oa1 = {}
        dxr_cnt = {}
        th = {}
        for c in range(NCH + 1):
            cm = c - 1
            if c < NCH:
                ffirst = True
                for half in range(2):
                    sl = slice(half * 512, (half + 1) * 512)
                    dwaits = (("vs", m1_free),)
                    if c == 0:
                        dwaits = dwaits + (("vt", hv),)
                    mm(psM1[:, sl], ones_row[:], cprerow[0:1, sl], True, False,
                       waits=dwaits if ffirst else ())
                    ffirst = False
                    for kk in range(KH):
                        mm(psM1[:, sl], xpT_s[:, kk, c * 128:(c + 1) * 128],
                           WD_s[:, kk, sl], False, False)
                    mm(psM1[:, sl], hsb_s[:, c * 128:(c + 1) * 128],
                       C_s[0:S, sl], False, True)
                dA = tdrain("td")
            if c >= 1:
                ffirst = True
                for half in range(2):
                    sl = slice(half * 512, (half + 1) * 512)
                    ewaits = (("vt", y1tv[cm]),)
                    if cm >= 1:
                        ewaits = ewaits + (("vs", av[cm - 1]),)
                    mm(psM2[:, sl], ones_row[:], coutrow[0:1, sl], True, False,
                       waits=ewaits if ffirst else ())
                    ffirst = False
                    for kk in range(KH):
                        mm(psM2[:, sl], y1T_c[:, kk * 128:(kk + 1) * 128],
                           Wout_s[:, kk, sl], False, False)
                    for kk in range(KH):
                        mm(psM2[:, sl], xnT_s[:, kk, cm * 128:(cm + 1) * 128],
                           Win_s[:, kk, sl], False, kk == KH - 1)
                dB = tdrain("td")
                if cm == NCH - 1:
                    last_e_drain = dB
            if c < NCH:
                yv1[c] = v_op(lambda e: e.tensor_copy(y1pre[:], psM1[:]),
                              waits=(("td", dA),), sig="vs")
                m1_free = yv1[c]
                th[c] = s_op(lambda e: e.activation(tpsrc[:], y1pre[:],
                                                    AF.Tanh),
                             waits=(("vs", yv1[c]),), sig="sv")
                if c == 0:
                    g = st_i * NCH
                    dma("y", gsec.ap(), x_h[g * 128:(g + 1) * 128, :], "dxr",
                        waits=(("sv", p6_sv),))
                    dxr_cnt[0] = sch.now("dxr")
                if c == 0:
                    for k in range(KH):
                        E("t", lambda e, kk=k: e.transpose(
                            psT[:, kk * 128:(kk + 1) * 128],
                            tpsrc[:, kk * 128:(kk + 1) * 128], ident_s[:]),
                          waits=(("sv", th[0]), ("vt", p6_vt))
                          if k == 0 else ())
                    dC = tdrain("td")
                    y1tv[0] = v_op(lambda e: e.tensor_copy(y1T_c, psT[:]),
                                   waits=(("td", dC),), sig="vt")
            if c >= 1:
                par = cm % 2
                t1c, t2c, t3c = (tiny1.ap()[:, par:par + 1],
                                 tiny2.ap()[:, par:par + 1],
                                 tiny3.ap()[:, par:par + 1])
                aw = (("td", dB),)
                if cm >= 1:
                    aw = aw + (("sv", xs2d[cm - 1]),)
                av[cm] = v_op(lambda e: e.tensor_copy(q1b[:], psM2[:]),
                              waits=aw, sig="vs")
                v_op(lambda e: e.bn_stats(bn6[:, 0:6], q1b[:, 0:512]))
                v_op(lambda e: e.bn_stats(bn6[:, 6:12], q1b[:, 512:1024]))
                bs = vnop()
                v_op(lambda e: e.bn_aggr(bn2[:], bn6[:]), waits=(("vv", bs),))
                ba = vnop()
                v_op(lambda e, d=t1c: e.tensor_scalar(d, bn2[:, 1:2], LN_EPS,
                                                      None, ALU.add),
                     waits=(("vv", ba),))
                ba = vnop()
                v_op(lambda e, d=t2c, s_=t1c: e.reciprocal(d, s_),
                     waits=(("vv", ba),))
                v_op(lambda e, d=t1c: e.tensor_copy(d, bn2[:, 0:1]))
                vv = vsettle2("vs")
                sq2 = s_op(lambda e, d=t3c, s_=t2c: e.activation(d, s_,
                                                                 AF.Sqrt),
                           waits=(("vs", vv),), sig="sv")
                wv = E("v", lambda e: e.engine_nop(), waits=(("sv", sq2),),
                       sig="vv")
                v_op(lambda e, d=t2c, m=t1c, r=t3c:
                     e.scalar_tensor_tensor(d, m, -1.0, r, ALU.mult, ALU.mult),
                     waits=(("vv", wv),))
                vv = vnop(sig="vs")
                if 1 <= c < NCH:
                    for k in range(KH):
                        E("t", lambda e, kk=k: e.transpose(
                            psT[:, kk * 128:(kk + 1) * 128],
                            tpsrc[:, kk * 128:(kk + 1) * 128], ident_s[:]),
                          waits=(("sv", th[c]), ("vt", y1tv[cm]))
                          if k == 0 else ())
                    dC = tdrain("td")
                    y1tv[c] = v_op(lambda e: e.tensor_copy(y1T_c, psT[:]),
                                   waits=(("td", dC),), sig="vt")
                xs2d[cm] = s_op(lambda e, b=t2c, s_=t3c:
                                e.activation(xhat_b[:], q1b[:], AF.Identity,
                                             bias=b, scale=s_),
                                waits=(("vs", vv),), sig="sv")
                v_op(lambda e: e.tensor_tensor(a2q[:], xhat_b[:], gamma_t[:],
                                               ALU.mult),
                     waits=(("sv", xs2d[cm]),))
                ow = (("dxr", dxr_cnt[cm]),)
                ow = ow + (("dn", store_dn[cm - 1]),) if cm >= 1 else                     ow + (("dn", p6_dn),)
                oa1[cm] = v_op(lambda e, c_=cm: e.tensor_tensor(
                    outc[:], f_s[:, c_, :], gsec[:], ALU.add),
                    waits=ow, sig="vs")
                if cm + 1 < NCH:
                    g2 = st_i * NCH + cm + 1
                    dma("y", gsec.ap(), x_h[g2 * 128:(g2 + 1) * 128, :],
                        "dxr", waits=(("vs", oa1[cm]),))
                    dxr_cnt[cm + 1] = sch.now("dxr")
                ov = v_op(lambda e: e.tensor_tensor(outc[:], outc[:], a2q[:],
                                                    ALU.add), sig="vy")
                g = st_i * NCH + cm
                dma("y", out_h[g * 128:(g + 1) * 128, :], outc.ap(),
                    "dn", waits=(("vy", ov),))
                store_dn[cm] = sch.now("dn")
        else:
            pass
    E("y", lambda e: e.nop(), waits=(("dn", sch.now("dn")),))

    sems = {}
    import contextlib
    with contextlib.ExitStack() as ctx:
        def get_sem(name):
            if name not in sems:
                sems[name] = ctx.enter_context(nc.semaphore(name))
            return sems[name]
        for name in sch.cnt:
            get_sem(name)
        block = ctx.enter_context(nc.Block())
        eng_attr = {"t": "tensor", "v": "vector", "s": "scalar", "y": "sync",
                    "g": "gpsimd"}
        for ename, attr in eng_attr.items():
            ops = sch.q[ename]
            if not ops:
                continue
            def make_section(ops_):
                def section(eng):
                    for fn, waits, sig, inc in ops_:
                        for sname, val in waits:
                            if val > 0:
                                eng.wait_ge(get_sem(sname), val)
                        inst = fn(eng)
                        if sig is not None:
                            assert inst is not None, "signal op returned None"
                            inst.then_inc(get_sem(sig), inc)
                return section
            getattr(block, attr)(make_section(ops))
    return nc


_CACHED = {}


def kernel(**inputs):
    import ml_dtypes
    x = np.ascontiguousarray(inputs["L0_attn_x"], np.float32).reshape(B_FULL * T, H)
    ident = np.eye(128, dtype=ml_dtypes.bfloat16)
    onesr = np.ones((1, 128), dtype=ml_dtypes.bfloat16)
    row = lambda v: np.ascontiguousarray(np.asarray(v, np.float32).reshape(1, -1))
    common = {
        "W_in": np.ascontiguousarray(inputs["L0_attn_W_in"], np.float32),
        "W_B": np.ascontiguousarray(inputs["L0_attn_W_B"], np.float32),
        "W_D": np.ascontiguousarray(inputs["L0_attn_W_D"], np.float32),
        "W_out": np.ascontiguousarray(inputs["L0_attn_W_out"], np.float32),
        "B_mat": np.ascontiguousarray(inputs["L0_attn_B_mat"], np.float32),
        "C_mat": np.ascontiguousarray(inputs["L0_attn_C_mat"], np.float32),
        "W_ig": np.ascontiguousarray(inputs["L0_attn_W_ig"], np.float32),
        "W_fg": np.ascontiguousarray(inputs["L0_attn_W_fg"], np.float32),
        "b_in": row(inputs["L0_attn_b_in"]),
        "b_B": row(inputs["L0_attn_b_B"]),
        "b_D": row(inputs["L0_attn_b_D"]),
        "b_out": row(inputs["L0_attn_b_out"]),
        "b_ig": row(inputs["L0_attn_b_ig"]),
        "b_fg": row(inputs["L0_attn_b_fg"]),
        "gamma": row(inputs["L0_attn_gamma"]),
        "beta": row(inputs["L0_attn_beta"]),
        "w_gate": np.ascontiguousarray(inputs["L0_ffn_w_gate"], np.float32),
        "w_up": np.ascontiguousarray(inputs["L0_ffn_w_up"], np.float32),
        "w_down": np.ascontiguousarray(inputs["L0_ffn_w_down"], np.float32),
        "ident": ident,
        "onesr": onesr,
    }
    if "nc" not in _CACHED:
        _CACHED["nc"] = build_program()
    nc = _CACHED["nc"]
    in_maps = []
    for ci in range(NCORES):
        m = dict(common)
        m["x"] = np.ascontiguousarray(x[ci * R:(ci + 1) * R])
        in_maps.append(m)
    res = run_bass_kernel_spmd(nc, in_maps, core_ids=list(range(NCORES)))
    out = np.concatenate([r["out"] for r in res.results], axis=0)
    return out.reshape(B_FULL, T, H)


if __name__ == "__main__":
    pass
